# version 20
# baseline (speedup 1.0000x reference)
import numpy as np
import ml_dtypes
import concourse.bass as bass
import concourse.mybir as mybir
from concourse.bass_utils import run_bass_kernel_spmd
from contextlib import ExitStack

F32 = mybir.dt.float32
BF16 = mybir.dt.bfloat16
I32 = mybir.dt.int32
AF = mybir.ActivationFunctionType
ALU = mybir.AluOpType
AX = mybir.AxisListType

H = 4096
D = 8192
DL = 1024
NN = 16
R = 256
TOK = 8192
LB = 4096
COL = 512
NCORE = 8
EPS_LN = 1e-5
EPS_RMS = 1e-6


def build_program(upto=99):
    nc = bass.Bass("TRN2", target_bir_lowering=False, debug=False,
                   detect_race_conditions=False)

    def ein(name, shape, dt=F32):
        return nc.dram_tensor(name, shape, dt, kind="ExternalInput").ap()

    x_rows = ein("x_rows", [1024, H])
    x_cols = ein("x_cols", [TOK, COL])
    win_f = ein("win_f", [16, 128, H])
    conv_w = ein("conv_w", [128, 32])
    conv_b = ein("conv_b", [128, 8])
    wx_f = ein("wx_f", [128, 2304])
    wdt_f = ein("wdt_f", [128, 2048])
    dt_bias = ein("dt_bias", [128, 8])
    a_mat = ein("a_mat", [128, 128])
    d_param = ein("d_param", [128, 8])
    wout_f = ein("wout_f", [128, 64 * COL])
    wqT = ein("wqT", [32, 128, COL], I32)
    sc_bc = ein("sc_bc", [32, 128, COL])
    zr_bc = ein("zr_bc", [32, 128, COL])
    lnw_in = ein("lnw_in", [128, 288])
    ident_b = ein("ident_b", [128, 128], BF16)
    ident_f = ein("ident_f", [128, 128], F32)
    sel16 = ein("sel16", [16, 2048], BF16)

    out_cols = nc.dram_tensor("out_cols", [TOK, COL], F32,
                              kind="ExternalOutput").ap()

    def scr(name, shape, dt=BF16):
        return nc.dram_tensor(name, shape, dt).ap()

    xnT_mine = scr("xnT_mine", [H, 1024])
    xnT_parts = scr("xnT_parts", [NCORE, H, 1024])
    win_ld = scr("win_ld", [16, 128, H])
    wdt_ld = scr("wdt_ld", [128, 2048])
    wx_ld = scr("wx_ld", [128, 2304])
    wout_ld = scr("wout_ld", [128, 64, COL])
    wdeq_ld = scr("wdeq_ld", [32, 128, COL])
    uT_sp = scr("uT_sp", [DL, TOK])
    gsT_sp = scr("gsT_sp", [DL, TOK])
    sspT_mine = scr("sspT_mine", [288, TOK], F32)
    sspT_sum = scr("sspT_sum", [288, TOK], F32)
    yT_mine = [scr(f"yT_mine{h}", [DL, LB]) for h in range(2)]
    yT_parts = [scr(f"yT_parts{h}", [NCORE, DL, LB]) for h in range(2)]
    dtnT_sp = scr("dtnT_sp", [128, 2, TOK])
    gq_sp = scr("gq_sp", [64, 128, COL])
    BT_sp = scr("BT_sp", [16, TOK])
    CT_sp = scr("CT_sp", [16, TOK])

    st = ExitStack()
    with st:
        e = st.enter_context

        A_sb = e(nc.sbuf_tensor("A_sb", [128, 128], F32))
        cw_sb = e(nc.sbuf_tensor("cw_sb", [128, 32], F32))
        cb_sb = e(nc.sbuf_tensor("cb_sb", [128, 8], F32))
        dp_sb = e(nc.sbuf_tensor("dp_sb", [128, 8], F32))
        dtb_sb = e(nc.sbuf_tensor("dtb_sb", [128, 8], F32))
        identb = e(nc.sbuf_tensor("identb", [128, 128], BF16))
        identf = e(nc.sbuf_tensor("identf", [128, 128], F32))
        sel_sb = e(nc.sbuf_tensor("sel_sb", [16, 2048], BF16))
        lnw_bc = e(nc.sbuf_tensor("lnw_bc", [128, 288], F32))
        hst_sb = e(nc.sbuf_tensor("hst_sb", [128, 8, 16], BF16))
        halo_sb = e(nc.sbuf_tensor("halo_sb", [128, 8, 4], F32))
        kconst = e(nc.sbuf_tensor("kconst", [128, 8], F32))

        SEM = {}
        for k in ["t", "v", "s", "p", "ld", "l2", "sd",
                  "w0", "w1", "w2", "w3", "s0", "s1", "u0", "u1", "qq",
                  "y0", "y1", "y2", "y3"]:
            SEM[k] = e(nc.semaphore("sem_" + k))
        cc_sems = [e(nc.semaphore(f"sem_cc{i}")) for i in range(4)]
        C = {k: 0 for k in SEM}
        block = e(nc.Block())

        def tick(instr, k, by=1):
            C[k] += by
            instr.then_inc(SEM[k], by)
            return C[k]

        def mkop(key):
            def op(eng, emit, waits=()):
                for wk, wv in waits:
                    eng.wait_ge(SEM[wk], wv)
                eng.wait_ge(SEM[key], C[key])
                return tick(emit(), key)
            return op
        vop = mkop("v")
        sop = mkop("s")
        pop = mkop("p")

        def dma(eng, key, out, in_, waits=()):
            for wk, wv in waits:
                eng.wait_ge(SEM[wk], wv)
            return tick(eng.dma_start(out=out, in_=in_), key, 16)

        def fence_all():
            @block.sync
            def _(sy):
                for k in C:
                    sy.wait_ge(SEM[k], C[k])

        _CONSTS = [EPS_LN, EPS_RMS, 1.0 / H, 1.0 / R, 1.0 / 16.0]

        @block.vector
        def _(v):
            for i, val in enumerate(_CONSTS):
                vop(v, lambda i=i, val=val: v.memset(kconst[:, i:i + 1], val))
        for i, val in enumerate(_CONSTS):
            nc.const_aps.register(F32, val, kconst[:, i:i + 1])

        @block.sync
        def _(sy):
            dma(sy, "ld", A_sb[:], a_mat[:])
            dma(sy, "ld", cw_sb[:], conv_w[:])
            dma(sy, "ld", cb_sb[:], conv_b[:])
            dma(sy, "ld", dp_sb[:], d_param[:])
            dma(sy, "ld", dtb_sb[:], dt_bias[:])
            dma(sy, "ld", identb[:], ident_b[:])
            dma(sy, "ld", identf[:], ident_f[:])
            dma(sy, "ld", sel_sb[:], sel16[:])
            dma(sy, "ld", lnw_bc[:], lnw_in[:])
        ld_consts = C["ld"]
        v_lnbc = C["v"]

        if upto <= 0:
            fence_all()
            return nc


        if upto <= 1:
            fence_all()
            return nc

        with ExitStack() as p2:
            e2 = p2.enter_context
            xtile = [e2(nc.sbuf_tensor(f"p2_xt{i}", [128, H], F32)) for i in range(2)]
            xn_bf = [e2(nc.sbuf_tensor(f"p2_xn{i}", [128, H], BF16)) for i in range(8)]
            stat = e2(nc.sbuf_tensor("p2_stat", [128, 8], F32))
            sq = e2(nc.sbuf_tensor("p2_sq", [128, H], F32))
            band2 = e2(nc.sbuf_tensor("p2_band", [128, 1024], BF16))
            ps_tr = e2(nc.psum_tensor("p2_pstr", [128, 1024], BF16))
            for tt in range(8):
                @block.sync
                def _(sy, tt=tt):
                    dma(sy, "ld", xtile[tt % 2][:],
                        x_rows[tt * 128:(tt + 1) * 128, :],
                        waits=[("v", C["v"]), ("s", C["s"])])
                ld_here = C["ld"]
                xt = xtile[tt % 2]

                @block.vector
                def _(v, xt=xt, ld_here=ld_here):
                    vop(v, lambda: v.reduce_sum(stat[:, 0:1], xt[:], axis=AX.X),
                        waits=[("ld", ld_here)])
                v_m1 = C["v"]

                @block.scalar
                def _(s, xt=xt, v_m1=v_m1, ld_here=ld_here):
                    sop(s, lambda: s.mul(stat[:, 1:2], stat[:, 0:1], 1.0 / H),
                        waits=[("v", v_m1)])
                    sop(s, lambda: s.activation(sq[:], xt[:], AF.Square),
                        waits=[("ld", ld_here)])
                s_sq = C["s"]

                @block.vector
                def _(v, s_sq=s_sq):
                    vop(v, lambda: v.reduce_sum(stat[:, 2:3], sq[:], axis=AX.X),
                        waits=[("s", s_sq)])
                    vop(v, lambda: v.tensor_mul(stat[:, 3:4], stat[:, 1:2],
                                                stat[:, 1:2]))
                v_m2 = C["v"]

                @block.scalar
                def _(s, v_m2=v_m2):
                    sop(s, lambda: s.mul(stat[:, 4:5], stat[:, 2:3], 1.0 / H),
                        waits=[("v", v_m2)])
                s_ms = C["s"]

                @block.vector
                def _(v, s_ms=s_ms):
                    vop(v, lambda: v.tensor_sub(stat[:, 5:6], stat[:, 4:5],
                                                stat[:, 3:4]),
                        waits=[("s", s_ms)])
                v_var = C["v"]

                @block.scalar
                def _(s, v_var=v_var):
                    sop(s, lambda: s.activation(stat[:, 6:7], stat[:, 5:6],
                                                AF.Ln, bias=EPS_LN),
                        waits=[("v", v_var)])
                    sop(s, lambda: s.activation(stat[:, 7:8], stat[:, 6:7],
                                                AF.Exp, scale=-0.5))
                s_rstd = C["s"]

                @block.vector
                def _(v, tt=tt, xt=xt, s_rstd=s_rstd):
                    vop(v, lambda: v.tensor_scalar(
                        xn_bf[tt][:], xt[:], stat[:, 1:2], stat[:, 7:8],
                        op0=ALU.subtract, op1=ALU.mult),
                        waits=[("s", s_rstd)])
            v_xn = C["v"]

            for hb in range(32):
                @block.tensor
                def _(t, hb=hb, v_xn=v_xn):
                    t.wait_ge(SEM["v"], v_xn if hb == 0 else C["v"])
                    for tt in range(8):
                        last = t.transpose(ps_tr[:, tt * 128:(tt + 1) * 128],
                                    xn_bf[tt][:, hb * 128:(hb + 1) * 128],
                                    identb[:])
                    tick(last, "t")
                t_tr = C["t"]

                @block.vector
                def _(v, t_tr=t_tr):
                    vop(v, lambda: v.tensor_copy(band2[:], ps_tr[:]),
                        waits=[("t", t_tr), ("sd", C["sd"])])
                v_b = C["v"]

                @block.gpsimd
                def _(g, hb=hb, v_b=v_b):
                    dma(g, "sd", xnT_mine[hb * 128:(hb + 1) * 128, :],
                        band2[:], waits=[("v", v_b)])
            sd_xnT = C["sd"]

            @block.gpsimd
            def _(g):
                g.wait_ge(SEM["sd"], sd_xnT)
                g.collective_compute(
                    "AllGather", ALU.bypass,
                    replica_groups=[list(range(NCORE))],
                    ins=[xnT_mine.opt()],
                    outs=[xnT_parts.rearrange("r p t -> (r p) t").opt()],
                ).then_inc(cc_sems[0])

        with ExitStack() as p1:
            e1 = p1.enter_context
            wtile = [e1(nc.sbuf_tensor(f"p1_wt{i}", [128, 4096], F32))
                     for i in range(2)]
            wcast = [e1(nc.sbuf_tensor(f"p1_wc{i}", [128, 4096], BF16))
                     for i in range(2)]
            qtl = e1(nc.sbuf_tensor("p1_qt", [128, 2, COL], I32))
            sctl = e1(nc.sbuf_tensor("p1_sc", [128, 2, COL], F32))
            zrtl = e1(nc.sbuf_tensor("p1_zr", [128, 2, COL], F32))
            qf = e1(nc.sbuf_tensor("p1_qf", [128, 2, COL], F32))
            wqc = e1(nc.sbuf_tensor("p1_wqc", [128, 2, COL], BF16))
            h_cast = {0: 0, 1: 0}
            h_cst = {0: 0, 1: 0}

            cast_jobs = []
            for rb in range(16):
                cast_jobs.append((4096, win_f[rb],
                                  win_ld[rb]))
            cast_jobs.append((2048, wdt_f[:], wdt_ld[:]))
            cast_jobs.append((2304, wx_f[:], wx_ld[:]))
            for g8 in range(8):
                cast_jobs.append((4096, wout_f[:, g8 * 4096:(g8 + 1) * 4096],
                                  wout_ld[:, g8 * 8:(g8 + 1) * 8, :]
                                  .rearrange("p a b -> p (a b)")))

            for i, (wdth, srcap, dstap) in enumerate(cast_jobs):
                par = i % 2
                psem = "y%d" % par
                @block.sync
                def _(sy, i=i, par=par, psem=psem, srcap=srcap, wdth=wdth):
                    w = [("v", h_cast[par])]
                    if i < 2:
                        w += [("t", C["t"]), ("v", C["v"]), ("s", C["s"]),
                              ("sd", C["sd"])]
                    dma(sy, psem, wtile[par][:, 0:wdth], srcap, waits=w)
                ld_h = C[psem]

                @block.vector
                def _(v, par=par, psem=psem, wdth=wdth, ld_h=ld_h):
                    vop(v, lambda: v.tensor_copy(wcast[par][:, 0:wdth],
                                                 wtile[par][:, 0:wdth]),
                        waits=[(psem, ld_h), ("sd", C["sd"])])
                h_cast[par] = C["v"]
                v_c = C["v"]

                @block.gpsimd
                def _(g, par=par, wdth=wdth, dstap=dstap, v_c=v_c):
                    dma(g, "sd", dstap, wcast[par][:, 0:wdth],
                        waits=[("v", v_c)])
                h_cst[par] = C["sd"]
                if i == 15:
                    sd_win = C["sd"]

            for i in range(16):
                @block.sync
                def _(sy, i=i):
                    w = ([("v", C["v"])] if i >= 1 else
                         [("t", C["t"]), ("v", C["v"]), ("s", C["s"]),
                          ("sd", C["sd"])])
                    dma(sy, "ld", qtl.ap(),
                        wqT[2 * i:2 * i + 2].rearrange("k p c -> p k c"),
                        waits=w)
                    dma(sy, "ld", sctl.ap(),
                        sc_bc[2 * i:2 * i + 2].rearrange("k p c -> p k c"))
                    dma(sy, "ld", zrtl.ap(),
                        zr_bc[2 * i:2 * i + 2].rearrange("k p c -> p k c"))
                ld_h = C["ld"]

                @block.vector
                def _(v, i=i, ld_h=ld_h):
                    vop(v, lambda: v.tensor_copy(
                        qf.ap().rearrange("p a b -> p (a b)"),
                        qtl.ap().rearrange("p a b -> p (a b)")),
                        waits=[("ld", ld_h), ("sd", C["sd"])])
                    vop(v, lambda: v.scalar_tensor_tensor(
                        qf.ap().rearrange("p a b -> p (a b)"),
                        qf.ap().rearrange("p a b -> p (a b)"),
                        8.0,
                        sctl.ap().rearrange("p a b -> p (a b)"),
                        op0=ALU.subtract, op1=ALU.mult))
                    vop(v, lambda: v.tensor_add(
                        wqc.ap().rearrange("p a b -> p (a b)"),
                        qf.ap().rearrange("p a b -> p (a b)"),
                        zrtl.ap().rearrange("p a b -> p (a b)")))
                v_c = C["v"]

                @block.gpsimd
                def _(g, i=i, v_c=v_c):
                    dma(g, "sd",
                        wdeq_ld[2 * i:2 * i + 2].rearrange("k p c -> p k c"),
                        wqc.ap(), waits=[("v", v_c)])
        sd_prep = C["sd"]

        @block.gpsimd
        def _(g):
            g.wait_ge(cc_sems[0], 1)
            tick(g.engine_nop(), "p")
        p_ag0 = C["p"]

        with ExitStack() as pf:
            ef = pf.enter_context
            wqs = ef(nc.sbuf_tensor("pf_wq", [128, 32, COL], BF16))
            xwf = ef(nc.sbuf_tensor("pf_xw", [128, 4, 1024], BF16))
            gq_st = ef(nc.sbuf_tensor("pf_gst", [128, 8, COL], BF16))
            ps8f = ef(nc.psum_tensor("pf_ps", [128, 8, 512], F32))

            @block.sync
            def _(sy):
                for kb in range(32):
                    dma(sy, "ld", wqs[:, kb, :], wdeq_ld[kb],
                        waits=([("sd", sd_prep)] if kb == 0 else ()))
            ld_wq = C["ld"]

            x_hist = {i: 0 for i in range(4)}
            s_gq_last = 0
            for tg in range(8):
                for kb in range(32):
                    slot = kb % 4
                    qsel = kb % 2
                    eng_blk = block.sync if qsel == 0 else block.scalar
                    xsem = "w%d" % slot
                    @eng_blk
                    def _(sy, tg=tg, kb=kb, slot=slot, xsem=xsem):
                        w = [("t", x_hist[slot])]
                        if tg == 0 and kb < 2:
                            w.append(("p", p_ag0))
                            w.append(("sd", sd_prep))
                        dma(sy, xsem, xwf[:, slot, :],
                            xnT_parts[tg, kb * 128:(kb + 1) * 128, :],
                            waits=w)
                    ld_x = C[xsem]

                    @block.tensor
                    def _(t, kb=kb, slot=slot, xsem=xsem, ld_x=ld_x,
                          s_gq_last=s_gq_last):
                        t.wait_ge(SEM[xsem], ld_x)
                        if kb == 0:
                            t.wait_ge(SEM["ld"], ld_wq)
                            t.wait_ge(SEM["s"], s_gq_last)
                        for tt in range(8):
                            last = t.matmul(ps8f[:, tt, :],
                                     xwf[:, slot, tt * 128:(tt + 1) * 128],
                                     wqs[:, kb, :],
                                     start=(kb == 0), stop=(kb == 31))
                        tick(last, "t")
                    x_hist[slot] = C["t"]
                t_ffn = C["t"]

                @block.scalar
                def _(sc, tg=tg, t_ffn=t_ffn):
                    for tt in range(8):
                        sop(sc, lambda tt=tt: sc.activation(
                            gq_st[:, tt, :], ps8f[:, tt, :], AF.Silu),
                            waits=([("t", t_ffn), ("sd", C["sd"])]
                                   if tt == 0 else ()))
                s_gq_last = C["s"]

                @block.gpsimd
                def _(g, tg=tg, s_gq_last=s_gq_last):
                    dma(g, "sd", gq_sp[tg * 8:tg * 8 + 8]
                        .rearrange("a p b -> p a b"),
                        gq_st.ap(), waits=[("s", s_gq_last)])
        sd_ffn = C["sd"]
        s_ffn = C["s"]

        if upto <= 2:
            @block.gpsimd
            def _(g):
                g.wait_ge(cc_sems[0], 1)
            fence_all()
            return nc

        with ExitStack() as p3:
            e3 = p3.enter_context
            xng = e3(nc.sbuf_tensor("p3_xng", [128, 32, 1024], BF16))
            wslab = [e3(nc.sbuf_tensor(f"p3_ws{i}", [128, 32, 128], BF16))
                     for i in range(2)]
            upre = e3(nc.sbuf_tensor("p3_upre", [128, 2, 1028], F32))
            uacc = e3(nc.sbuf_tensor("p3_uacc", [128, 2, 1024], F32))
            ub = e3(nc.sbuf_tensor("p3_ub", [128, 2, 1024], BF16))
            ps_a = e3(nc.psum_tensor("p3_psa", [128, 2, 2, 512], F32))

            @block.vector
            def _(v):
                vop(v, lambda: v.memset(halo_sb.ap().rearrange("p a b -> p (a b)"),
                                        0.0))

            h_evac = {0: 0, 1: 0}
            h_vsave = {0: 0, 1: 0}
            h_uread = {0: 0, 1: 0}
            h_store = {0: 0, 1: 0}
            h_wld = {0: 0, 1: 0}

            for grp in range(8):
                xbuf = xng
                gate = [("t", C["t"]), ("p", p_ag0)]
                if grp == 0:
                    gate += [("sd", C["sd"]), ("s", C["s"]), ("v", C["v"])]
                for qsel, blk in ((0, block.sync), (1, block.scalar)):
                    @blk
                    def _(sy, grp=grp, xbuf=xbuf, qsel=qsel, gate=gate):
                        first = True
                        for hb in range(32):
                            if hb % 2 != qsel:
                                continue
                            dma(sy, "w%d" % (hb // 8), xbuf[:, hb, :],
                                xnT_parts[grp, hb * 128:(hb + 1) * 128, :],
                                waits=(gate if first else ()))
                            first = False
                wv_cnt = [C["w%d" % i] for i in range(4)]

                for dt in range(16):
                    par = dt % 2
                    wbuf = wslab[par]
                    @block.sync
                    def _(sy, dt=dt, par=par, wbuf=wbuf, grp=grp):
                        w = [("t", h_wld[par])]
                        if grp == 0 and dt < 2:
                            w.append(("sd", sd_ffn))
                            w.append(("s", s_ffn))
                        dma(sy, "s%d" % par,
                            wbuf.ap().rearrange("p a b -> p (a b)"),
                            win_ld[dt], waits=w)
                    ld_w = C["s%d" % par]

                    @block.tensor
                    def _(t, dt=dt, par=par, wbuf=wbuf, xbuf=xbuf, grp=grp,
                          wv_cnt=wv_cnt, ld_w=ld_w):
                        t.wait_ge(SEM["s%d" % par], ld_w)
                        t.wait_ge(SEM["s"], h_evac[par])
                        for k in range(32):
                            if dt == 0 and k % 8 == 0:
                                t.wait_ge(SEM["w%d" % (k // 8)],
                                          wv_cnt[k // 8])
                            for bk in range(2):
                                last = t.matmul(ps_a[:, par, bk, :],
                                         wbuf[:, k, :],
                                         xbuf[:, k, bk * 512:(bk + 1) * 512],
                                         start=(k == 0), stop=(k == 31))
                        tick(last, "t")
                    t_mm = C["t"]
                    h_wld[par] = t_mm

                    ps_flat = ps_a.ap()[:, par].rearrange("p a b -> p (a b)")
                    if dt < 8:
                        @block.scalar
                        def _(s, par=par, t_mm=t_mm, ps_flat=ps_flat):
                            sop(s, lambda: s.copy(upre[:, par, 4:1028],
                                                  ps_flat),
                                waits=[("t", t_mm), ("v", h_vsave[par])])
                        s_evac = C["s"]
                        h_evac[par] = s_evac

                        @block.vector
                        def _(v, dt=dt, par=par, s_evac=s_evac):
                            vop(v, lambda: v.tensor_copy(upre[:, par, 1:4],
                                                         halo_sb[:, dt, 1:4]),
                                waits=[("s", s_evac)])
                            vop(v, lambda: v.tensor_scalar(
                                uacc[:, par, :], upre[:, par, 4:1028],
                                cw_sb[:, dt * 4 + 3:dt * 4 + 4], None,
                                op0=ALU.mult),
                                waits=[("s", h_uread[par])])
                            for kk in range(3):
                                vop(v, lambda kk=kk: v.scalar_tensor_tensor(
                                    uacc[:, par, :],
                                    upre[:, par, 3 - kk:1027 - kk],
                                    cw_sb[:, dt * 4 + 2 - kk:dt * 4 + 3 - kk],
                                    uacc[:, par, :],
                                    op0=ALU.mult, op1=ALU.add))
                            vop(v, lambda: v.tensor_copy(
                                halo_sb[:, dt, 1:4],
                                upre[:, par, 1025:1028]))
                        v_savehalo = C["v"]
                        h_vsave[par] = v_savehalo

                        @block.scalar
                        def _(s, dt=dt, par=par, v_savehalo=v_savehalo):
                            sop(s, lambda: s.activation(
                                ub[:, par, :], uacc[:, par, :], AF.Silu,
                                bias=cb_sb[:, dt:dt + 1]),
                                waits=[("v", v_savehalo),
                                       ("sd", C["sd"])])
                        s_u = C["s"]
                        h_uread[par] = s_u

                        @block.gpsimd
                        def _(g, grp=grp, dt=dt, par=par, s_u=s_u):
                            dma(g, "sd",
                                uT_sp[dt * 128:(dt + 1) * 128,
                                      grp * 1024:(grp + 1) * 1024],
                                ub[:, par, :], waits=[("s", s_u)])
                        h_store[par] = C["sd"]
                    else:
                        @block.scalar
                        def _(s, par=par, t_mm=t_mm, ps_flat=ps_flat):
                            sop(s, lambda: s.activation(
                                ub[:, par, :], ps_flat, AF.Silu),
                                waits=[("t", t_mm), ("sd", C["sd"])])
                        s_g = C["s"]
                        h_evac[par] = s_g

                        @block.gpsimd
                        def _(g, grp=grp, dt=dt, par=par, s_g=s_g):
                            dma(g, "sd",
                                gsT_sp[(dt - 8) * 128:(dt - 7) * 128,
                                       grp * 1024:(grp + 1) * 1024],
                                ub[:, par, :], waits=[("s", s_g)])
                        h_store[par] = C["sd"]

                if grp == 3:
                    @block.vector
                    def _(v):
                        vop(v, lambda: v.memset(
                            halo_sb.ap().rearrange("p a b -> p (a b)"), 0.0))

        if upto <= 3:
            fence_all()
            return nc

        with ExitStack() as p4:
            e4 = p4.enter_context
            ubig = [e4(nc.sbuf_tensor(f"p4_u{i}", [128, 8, 2048], BF16))
                    for i in range(2)]
            wxs = e4(nc.sbuf_tensor("p4_wx", [128, 8, 288], BF16))
            ev = e4(nc.sbuf_tensor("p4_ev", [128, 2, 2048], F32))
            ps_m = e4(nc.psum_tensor("p4_psm", [128, 2, 4, 512], F32))

            @block.sync
            def _(sy):
                dma(sy, "ld", wxs.ap().rearrange("p a b -> p (a b)"),
                    wx_ld[:],
                    waits=[("t", C["t"]), ("s", C["s"]), ("v", C["v"]),
                           ("sd", C["sd"])])
            ld_wx = C["ld"]

            h_u4 = {0: 0, 1: 0}
            h_pm = {0: 0, 1: 0}
            h_ev = {0: 0, 1: 0}
            mi = 0
            for qt in range(4):
                ubuf = ubig[qt % 2]
                usem = "u%d" % (qt % 2)
                @block.sync
                def _(sy, qt=qt, ubuf=ubuf, usem=usem):
                    w = [("t", h_u4[qt % 2]), ("sd", C["sd"])]
                    for kb in range(8):
                        dma(sy, usem, ubuf[:, kb, :],
                            uT_sp[kb * 128:(kb + 1) * 128,
                                  qt * 2048:(qt + 1) * 2048],
                            waits=(w if kb == 0 else ()))
                ld_u = C[usem]

                for mt in range(3):
                    mrows = 128 if mt < 2 else 32
                    mp = mi % 2
                    @block.tensor
                    def _(t, qt=qt, mt=mt, mp=mp, mrows=mrows, ubuf=ubuf,
                          usem=usem, ld_u=ld_u):
                        t.wait_ge(SEM[usem], ld_u)
                        t.wait_ge(SEM["ld"], ld_wx)
                        t.wait_ge(SEM["v"], h_pm[mp])
                        for k in range(8):
                            for bk in range(4):
                                last = t.matmul(ps_m[0:mrows, mp, bk, :],
                                         wxs[:, k, mt * 128:mt * 128 + mrows],
                                         ubuf[:, k, bk * 512:(bk + 1) * 512],
                                         start=(k == 0), stop=(k == 7))
                        tick(last, "t")
                    t_mm = C["t"]
                    h_u4[qt % 2] = t_mm

                    @block.vector
                    def _(v, mp=mp, mrows=mrows, t_mm=t_mm):
                        vop(v, lambda: v.tensor_copy(
                            ev[0:mrows, mp, :],
                            ps_m.ap()[0:mrows, mp]
                            .rearrange("p a b -> p (a b)")),
                            waits=[("t", t_mm), ("sd", C["sd"])])
                    v_ev = C["v"]
                    h_pm[mp] = v_ev

                    @block.gpsimd
                    def _(g, qt=qt, mt=mt, mp=mp, mrows=mrows, v_ev=v_ev):
                        dma(g, "sd",
                            sspT_mine[mt * 128:mt * 128 + mrows,
                                      qt * 2048:(qt + 1) * 2048],
                            ev[0:mrows, mp, :], waits=[("v", v_ev)])
                    h_ev[mp] = C["sd"]
                    mi += 1
            sd_ssp = C["sd"]

            @block.gpsimd
            def _(g):
                g.wait_ge(SEM["sd"], sd_ssp)
                g.collective_compute(
                    "AllReduce", ALU.add,
                    replica_groups=[list(range(NCORE))],
                    ins=[sspT_mine.opt()], outs=[sspT_sum.opt()],
                ).then_inc(cc_sems[1])

        if upto <= 4:
            @block.gpsimd
            def _(g):
                g.wait_ge(cc_sems[1], 1)
            fence_all()
            return nc

        with ExitStack() as p5:
            e5 = p5.enter_context
            ssp = [e5(nc.sbuf_tensor(f"p5_ssp{i}", [128, TOK // 2], F32))
                   for i in range(4)]
            ssp3 = e5(nc.sbuf_tensor("p5_sspC", [32, TOK], F32))
            tok = e5(nc.sbuf_tensor("p5_tok", [128, 288], F32))
            dtn = e5(nc.sbuf_tensor("p5_dtn", [128, 288], BF16))
            nrm = e5(nc.sbuf_tensor("p5_nrm", [128, 8], F32))
            sqv = e5(nc.sbuf_tensor("p5_sq", [128, 288], F32))
            ps_f = e5(nc.psum_tensor("p5_psf", [128, 512], F32))
            ps_b = e5(nc.psum_tensor("p5_psb", [128, 2, 512], BF16))
            stage5 = e5(nc.sbuf_tensor("p5_stage", [128, 2, 512], BF16))
            sd_hist5 = {}

            @block.gpsimd
            def _(g):
                g.wait_ge(cc_sems[1], 1)
                tick(g.engine_nop(), "p")
            p_ar = C["p"]

            @block.sync
            def _(sy):
                w = [("p", p_ar), ("v", C["v"]), ("t", C["t"])]
                for i in range(4):
                    dma(sy, "ld", ssp[i][:],
                        sspT_sum[(i % 2) * 128:(i % 2) * 128 + 128,
                                 (i // 2) * 4096:(i // 2) * 4096 + 4096],
                        waits=(w if i == 0 else ()))
                dma(sy, "ld", ssp3[:], sspT_sum[256:288, :])
            ld_ssp = C["ld"]

            for tt in range(64):
                half = (tt * 128) // 4096
                off = (tt * 128) % 4096
                @block.tensor
                def _(t, tt=tt, half=half, off=off, ld_ssp=ld_ssp):
                    t.wait_ge(SEM["ld"], ld_ssp)
                    t.wait_ge(SEM["v"], C["v"])
                    last = t.transpose(ps_f[:, 0:128],
                                ssp[2 * half][:, off:off + 128], identf[:])
                    last = t.transpose(ps_f[:, 128:256],
                                ssp[2 * half + 1][:, off:off + 128], identf[:])
                    last = t.transpose(ps_f[:, 256:288],
                                ssp3[:, tt * 128:tt * 128 + 128],
                                identf[0:32, 0:32])
                    tick(last, "t")
                t_tr = C["t"]

                @block.vector
                def _(v, t_tr=t_tr):
                    vop(v, lambda: v.tensor_copy(tok[:], ps_f[:, 0:288]),
                        waits=[("t", t_tr)])
                v_tok = C["v"]

                @block.scalar
                def _(s, v_tok=v_tok):
                    sop(s, lambda: s.activation(sqv[:], tok[:], AF.Square),
                        waits=[("v", v_tok)])
                s_sq = C["s"]

                @block.vector
                def _(v, s_sq=s_sq):
                    vop(v, lambda: v.reduce_sum(nrm[:, 0:1], sqv[:, 0:R],
                                                axis=AX.X), waits=[("s", s_sq)])
                    vop(v, lambda: v.reduce_sum(nrm[:, 1:2], sqv[:, R:R + 16],
                                                axis=AX.X))
                    vop(v, lambda: v.reduce_sum(nrm[:, 2:3], sqv[:, R + 16:288],
                                                axis=AX.X))
                v_rs = C["v"]

                @block.scalar
                def _(s, v_rs=v_rs):
                    for j, w_ in [(0, R), (1, 16), (2, 16)]:
                        sop(s, lambda j=j, w_=w_: s.activation(
                            nrm[:, 3 + j:4 + j], nrm[:, j:j + 1], AF.Ln,
                            scale=1.0 / w_, bias=EPS_RMS),
                            waits=([("v", v_rs)] if j == 0 else ()))
                        sop(s, lambda j=j: s.activation(
                            nrm[:, 3 + j:4 + j], nrm[:, 3 + j:4 + j], AF.Exp,
                            scale=-0.5))
                s_r = C["s"]

                @block.vector
                def _(v, s_r=s_r):
                    vop(v, lambda: v.tensor_scalar(
                        tok[:, 0:R], tok[:, 0:R], nrm[:, 3:4], None,
                        op0=ALU.mult), waits=[("s", s_r)])
                    vop(v, lambda: v.tensor_scalar(
                        tok[:, R:R + 16], tok[:, R:R + 16], nrm[:, 4:5], None,
                        op0=ALU.mult))
                    vop(v, lambda: v.tensor_scalar(
                        tok[:, R + 16:288], tok[:, R + 16:288], nrm[:, 5:6],
                        None, op0=ALU.mult))
                    vop(v, lambda: v.tensor_mul(dtn[:], tok[:], lnw_bc[:]),
                        waits=[("ld", ld_consts)])
                v_dtn = C["v"]

                par = tt % 2
                @block.tensor
                def _(t, tt=tt, par=par, v_dtn=v_dtn):
                    t.wait_ge(SEM["v"], v_dtn)
                    last = t.transpose(ps_b[:, par, 0:128], dtn[:, 0:128],
                                       identb[:])
                    last = t.transpose(ps_b[:, par, 128:256], dtn[:, 128:256],
                                       identb[:])
                    last = t.transpose(ps_b[0:16, par, 256:384],
                                       dtn[:, 256:272], identb[:])
                    last = t.transpose(ps_b[0:16, par, 384:512],
                                       dtn[:, 272:288], identb[:])
                    tick(last, "t")
                t_tr2 = C["t"]

                @block.vector
                def _(v, tt=tt, par=par, t_tr2=t_tr2):
                    w = [("t", t_tr2), ("sd", C["sd"])]
                    vop(v, lambda: v.tensor_copy(stage5[:, par, :],
                                                 ps_b[:, par, :]), waits=w)
                v_stage = C["v"]

                @block.gpsimd
                def _(g, tt=tt, par=par, v_stage=v_stage):
                    dma(g, "sd", dtnT_sp[:, :, tt * 128:(tt + 1) * 128],
                        stage5.ap()[:, par, 0:256]
                        .rearrange("p (a b) -> p a b", a=2),
                        waits=[("v", v_stage)])
                    dma(g, "sd", BT_sp[:, tt * 128:(tt + 1) * 128],
                        stage5[0:16, par, 256:384])
                    dma(g, "sd", CT_sp[:, tt * 128:(tt + 1) * 128],
                        stage5[0:16, par, 384:512])
                sd_hist5[tt] = C["sd"]
        v_p5 = C["v"]
        sd_p5 = C["sd"]

        if upto <= 5:
            fence_all()
            return nc

        with ExitStack() as p6:
            e6 = p6.enter_context
            QW = 1024
            wdts = e6(nc.sbuf_tensor("p6_wdt", [128, 8, 2, 128], BF16))
            dtn_q = e6(nc.sbuf_tensor("p6_dtnq", [128, 2, 2, QW], BF16))
            btq = e6(nc.sbuf_tensor("p6_btq", [16, 2, QW], BF16))
            ctq = e6(nc.sbuf_tensor("p6_ctq", [16, 2, QW], BF16))
            Bbc = e6(nc.sbuf_tensor("p6_B", [128, 2, 8, QW], BF16))
            Cbc = e6(nc.sbuf_tensor("p6_C", [128, 2, 8, QW], BF16))
            dl_all = e6(nc.sbuf_tensor("p6_dla", [128, 8, QW], BF16))
            du_all = e6(nc.sbuf_tensor("p6_dua", [128, 8, QW], BF16))
            a_buf = e6(nc.sbuf_tensor("p6_ab", [128, 8, QW], BF16))
            h4 = e6(nc.sbuf_tensor("p6_h", [128, 4, QW], BF16))
            bsc2 = e6(nc.sbuf_tensor("p6_bs", [128, 2, QW], BF16))
            ptmp = e6(nc.sbuf_tensor("p6_pt", [128, 2, QW], BF16))
            yac = e6(nc.sbuf_tensor("p6_yac", [128, 8, QW], BF16))
            u_b = e6(nc.sbuf_tensor("p6_u", [128, QW], BF16))
            gs_b = e6(nc.sbuf_tensor("p6_gs", [128, QW], BF16))
            yf = e6(nc.sbuf_tensor("p6_yf", [128, QW], F32))
            yo = e6(nc.sbuf_tensor("p6_yo", [128, QW], BF16))
            az = e6(nc.sbuf_tensor("p6_az", [128, QW], F32))
            zp2 = e6(nc.sbuf_tensor("p6_zp", [128, 2, QW], F32))
            el2 = e6(nc.sbuf_tensor("p6_el", [128, 2, QW], F32))
            ps_d = e6(nc.psum_tensor("p6_psd", [128, 2, 512], F32))
            ps_bb = e6(nc.psum_tensor("p6_psbb", [128, 2, 2, 512], F32))

            @block.sync
            def _(sy):
                dma(sy, "qq",
                    wdts.ap().rearrange("p a b c -> p (a b c)"), wdt_ld[:],
                    waits=[("t", C["t"]), ("v", C["v"]), ("sd", C["sd"])])
            ld_wdt = C["qq"]

            psd_flat = ps_d.ap().rearrange("p a b -> p (a b)")
            v_scan_hist = {j: 0 for j in range(8)}
            v_bc_war = {0: 0, 1: 0}
            v_az_last = 0

            def emit_bc(ng, qpar, ld_q):
                s_bc = C["s"]
                for j in range(8):
                    n = ng * 8 + j
                    @block.tensor
                    def _(t, n=n, qpar=qpar, ld_q=ld_q, s_bc=s_bc):
                        t.wait_ge(SEM["s"], s_bc)
                        t.wait_ge(SEM["qq"], ld_q)
                        for hk in range(2):
                            last = t.matmul(ps_bb[:, 0, hk, :],
                                     sel_sb[:, n * 128:(n + 1) * 128],
                                     btq[0:16, qpar,
                                         hk * 512:(hk + 1) * 512],
                                     start=True, stop=True)
                        for hk in range(2):
                            last = t.matmul(ps_bb[:, 1, hk, :],
                                     sel_sb[:, n * 128:(n + 1) * 128],
                                     ctq[0:16, qpar,
                                         hk * 512:(hk + 1) * 512],
                                     start=True, stop=True)
                        tick(last, "t")
                    t_bc = C["t"]

                    @block.scalar
                    def _(s, j=j, t_bc=t_bc, ng=ng):
                        sop(s, lambda: s.copy(
                            Bbc[:, ng, j, :],
                            ps_bb.ap()[:, 0]
                            .rearrange("p a b -> p (a b)")),
                            waits=[("t", t_bc),
                                   ("v", v_bc_war[ng])])
                        sop(s, lambda: s.copy(
                            Cbc[:, ng, j, :],
                            ps_bb.ap()[:, 1]
                            .rearrange("p a b -> p (a b)")))
                    s_bc = C["s"]
                return C["s"]

            for b in range(2):
                for qt in range(4):
                    qi = b * 4 + qt
                    t0 = b * LB + qt * QW
                    qpar = qi % 2
                    @block.sync
                    def _(sy, t0=t0, qpar=qpar):
                        w = [("sd", sd_p5), ("t", C["t"])]
                        dma(sy, "qq", dtn_q.ap()[:, qpar, :, :],
                            dtnT_sp[:, :, t0:t0 + QW], waits=w)
                        dma(sy, "qq", btq[:, qpar, :], BT_sp[:, t0:t0 + QW])
                        dma(sy, "qq", ctq[:, qpar, :], CT_sp[:, t0:t0 + QW])
                    ld_q = C["qq"]

                    s_bc_done0 = emit_bc(0, qpar, ld_q)
                    v_dl = {}
                    for sl in range(8):
                        spar = sl % 2
                        @block.tensor
                        def _(t, sl=sl, qpar=qpar, ld_q=ld_q):
                            t.wait_ge(SEM["qq"], ld_q)
                            t.wait_ge(SEM["v"], v_az_last)
                            for k in range(2):
                                for bk in range(2):
                                    last = t.matmul(ps_d[:, bk, :],
                                             wdts[:, sl, k, :],
                                             dtn_q[:, qpar, k,
                                                   bk * 512:(bk + 1) * 512],
                                             start=(k == 0), stop=(k == 1))
                            tick(last, "t")
                        t_d = C["t"]

                        @block.sync
                        def _(sy, sl=sl, t0=t0):
                            dma(sy, "ld", u_b[:],
                                uT_sp[sl * 128:(sl + 1) * 128, t0:t0 + QW],
                                waits=[("v", C["v"]), ("sd", C["sd"])])
                        ld_u = C["ld"]

                        @block.vector
                        def _(v, sl=sl, spar=spar, t_d=t_d):
                            vop(v, lambda: v.tensor_scalar(
                                zp2[:, spar, :], psd_flat,
                                dtb_sb[:, sl:sl + 1], None,
                                op0=ALU.add),
                                waits=[("t", t_d)])
                        v_az_last = C["v"]

                        @block.scalar
                        def _(s, spar=spar, v_az=v_az_last):
                            sop(s, lambda: s.activation(az[:], zp2[:, spar, :],
                                                        AF.Abs),
                                waits=[("v", v_az)])
                            sop(s, lambda: s.activation(el2[:, spar, :], az[:],
                                                        AF.Exp, scale=-1.0))
                            sop(s, lambda: s.activation(el2[:, spar, :],
                                                        el2[:, spar, :],
                                                        AF.Ln, bias=1.0))
                            sop(s, lambda: s.activation(zp2[:, spar, :],
                                                        zp2[:, spar, :],
                                                        AF.Relu))
                        s_sp = C["s"]

                        @block.vector
                        def _(v, sl=sl, spar=spar, s_sp=s_sp, ld_u=ld_u):
                            vop(v, lambda: v.tensor_add(dl_all[:, sl, :],
                                                        zp2[:, spar, :],
                                                        el2[:, spar, :]),
                                waits=[("s", s_sp)])
                            vop(v, lambda: v.tensor_mul(du_all[:, sl, :],
                                                        dl_all[:, sl, :],
                                                        u_b[:]),
                                waits=[("ld", ld_u)])
                        v_dl[sl] = C["v"]
                    s_bc_done1 = emit_bc(1, qpar, ld_q)


                    for ng in range(2):
                        s_bc_done = s_bc_done0 if ng == 0 else s_bc_done1

                        for sl in range(8):
                            s_base = C["s"]
                            @block.scalar
                            def _(s, sl=sl, ng=ng):
                                for j in range(8):
                                    n = ng * 8 + j
                                    w = [("v", max(v_dl[sl],
                                                   v_scan_hist[j]))]
                                    sop(s, lambda j=j, n=n: s.activation(
                                        a_buf[:, j, :], dl_all[:, sl, :],
                                        AF.Exp,
                                        scale=A_sb[:, sl * 16 + n:
                                                   sl * 16 + n + 1]),
                                        waits=w)

                            for j in range(8):
                                n = ng * 8 + j
                                hslot = j % 4
                                s_exp = s_base + j + 1
                                @block.vector
                                def _(v, sl=sl, j=j, n=n, hslot=hslot,
                                      s_exp=s_exp, qt=qt, ng=ng):
                                    vop(v, lambda: v.tensor_mul(
                                        bsc2[:, j % 2, :], du_all[:, sl, :],
                                        Bbc[:, ng, j, :]),
                                        waits=([("s", s_bc_done)]
                                               if j == 0 else ()))
                                    if qt == 0:
                                        init = 0.0
                                    else:
                                        init = hst_sb[:, sl, n:n + 1]
                                    vop(v, lambda: v.tensor_tensor_scan(
                                        h4[:, hslot, :], a_buf[:, j, :],
                                        bsc2[:, j % 2, :], init,
                                        op0=ALU.mult, op1=ALU.add),
                                        waits=[("s", s_exp)])
                                    vop(v, lambda: v.tensor_copy(
                                        hst_sb[:, sl, n:n + 1],
                                        h4[:, hslot, QW - 1:QW]))
                                    acc = yac[:, sl, :] if ng == 0 \
                                        else ptmp[:, 0, :]
                                    if j == 0:
                                        vop(v, lambda: v.tensor_mul(
                                            acc, h4[:, hslot, :],
                                            Cbc[:, ng, j, :]))
                                    else:
                                        vop(v, lambda: v.tensor_mul(
                                            ptmp[:, 1, :], h4[:, hslot, :],
                                            Cbc[:, ng, j, :]))
                                        vop(v, lambda: v.tensor_add(
                                            acc, acc, ptmp[:, 1, :]))
                                v_scan_hist[j] = C["v"]

                            if ng == 1:
                                @block.sync
                                def _(sy, sl=sl, t0=t0):
                                    dma(sy, "ld", u_b[:],
                                        uT_sp[sl * 128:(sl + 1) * 128,
                                              t0:t0 + QW],
                                        waits=[("v", C["v"])])
                                    dma(sy, "ld", gs_b[:],
                                        gsT_sp[sl * 128:(sl + 1) * 128,
                                               t0:t0 + QW])
                                ld_ug = C["ld"]

                                @block.vector
                                def _(v, sl=sl, ld_ug=ld_ug):
                                    vop(v, lambda: v.tensor_add(
                                        yf[:], ptmp[:, 0, :], yac[:, sl, :]))
                                    vop(v, lambda: v.scalar_tensor_tensor(
                                        yf[:], u_b[:], dp_sb[:, sl:sl + 1],
                                        yf[:], op0=ALU.mult, op1=ALU.add),
                                        waits=[("ld", ld_ug)])
                                    vop(v, lambda: v.tensor_mul(
                                        yo[:], yf[:], gs_b[:]),
                                        waits=[("sd", C["sd"])])
                                v_y = C["v"]

                                @block.gpsimd
                                def _(g, sl=sl, b=b, qt=qt, v_y=v_y):
                                    dma(g, "sd",
                                        yT_mine[b][sl * 128:(sl + 1) * 128,
                                                   qt * QW:(qt + 1) * QW],
                                        yo[:], waits=[("v", v_y)])
                        v_bc_war[ng] = C["v"]
                    if qt == 3:
                        sd_yh = C["sd"]
                        @block.gpsimd
                        def _(g, b=b, sd_yh=sd_yh):
                            g.wait_ge(SEM["sd"], sd_yh)
                            g.collective_compute(
                                "AllGather", ALU.bypass,
                                replica_groups=[list(range(NCORE))],
                                ins=[yT_mine[b].opt()],
                                outs=[yT_parts[b]
                                      .rearrange("r p t -> (r p) t").opt()],
                            ).then_inc(cc_sems[2 + b])

        if upto <= 6:
            @block.gpsimd
            def _(g):
                g.wait_ge(cc_sems[2], 1)
                g.wait_ge(cc_sems[3], 1)
            fence_all()
            return nc

        with ExitStack() as p8:
            e8 = p8.enter_context
            wos = e8(nc.sbuf_tensor("p8_wo", [128, 64, COL], BF16))
            ywin = e8(nc.sbuf_tensor("p8_yw", [128, 4, 1024], BF16))
            gq = e8(nc.sbuf_tensor("p8_g", [128, 8, COL], BF16))
            xres = e8(nc.sbuf_tensor("p8_x", [128, 8, COL], F32))
            sg = e8(nc.sbuf_tensor("p8_sg", [128, 2, COL], F32))
            ps8 = e8(nc.psum_tensor("p8_ps", [128, 8, 512], F32))

            @block.gpsimd
            def _(g):
                g.wait_ge(cc_sems[2], 1)
                tick(g.engine_nop(), "p")
            p_agy = [C["p"], None]

            @block.scalar
            def _(sy):
                for g4 in range(4):
                    dma(sy, "l2",
                        wos.ap()[:, g4 * 16:(g4 + 1) * 16, :]
                        .rearrange("p a b -> p (a b)"),
                        wout_ld[:, g4 * 16:(g4 + 1) * 16, :]
                        .rearrange("p a b -> p (a b)"),
                        waits=([("t", C["t"]), ("v", C["v"])]
                               if g4 == 0 else ()))
            ld_wo = C["l2"]

            x_hist = {i: 0 for i in range(4)}
            y_hist = {i: 0 for i in range(4)}
            sg_hist = {0: 0, 1: 0}
            v_comb_last = 0
            first_q = {(h, q): True for h in range(2) for q in range(2)}

            for tg in range(8):
                if tg == 4:
                    @block.gpsimd
                    def _(g):
                        g.wait_ge(cc_sems[3], 1)
                        tick(g.engine_nop(), "p")
                    p_agy[1] = C["p"]
                @block.gpsimd
                def _(g, tg=tg):
                    dma(g, "sd", gq.ap(),
                        gq_sp[tg * 8:tg * 8 + 8].rearrange("a p b -> p a b"),
                        waits=[("v", C["v"])])
                    for tt in range(8):
                        dma(g, "sd", xres[:, tt, :],
                            x_cols[tg * 1024 + tt * 128:
                                   tg * 1024 + (tt + 1) * 128, :])
                ld_xr = C["sd"]

                for kb in range(64):
                    slot = kb % 4
                    qsel = kb % 2
                    eng_blk = block.sync if qsel == 0 else block.scalar
                    ysem = "y%d" % slot
                    @eng_blk
                    def _(sy, tg=tg, kb=kb, slot=slot, qsel=qsel, ysem=ysem):
                        w = [("t", y_hist[slot])]
                        if first_q[(tg // 4, qsel)]:
                            w.append(("p", p_agy[tg // 4]))
                            first_q[(tg // 4, qsel)] = False
                        dma(sy, ysem, ywin[:, slot, :],
                            yT_parts[tg // 4][kb // 8, (kb % 8) * 128:
                                              (kb % 8) * 128 + 128,
                                              (tg % 4) * 1024:
                                              (tg % 4 + 1) * 1024],
                            waits=w)
                    ld_y = C[ysem]

                    @block.tensor
                    def _(t, kb=kb, slot=slot, ysem=ysem, ld_y=ld_y,
                          v_comb_last=v_comb_last):
                        t.wait_ge(SEM[ysem], ld_y)
                        t.wait_ge(SEM["l2"], ld_wo)
                        if kb == 0:
                            t.wait_ge(SEM["v"], v_comb_last)
                        for tt in range(8):
                            last = t.matmul(ps8[:, tt, :],
                                     ywin[:, slot, tt * 128:(tt + 1) * 128],
                                     wos[:, kb, :],
                                     start=(kb == 0), stop=(kb == 63))
                        tick(last, "t")
                    y_hist[slot] = C["t"]
                t_s = C["t"]

                for tt in range(8):
                    sp = tt % 2
                    @block.vector
                    def _(v, tg=tg, tt=tt, sp=sp, t_s=t_s, ld_xr=ld_xr):
                        vop(v, lambda: v.tensor_mul(sg[:, sp, :],
                                                    ps8[:, tt, :],
                                                    gq[:, tt, :]),
                            waits=[("t", t_s), ("sd", C["sd"])])
                        vop(v, lambda: v.tensor_add(sg[:, sp, :],
                                                    sg[:, sp, :],
                                                    xres[:, tt, :]),
                            waits=[("sd", ld_xr)])
                    v_o = C["v"]

                    @block.gpsimd
                    def _(g, tg=tg, tt=tt, sp=sp, v_o=v_o):
                        dma(g, "sd",
                            out_cols[tg * 1024 + tt * 128:
                                     tg * 1024 + (tt + 1) * 128, :],
                            sg[:, sp, :], waits=[("v", v_o)])
                    sg_hist[sp] = C["sd"]
                v_comb_last = C["v"]

            sd_total = C["sd"]

            @block.gpsimd
            def _(g):
                g.wait_ge(SEM["sd"], sd_total)

            @block.sync
            def _(sy):
                sy.wait_ge(SEM["sd"], sd_total)

    return nc


_NC_CACHE = None


def kernel(L0_attn_hidden_states, L0_attn_W_in, L0_attn_conv_w, L0_attn_conv_b,
           L0_attn_Wx, L0_attn_dt_ln_w, L0_attn_b_ln_w, L0_attn_c_ln_w,
           L0_attn_Wdt, L0_attn_dt_bias, L0_attn_A, L0_attn_D_param,
           L0_attn_W_out, L0_ffn_weight_q, L0_ffn_scales_and_zeros):
    global _NC_CACHE
    x = np.asarray(L0_attn_hidden_states, dtype=np.float32)
    w_in = np.asarray(L0_attn_W_in, dtype=np.float32)
    conv_w = np.asarray(L0_attn_conv_w, dtype=np.float32)
    conv_b = np.asarray(L0_attn_conv_b, dtype=np.float32)
    wx = np.asarray(L0_attn_Wx, dtype=np.float32)
    dt_ln = np.asarray(L0_attn_dt_ln_w, dtype=np.float32)
    b_ln = np.asarray(L0_attn_b_ln_w, dtype=np.float32)
    c_ln = np.asarray(L0_attn_c_ln_w, dtype=np.float32)
    wdt = np.asarray(L0_attn_Wdt, dtype=np.float32)
    dt_bias = np.asarray(L0_attn_dt_bias, dtype=np.float32)
    a_mat = np.asarray(L0_attn_A, dtype=np.float32)
    d_param = np.asarray(L0_attn_D_param, dtype=np.float32)
    w_out = np.asarray(L0_attn_W_out, dtype=np.float32)
    wq = np.asarray(L0_ffn_weight_q, dtype=np.int32)
    snz = np.asarray(L0_ffn_scales_and_zeros, dtype=np.float32)

    return _run(x, w_in, conv_w, conv_b, wx, dt_ln, b_ln, c_ln, wdt, dt_bias,
                a_mat, d_param, w_out, wq, snz)


def make_in_maps(x, w_in, conv_w, conv_b, wx, dt_ln, b_ln, c_ln, wdt, dt_bias,
                 a_mat, d_param, w_out, wq, snz):
    xf = x.reshape(TOK, H)
    ident_b = np.eye(128, dtype=ml_dtypes.bfloat16)
    ident_f = np.eye(128, dtype=np.float32)
    sel16 = np.zeros((16, 2048), dtype=ml_dtypes.bfloat16)
    for n in range(16):
        sel16[n, n * 128:(n + 1) * 128] = 1.0

    in_maps = []
    for c in range(NCORE):
        dlo = c * DL
        clo = c * COL
        w_slab = np.concatenate([w_in[dlo:dlo + DL, :],
                                 w_in[D + dlo:D + dlo + DL, :]], axis=0)
        wdt_local = wdt[dlo:dlo + DL, :]
        wx_local = wx[:, dlo:dlo + DL]
        wout_local = w_out[clo:clo + COL, :]
        wq_local = wq[clo:clo + COL, :]
        in_maps.append(dict(
            x_rows=np.ascontiguousarray(xf[c * 1024:(c + 1) * 1024, :]),
            x_cols=np.ascontiguousarray(xf[:, clo:clo + COL]),
            win_f=np.ascontiguousarray(
                w_slab.reshape(16, 128, 32, 128)
                .transpose(0, 3, 2, 1).reshape(16, 128, H)),
            conv_w=np.ascontiguousarray(
                conv_w[dlo:dlo + DL, :].reshape(8, 128, 4)
                .transpose(1, 0, 2).reshape(128, 32)),
            conv_b=np.ascontiguousarray(
                conv_b[dlo:dlo + DL].reshape(8, 128).T),
            wx_f=np.ascontiguousarray(
                wx_local.T.reshape(8, 128, 288)
                .transpose(1, 0, 2).reshape(128, 2304)),
            wdt_f=np.ascontiguousarray(
                wdt_local.reshape(8, 128, 2, 128)
                .transpose(3, 0, 2, 1).reshape(128, 2048)),
            dt_bias=np.ascontiguousarray(
                dt_bias[dlo:dlo + DL].reshape(8, 128).T),
            a_mat=np.ascontiguousarray(
                a_mat[dlo:dlo + DL, :].reshape(8, 128, NN)
                .transpose(1, 0, 2).reshape(128, 128)),
            d_param=np.ascontiguousarray(
                d_param[dlo:dlo + DL].reshape(8, 128).T),
            wout_f=np.ascontiguousarray(
                wout_local.T.reshape(64, 128, COL)
                .transpose(1, 0, 2).reshape(128, 64 * COL)),
            wqT=np.ascontiguousarray(
                wq_local.T.reshape(32, 128, COL)),
            sc_bc=np.ascontiguousarray(np.broadcast_to(
                snz[:, clo:clo + COL, 0][:, None, :], (32, 128, COL))),
            zr_bc=np.ascontiguousarray(np.broadcast_to(
                snz[:, clo:clo + COL, 1][:, None, :], (32, 128, COL))),
            lnw_in=np.ascontiguousarray(np.broadcast_to(
                np.concatenate([dt_ln, b_ln, c_ln])[None, :], (128, 288))),
            ident_b=ident_b, ident_f=ident_f, sel16=sel16,
        ))

    return in_maps


def _run(x, w_in, conv_w, conv_b, wx, dt_ln, b_ln, c_ln, wdt, dt_bias,
         a_mat, d_param, w_out, wq, snz):
    global _NC_CACHE
    in_maps = make_in_maps(x, w_in, conv_w, conv_b, wx, dt_ln, b_ln, c_ln,
                           wdt, dt_bias, a_mat, d_param, w_out, wq, snz)
    if _NC_CACHE is None:
        _NC_CACHE = build_program()
    res = run_bass_kernel_spmd(_NC_CACHE, in_maps, core_ids=list(range(NCORE)))
    parts = [res.results[c]["out_cols"] for c in range(NCORE)]
    out = np.concatenate(parts, axis=1)
    return out.reshape(2, LB, H)


# version 21
# speedup vs baseline: 1.0357x; 1.0357x over previous
import numpy as np
import ml_dtypes
import concourse.bass as bass
import concourse.mybir as mybir
from concourse.bass_utils import run_bass_kernel_spmd
from contextlib import ExitStack

F32 = mybir.dt.float32
BF16 = mybir.dt.bfloat16
I32 = mybir.dt.int32
AF = mybir.ActivationFunctionType
ALU = mybir.AluOpType
AX = mybir.AxisListType

H = 4096
D = 8192
DL = 1024
NN = 16
R = 256
TOK = 8192
LB = 4096
COL = 512
NCORE = 8
EPS_LN = 1e-5
EPS_RMS = 1e-6


def build_program(upto=99):
    nc = bass.Bass("TRN2", target_bir_lowering=False, debug=False,
                   detect_race_conditions=False)

    def ein(name, shape, dt=F32):
        return nc.dram_tensor(name, shape, dt, kind="ExternalInput").ap()

    x_rows = ein("x_rows", [1024, H])
    x_cols = ein("x_cols", [TOK, COL])
    win_f = ein("win_f", [16, 128, H])
    conv_w = ein("conv_w", [128, 32])
    conv_b = ein("conv_b", [128, 8])
    wx_f = ein("wx_f", [128, 2304])
    wdt_f = ein("wdt_f", [128, 2048])
    dt_bias = ein("dt_bias", [128, 8])
    a_mat = ein("a_mat", [128, 128])
    d_param = ein("d_param", [128, 8])
    wout_f = ein("wout_f", [128, 64 * COL])
    wqT = ein("wqT", [32, 128, COL], I32)
    sc_bc = ein("sc_bc", [32, 128, COL])
    zr_bc = ein("zr_bc", [32, 128, COL])
    lnw_in = ein("lnw_in", [128, 288])
    ident_b = ein("ident_b", [128, 128], BF16)
    ident_f = ein("ident_f", [128, 128], F32)
    sel16 = ein("sel16", [16, 2048], BF16)

    out_cols = nc.dram_tensor("out_cols", [TOK, COL], F32,
                              kind="ExternalOutput").ap()

    def scr(name, shape, dt=BF16):
        return nc.dram_tensor(name, shape, dt).ap()

    xnT_mine = scr("xnT_mine", [H, 1024])
    xnT_parts = scr("xnT_parts", [NCORE, H, 1024])
    win_ld = scr("win_ld", [16, 128, H])
    wdt_ld = scr("wdt_ld", [128, 2048])
    wx_ld = scr("wx_ld", [128, 2304])
    wout_ld = scr("wout_ld", [128, 64, COL])
    wdeq_ld = scr("wdeq_ld", [32, 128, COL])
    uT_sp = scr("uT_sp", [DL, TOK])
    gsT_sp = scr("gsT_sp", [DL, TOK])
    sspT_mine = scr("sspT_mine", [288, TOK], F32)
    sspT_sum = scr("sspT_sum", [288, TOK], F32)
    yT_mine = [scr(f"yT_mine{h}", [DL, LB]) for h in range(2)]
    yT_parts = [scr(f"yT_parts{h}", [NCORE, DL, LB]) for h in range(2)]
    dtnT_sp = scr("dtnT_sp", [128, 2, TOK])
    gq_sp = scr("gq_sp", [64, 128, COL])
    BT_sp = scr("BT_sp", [16, TOK])
    CT_sp = scr("CT_sp", [16, TOK])

    st = ExitStack()
    with st:
        e = st.enter_context

        A_sb = e(nc.sbuf_tensor("A_sb", [128, 128], F32))
        cw_sb = e(nc.sbuf_tensor("cw_sb", [128, 32], F32))
        cb_sb = e(nc.sbuf_tensor("cb_sb", [128, 8], F32))
        dp_sb = e(nc.sbuf_tensor("dp_sb", [128, 8], F32))
        dtb_sb = e(nc.sbuf_tensor("dtb_sb", [128, 8], F32))
        identb = e(nc.sbuf_tensor("identb", [128, 128], BF16))
        identf = e(nc.sbuf_tensor("identf", [128, 128], F32))
        sel_sb = e(nc.sbuf_tensor("sel_sb", [16, 2048], BF16))
        lnw_bc = e(nc.sbuf_tensor("lnw_bc", [128, 288], F32))
        hst_sb = e(nc.sbuf_tensor("hst_sb", [128, 8, 16], BF16))
        halo_sb = e(nc.sbuf_tensor("halo_sb", [128, 8, 4], F32))
        kconst = e(nc.sbuf_tensor("kconst", [128, 8], F32))

        SEM = {}
        for k in ["t", "v", "s", "p", "ld", "l2", "sd",
                  "w0", "w1", "w2", "w3", "s0", "s1", "u0", "u1", "qq",
                  "y0", "y1", "y2", "y3"]:
            SEM[k] = e(nc.semaphore("sem_" + k))
        cc_sems = [e(nc.semaphore(f"sem_cc{i}")) for i in range(4)]
        C = {k: 0 for k in SEM}
        block = e(nc.Block())

        def tick(instr, k, by=1):
            C[k] += by
            instr.then_inc(SEM[k], by)
            return C[k]

        def mkop(key):
            def op(eng, emit, waits=()):
                for wk, wv in waits:
                    eng.wait_ge(SEM[wk], wv)
                eng.wait_ge(SEM[key], C[key])
                return tick(emit(), key)
            return op
        vop = mkop("v")
        sop = mkop("s")
        pop = mkop("p")

        def dma(eng, key, out, in_, waits=()):
            for wk, wv in waits:
                eng.wait_ge(SEM[wk], wv)
            return tick(eng.dma_start(out=out, in_=in_), key, 16)

        def fence_all():
            @block.sync
            def _(sy):
                for k in C:
                    sy.wait_ge(SEM[k], C[k])

        _CONSTS = [EPS_LN, EPS_RMS, 1.0 / H, 1.0 / R, 1.0 / 16.0]

        @block.vector
        def _(v):
            for i, val in enumerate(_CONSTS):
                vop(v, lambda i=i, val=val: v.memset(kconst[:, i:i + 1], val))
        for i, val in enumerate(_CONSTS):
            nc.const_aps.register(F32, val, kconst[:, i:i + 1])

        @block.sync
        def _(sy):
            dma(sy, "ld", A_sb[:], a_mat[:])
            dma(sy, "ld", cw_sb[:], conv_w[:])
            dma(sy, "ld", cb_sb[:], conv_b[:])
            dma(sy, "ld", dp_sb[:], d_param[:])
            dma(sy, "ld", dtb_sb[:], dt_bias[:])
            dma(sy, "ld", identb[:], ident_b[:])
            dma(sy, "ld", identf[:], ident_f[:])
            dma(sy, "ld", sel_sb[:], sel16[:])
            dma(sy, "ld", lnw_bc[:], lnw_in[:])
        ld_consts = C["ld"]
        v_lnbc = C["v"]

        if upto <= 0:
            fence_all()
            return nc


        if upto <= 1:
            fence_all()
            return nc

        with ExitStack() as p2:
            e2 = p2.enter_context
            xtile = [e2(nc.sbuf_tensor(f"p2_xt{i}", [128, H], F32)) for i in range(2)]
            xn_bf = [e2(nc.sbuf_tensor(f"p2_xn{i}", [128, H], BF16)) for i in range(8)]
            stat = e2(nc.sbuf_tensor("p2_stat", [128, 8], F32))
            sq = e2(nc.sbuf_tensor("p2_sq", [128, H], F32))
            band2 = e2(nc.sbuf_tensor("p2_band", [128, 1024], BF16))
            ps_tr = e2(nc.psum_tensor("p2_pstr", [128, 1024], BF16))
            for tt in range(8):
                @block.sync
                def _(sy, tt=tt):
                    dma(sy, "ld", xtile[tt % 2][:],
                        x_rows[tt * 128:(tt + 1) * 128, :],
                        waits=[("v", C["v"]), ("s", C["s"])])
                ld_here = C["ld"]
                xt = xtile[tt % 2]

                @block.vector
                def _(v, xt=xt, ld_here=ld_here):
                    vop(v, lambda: v.reduce_sum(stat[:, 0:1], xt[:], axis=AX.X),
                        waits=[("ld", ld_here)])
                v_m1 = C["v"]

                @block.scalar
                def _(s, xt=xt, v_m1=v_m1, ld_here=ld_here):
                    sop(s, lambda: s.mul(stat[:, 1:2], stat[:, 0:1], 1.0 / H),
                        waits=[("v", v_m1)])
                    sop(s, lambda: s.activation(sq[:], xt[:], AF.Square),
                        waits=[("ld", ld_here)])
                s_sq = C["s"]

                @block.vector
                def _(v, s_sq=s_sq):
                    vop(v, lambda: v.reduce_sum(stat[:, 2:3], sq[:], axis=AX.X),
                        waits=[("s", s_sq)])
                    vop(v, lambda: v.tensor_mul(stat[:, 3:4], stat[:, 1:2],
                                                stat[:, 1:2]))
                v_m2 = C["v"]

                @block.scalar
                def _(s, v_m2=v_m2):
                    sop(s, lambda: s.mul(stat[:, 4:5], stat[:, 2:3], 1.0 / H),
                        waits=[("v", v_m2)])
                s_ms = C["s"]

                @block.vector
                def _(v, s_ms=s_ms):
                    vop(v, lambda: v.tensor_sub(stat[:, 5:6], stat[:, 4:5],
                                                stat[:, 3:4]),
                        waits=[("s", s_ms)])
                v_var = C["v"]

                @block.scalar
                def _(s, v_var=v_var):
                    sop(s, lambda: s.activation(stat[:, 6:7], stat[:, 5:6],
                                                AF.Ln, bias=EPS_LN),
                        waits=[("v", v_var)])
                    sop(s, lambda: s.activation(stat[:, 7:8], stat[:, 6:7],
                                                AF.Exp, scale=-0.5))
                s_rstd = C["s"]

                @block.vector
                def _(v, tt=tt, xt=xt, s_rstd=s_rstd):
                    vop(v, lambda: v.tensor_scalar(
                        xn_bf[tt][:], xt[:], stat[:, 1:2], stat[:, 7:8],
                        op0=ALU.subtract, op1=ALU.mult),
                        waits=[("s", s_rstd)])
            v_xn = C["v"]

            for hb in range(32):
                @block.tensor
                def _(t, hb=hb, v_xn=v_xn):
                    t.wait_ge(SEM["v"], v_xn if hb == 0 else C["v"])
                    for tt in range(8):
                        last = t.transpose(ps_tr[:, tt * 128:(tt + 1) * 128],
                                    xn_bf[tt][:, hb * 128:(hb + 1) * 128],
                                    identb[:])
                    tick(last, "t")
                t_tr = C["t"]

                @block.vector
                def _(v, t_tr=t_tr):
                    vop(v, lambda: v.tensor_copy(band2[:], ps_tr[:]),
                        waits=[("t", t_tr), ("sd", C["sd"])])
                v_b = C["v"]

                @block.gpsimd
                def _(g, hb=hb, v_b=v_b):
                    dma(g, "sd", xnT_mine[hb * 128:(hb + 1) * 128, :],
                        band2[:], waits=[("v", v_b)])
            sd_xnT = C["sd"]

            @block.gpsimd
            def _(g):
                g.wait_ge(SEM["sd"], sd_xnT)
                g.collective_compute(
                    "AllGather", ALU.bypass,
                    replica_groups=[list(range(NCORE))],
                    ins=[xnT_mine.opt()],
                    outs=[xnT_parts.rearrange("r p t -> (r p) t").opt()],
                ).then_inc(cc_sems[0])

        with ExitStack() as p1:
            e1 = p1.enter_context
            wtile = [e1(nc.sbuf_tensor(f"p1_wt{i}", [128, 4096], F32))
                     for i in range(2)]
            wcast = [e1(nc.sbuf_tensor(f"p1_wc{i}", [128, 4096], BF16))
                     for i in range(2)]
            qtl = e1(nc.sbuf_tensor("p1_qt", [128, 2, COL], I32))
            sctl = e1(nc.sbuf_tensor("p1_sc", [128, 2, COL], F32))
            zrtl = e1(nc.sbuf_tensor("p1_zr", [128, 2, COL], F32))
            qf = e1(nc.sbuf_tensor("p1_qf", [128, 2, COL], F32))
            wqc = e1(nc.sbuf_tensor("p1_wqc", [128, 2, COL], BF16))
            h_cast = {0: 0, 1: 0}
            h_cst = {0: 0, 1: 0}

            cast_jobs = []
            for rb in range(16):
                cast_jobs.append((4096, win_f[rb],
                                  win_ld[rb]))
            cast_jobs.append((2048, wdt_f[:], wdt_ld[:]))
            cast_jobs.append((2304, wx_f[:], wx_ld[:]))
            for g8 in range(8):
                cast_jobs.append((4096, wout_f[:, g8 * 4096:(g8 + 1) * 4096],
                                  wout_ld[:, g8 * 8:(g8 + 1) * 8, :]
                                  .rearrange("p a b -> p (a b)")))

            for i, (wdth, srcap, dstap) in enumerate(cast_jobs):
                par = i % 2
                psem = "y%d" % par
                @block.sync
                def _(sy, i=i, par=par, psem=psem, srcap=srcap, wdth=wdth):
                    w = [("v", h_cast[par])]
                    if i < 2:
                        w += [("t", C["t"]), ("v", C["v"]), ("s", C["s"]),
                              ("sd", C["sd"])]
                    dma(sy, psem, wtile[par][:, 0:wdth], srcap, waits=w)
                ld_h = C[psem]

                @block.vector
                def _(v, par=par, psem=psem, wdth=wdth, ld_h=ld_h):
                    vop(v, lambda: v.tensor_copy(wcast[par][:, 0:wdth],
                                                 wtile[par][:, 0:wdth]),
                        waits=[(psem, ld_h), ("sd", C["sd"])])
                h_cast[par] = C["v"]
                v_c = C["v"]

                @block.gpsimd
                def _(g, par=par, wdth=wdth, dstap=dstap, v_c=v_c):
                    dma(g, "sd", dstap, wcast[par][:, 0:wdth],
                        waits=[("v", v_c)])
                h_cst[par] = C["sd"]
                if i == 15:
                    sd_win = C["sd"]

            for i in range(16):
                @block.sync
                def _(sy, i=i):
                    w = ([("v", C["v"])] if i >= 1 else
                         [("t", C["t"]), ("v", C["v"]), ("s", C["s"]),
                          ("sd", C["sd"])])
                    dma(sy, "ld", qtl.ap(),
                        wqT[2 * i:2 * i + 2].rearrange("k p c -> p k c"),
                        waits=w)
                    dma(sy, "ld", sctl.ap(),
                        sc_bc[2 * i:2 * i + 2].rearrange("k p c -> p k c"))
                    dma(sy, "ld", zrtl.ap(),
                        zr_bc[2 * i:2 * i + 2].rearrange("k p c -> p k c"))
                ld_h = C["ld"]

                @block.vector
                def _(v, i=i, ld_h=ld_h):
                    vop(v, lambda: v.tensor_copy(
                        qf.ap().rearrange("p a b -> p (a b)"),
                        qtl.ap().rearrange("p a b -> p (a b)")),
                        waits=[("ld", ld_h), ("sd", C["sd"])])
                    vop(v, lambda: v.scalar_tensor_tensor(
                        qf.ap().rearrange("p a b -> p (a b)"),
                        qf.ap().rearrange("p a b -> p (a b)"),
                        8.0,
                        sctl.ap().rearrange("p a b -> p (a b)"),
                        op0=ALU.subtract, op1=ALU.mult))
                    vop(v, lambda: v.tensor_add(
                        wqc.ap().rearrange("p a b -> p (a b)"),
                        qf.ap().rearrange("p a b -> p (a b)"),
                        zrtl.ap().rearrange("p a b -> p (a b)")))
                v_c = C["v"]

                @block.gpsimd
                def _(g, i=i, v_c=v_c):
                    dma(g, "sd",
                        wdeq_ld[2 * i:2 * i + 2].rearrange("k p c -> p k c"),
                        wqc.ap(), waits=[("v", v_c)])
        sd_prep = C["sd"]

        @block.gpsimd
        def _(g):
            g.wait_ge(cc_sems[0], 1)
            tick(g.engine_nop(), "p")
        p_ag0 = C["p"]

        with ExitStack() as pf:
            ef = pf.enter_context
            wqs = ef(nc.sbuf_tensor("pf_wq", [128, 32, COL], BF16))
            xwf = ef(nc.sbuf_tensor("pf_xw", [128, 4, 1024], BF16))
            gq_st = ef(nc.sbuf_tensor("pf_gst", [128, 8, COL], BF16))
            ps8f = ef(nc.psum_tensor("pf_ps", [128, 8, 512], F32))

            @block.sync
            def _(sy):
                for kb in range(32):
                    dma(sy, "ld", wqs[:, kb, :], wdeq_ld[kb],
                        waits=([("sd", sd_prep)] if kb == 0 else ()))
            ld_wq = C["ld"]

            x_hist = {i: 0 for i in range(4)}
            s_gq_last = 0
            for tg in range(8):
                for kb in range(32):
                    slot = kb % 4
                    qsel = kb % 2
                    eng_blk = block.sync if qsel == 0 else block.scalar
                    xsem = "w%d" % slot
                    @eng_blk
                    def _(sy, tg=tg, kb=kb, slot=slot, xsem=xsem):
                        w = [("t", x_hist[slot])]
                        if tg == 0 and kb < 2:
                            w.append(("p", p_ag0))
                            w.append(("sd", sd_prep))
                        dma(sy, xsem, xwf[:, slot, :],
                            xnT_parts[tg, kb * 128:(kb + 1) * 128, :],
                            waits=w)
                    ld_x = C[xsem]

                    @block.tensor
                    def _(t, kb=kb, slot=slot, xsem=xsem, ld_x=ld_x,
                          s_gq_last=s_gq_last):
                        t.wait_ge(SEM[xsem], ld_x)
                        if kb == 0:
                            t.wait_ge(SEM["ld"], ld_wq)
                            t.wait_ge(SEM["s"], s_gq_last)
                        for tt in range(8):
                            last = t.matmul(ps8f[:, tt, :],
                                     xwf[:, slot, tt * 128:(tt + 1) * 128],
                                     wqs[:, kb, :],
                                     start=(kb == 0), stop=(kb == 31))
                        tick(last, "t")
                    x_hist[slot] = C["t"]
                t_ffn = C["t"]

                @block.scalar
                def _(sc, tg=tg, t_ffn=t_ffn):
                    for tt in range(8):
                        sop(sc, lambda tt=tt: sc.activation(
                            gq_st[:, tt, :], ps8f[:, tt, :], AF.Silu),
                            waits=([("t", t_ffn), ("sd", C["sd"])]
                                   if tt == 0 else ()))
                s_gq_last = C["s"]

                @block.gpsimd
                def _(g, tg=tg, s_gq_last=s_gq_last):
                    dma(g, "sd", gq_sp[tg * 8:tg * 8 + 8]
                        .rearrange("a p b -> p a b"),
                        gq_st.ap(), waits=[("s", s_gq_last)])
        sd_ffn = C["sd"]
        s_ffn = C["s"]

        if upto <= 2:
            @block.gpsimd
            def _(g):
                g.wait_ge(cc_sems[0], 1)
            fence_all()
            return nc

        with ExitStack() as p3:
            e3 = p3.enter_context
            xng = e3(nc.sbuf_tensor("p3_xng", [128, 32, 1024], BF16))
            wslab = [e3(nc.sbuf_tensor(f"p3_ws{i}", [128, 32, 128], BF16))
                     for i in range(2)]
            upre = e3(nc.sbuf_tensor("p3_upre", [128, 2, 1028], F32))
            uacc = e3(nc.sbuf_tensor("p3_uacc", [128, 2, 1024], F32))
            ub = e3(nc.sbuf_tensor("p3_ub", [128, 2, 1024], BF16))
            ps_a = e3(nc.psum_tensor("p3_psa", [128, 2, 2, 512], F32))

            @block.vector
            def _(v):
                vop(v, lambda: v.memset(halo_sb.ap().rearrange("p a b -> p (a b)"),
                                        0.0))

            h_evac = {0: 0, 1: 0}
            h_vsave = {0: 0, 1: 0}
            h_uread = {0: 0, 1: 0}
            h_store = {0: 0, 1: 0}
            h_wld = {0: 0, 1: 0}

            for grp in range(8):
                xbuf = xng
                gate = [("t", C["t"]), ("p", p_ag0)]
                if grp == 0:
                    gate += [("sd", C["sd"]), ("s", C["s"]), ("v", C["v"])]
                for qsel, blk in ((0, block.sync), (1, block.scalar)):
                    @blk
                    def _(sy, grp=grp, xbuf=xbuf, qsel=qsel, gate=gate):
                        first = True
                        for hb in range(32):
                            if hb % 2 != qsel:
                                continue
                            dma(sy, "w%d" % (hb // 8), xbuf[:, hb, :],
                                xnT_parts[grp, hb * 128:(hb + 1) * 128, :],
                                waits=(gate if first else ()))
                            first = False
                wv_cnt = [C["w%d" % i] for i in range(4)]

                for dt in range(16):
                    par = dt % 2
                    wbuf = wslab[par]
                    @block.sync
                    def _(sy, dt=dt, par=par, wbuf=wbuf, grp=grp):
                        w = [("t", h_wld[par])]
                        if grp == 0 and dt < 2:
                            w.append(("sd", sd_ffn))
                            w.append(("s", s_ffn))
                        dma(sy, "s%d" % par,
                            wbuf.ap().rearrange("p a b -> p (a b)"),
                            win_ld[dt], waits=w)
                    ld_w = C["s%d" % par]

                    @block.tensor
                    def _(t, dt=dt, par=par, wbuf=wbuf, xbuf=xbuf, grp=grp,
                          wv_cnt=wv_cnt, ld_w=ld_w):
                        t.wait_ge(SEM["s%d" % par], ld_w)
                        t.wait_ge(SEM["s"], h_evac[par])
                        for k in range(32):
                            if dt == 0 and k % 8 == 0:
                                t.wait_ge(SEM["w%d" % (k // 8)],
                                          wv_cnt[k // 8])
                            for bk in range(2):
                                last = t.matmul(ps_a[:, par, bk, :],
                                         wbuf[:, k, :],
                                         xbuf[:, k, bk * 512:(bk + 1) * 512],
                                         start=(k == 0), stop=(k == 31))
                        tick(last, "t")
                    t_mm = C["t"]
                    h_wld[par] = t_mm

                    ps_flat = ps_a.ap()[:, par].rearrange("p a b -> p (a b)")
                    if dt < 8:
                        @block.scalar
                        def _(s, par=par, t_mm=t_mm, ps_flat=ps_flat):
                            sop(s, lambda: s.copy(upre[:, par, 4:1028],
                                                  ps_flat),
                                waits=[("t", t_mm), ("v", h_vsave[par])])
                        s_evac = C["s"]
                        h_evac[par] = s_evac

                        @block.vector
                        def _(v, dt=dt, par=par, s_evac=s_evac):
                            vop(v, lambda: v.tensor_copy(upre[:, par, 1:4],
                                                         halo_sb[:, dt, 1:4]),
                                waits=[("s", s_evac)])
                            vop(v, lambda: v.tensor_scalar(
                                uacc[:, par, :], upre[:, par, 4:1028],
                                cw_sb[:, dt * 4 + 3:dt * 4 + 4], None,
                                op0=ALU.mult),
                                waits=[("s", h_uread[par])])
                            for kk in range(3):
                                vop(v, lambda kk=kk: v.scalar_tensor_tensor(
                                    uacc[:, par, :],
                                    upre[:, par, 3 - kk:1027 - kk],
                                    cw_sb[:, dt * 4 + 2 - kk:dt * 4 + 3 - kk],
                                    uacc[:, par, :],
                                    op0=ALU.mult, op1=ALU.add))
                            vop(v, lambda: v.tensor_copy(
                                halo_sb[:, dt, 1:4],
                                upre[:, par, 1025:1028]))
                        v_savehalo = C["v"]
                        h_vsave[par] = v_savehalo

                        @block.scalar
                        def _(s, dt=dt, par=par, v_savehalo=v_savehalo):
                            sop(s, lambda: s.activation(
                                ub[:, par, :], uacc[:, par, :], AF.Silu,
                                bias=cb_sb[:, dt:dt + 1]),
                                waits=[("v", v_savehalo),
                                       ("sd", C["sd"])])
                        s_u = C["s"]
                        h_uread[par] = s_u

                        @block.gpsimd
                        def _(g, grp=grp, dt=dt, par=par, s_u=s_u):
                            dma(g, "sd",
                                uT_sp[dt * 128:(dt + 1) * 128,
                                      grp * 1024:(grp + 1) * 1024],
                                ub[:, par, :], waits=[("s", s_u)])
                        h_store[par] = C["sd"]
                    else:
                        @block.scalar
                        def _(s, par=par, t_mm=t_mm, ps_flat=ps_flat):
                            sop(s, lambda: s.activation(
                                ub[:, par, :], ps_flat, AF.Silu),
                                waits=[("t", t_mm), ("sd", C["sd"])])
                        s_g = C["s"]
                        h_evac[par] = s_g

                        @block.gpsimd
                        def _(g, grp=grp, dt=dt, par=par, s_g=s_g):
                            dma(g, "sd",
                                gsT_sp[(dt - 8) * 128:(dt - 7) * 128,
                                       grp * 1024:(grp + 1) * 1024],
                                ub[:, par, :], waits=[("s", s_g)])
                        h_store[par] = C["sd"]

                if grp == 3:
                    @block.vector
                    def _(v):
                        vop(v, lambda: v.memset(
                            halo_sb.ap().rearrange("p a b -> p (a b)"), 0.0))

        if upto <= 3:
            fence_all()
            return nc

        with ExitStack() as p4:
            e4 = p4.enter_context
            ubig = [e4(nc.sbuf_tensor(f"p4_u{i}", [128, 8, 2048], BF16))
                    for i in range(2)]
            wxs = e4(nc.sbuf_tensor("p4_wx", [128, 8, 288], BF16))
            ev = e4(nc.sbuf_tensor("p4_ev", [128, 2, 2048], F32))
            ps_m = e4(nc.psum_tensor("p4_psm", [128, 2, 4, 512], F32))

            @block.sync
            def _(sy):
                dma(sy, "ld", wxs.ap().rearrange("p a b -> p (a b)"),
                    wx_ld[:],
                    waits=[("t", C["t"]), ("s", C["s"]), ("v", C["v"]),
                           ("sd", C["sd"])])
            ld_wx = C["ld"]

            h_u4 = {0: 0, 1: 0}
            h_pm = {0: 0, 1: 0}
            h_ev = {0: 0, 1: 0}
            mi = 0
            for qt in range(4):
                ubuf = ubig[qt % 2]
                usem = "u%d" % (qt % 2)
                @block.sync
                def _(sy, qt=qt, ubuf=ubuf, usem=usem):
                    w = [("t", h_u4[qt % 2]), ("sd", C["sd"])]
                    for kb in range(8):
                        dma(sy, usem, ubuf[:, kb, :],
                            uT_sp[kb * 128:(kb + 1) * 128,
                                  qt * 2048:(qt + 1) * 2048],
                            waits=(w if kb == 0 else ()))
                ld_u = C[usem]

                for mt in range(3):
                    mrows = 128 if mt < 2 else 32
                    mp = mi % 2
                    @block.tensor
                    def _(t, qt=qt, mt=mt, mp=mp, mrows=mrows, ubuf=ubuf,
                          usem=usem, ld_u=ld_u):
                        t.wait_ge(SEM[usem], ld_u)
                        t.wait_ge(SEM["ld"], ld_wx)
                        t.wait_ge(SEM["v"], h_pm[mp])
                        for k in range(8):
                            for bk in range(4):
                                last = t.matmul(ps_m[0:mrows, mp, bk, :],
                                         wxs[:, k, mt * 128:mt * 128 + mrows],
                                         ubuf[:, k, bk * 512:(bk + 1) * 512],
                                         start=(k == 0), stop=(k == 7))
                        tick(last, "t")
                    t_mm = C["t"]
                    h_u4[qt % 2] = t_mm

                    @block.vector
                    def _(v, mp=mp, mrows=mrows, t_mm=t_mm):
                        vop(v, lambda: v.tensor_copy(
                            ev[0:mrows, mp, :],
                            ps_m.ap()[0:mrows, mp]
                            .rearrange("p a b -> p (a b)")),
                            waits=[("t", t_mm), ("sd", C["sd"])])
                    v_ev = C["v"]
                    h_pm[mp] = v_ev

                    @block.gpsimd
                    def _(g, qt=qt, mt=mt, mp=mp, mrows=mrows, v_ev=v_ev):
                        dma(g, "sd",
                            sspT_mine[mt * 128:mt * 128 + mrows,
                                      qt * 2048:(qt + 1) * 2048],
                            ev[0:mrows, mp, :], waits=[("v", v_ev)])
                    h_ev[mp] = C["sd"]
                    mi += 1
            sd_ssp = C["sd"]

            @block.gpsimd
            def _(g):
                g.wait_ge(SEM["sd"], sd_ssp)
                g.collective_compute(
                    "AllReduce", ALU.add,
                    replica_groups=[list(range(NCORE))],
                    ins=[sspT_mine.opt()], outs=[sspT_sum.opt()],
                ).then_inc(cc_sems[1])

        if upto <= 4:
            @block.gpsimd
            def _(g):
                g.wait_ge(cc_sems[1], 1)
            fence_all()
            return nc

        with ExitStack() as p5:
            e5 = p5.enter_context
            ssp = [e5(nc.sbuf_tensor(f"p5_ssp{i}", [128, TOK // 2], F32))
                   for i in range(4)]
            ssp3 = e5(nc.sbuf_tensor("p5_sspC", [32, TOK], F32))
            tok = e5(nc.sbuf_tensor("p5_tok", [128, 288], F32))
            dtn = e5(nc.sbuf_tensor("p5_dtn", [128, 288], BF16))
            nrm = e5(nc.sbuf_tensor("p5_nrm", [128, 8], F32))
            sqv = e5(nc.sbuf_tensor("p5_sq", [128, 288], F32))
            ps_f = e5(nc.psum_tensor("p5_psf", [128, 512], F32))
            ps_b = e5(nc.psum_tensor("p5_psb", [128, 2, 512], BF16))
            stage5 = e5(nc.sbuf_tensor("p5_stage", [128, 2, 512], BF16))
            sd_hist5 = {}

            @block.gpsimd
            def _(g):
                g.wait_ge(cc_sems[1], 1)
                tick(g.engine_nop(), "p")
            p_ar = C["p"]

            @block.sync
            def _(sy):
                w = [("p", p_ar), ("v", C["v"]), ("t", C["t"])]
                for i in range(4):
                    dma(sy, "ld", ssp[i][:],
                        sspT_sum[(i % 2) * 128:(i % 2) * 128 + 128,
                                 (i // 2) * 4096:(i // 2) * 4096 + 4096],
                        waits=(w if i == 0 else ()))
                dma(sy, "ld", ssp3[:], sspT_sum[256:288, :])
            ld_ssp = C["ld"]

            for tt in range(64):
                half = (tt * 128) // 4096
                off = (tt * 128) % 4096
                @block.tensor
                def _(t, tt=tt, half=half, off=off, ld_ssp=ld_ssp):
                    t.wait_ge(SEM["ld"], ld_ssp)
                    t.wait_ge(SEM["v"], C["v"])
                    last = t.transpose(ps_f[:, 0:128],
                                ssp[2 * half][:, off:off + 128], identf[:])
                    last = t.transpose(ps_f[:, 128:256],
                                ssp[2 * half + 1][:, off:off + 128], identf[:])
                    last = t.transpose(ps_f[:, 256:288],
                                ssp3[:, tt * 128:tt * 128 + 128],
                                identf[0:32, 0:32])
                    tick(last, "t")
                t_tr = C["t"]

                @block.vector
                def _(v, t_tr=t_tr):
                    vop(v, lambda: v.tensor_copy(tok[:], ps_f[:, 0:288]),
                        waits=[("t", t_tr)])
                v_tok = C["v"]

                @block.scalar
                def _(s, v_tok=v_tok):
                    sop(s, lambda: s.activation(sqv[:], tok[:], AF.Square),
                        waits=[("v", v_tok)])
                s_sq = C["s"]

                @block.vector
                def _(v, s_sq=s_sq):
                    vop(v, lambda: v.reduce_sum(nrm[:, 0:1], sqv[:, 0:R],
                                                axis=AX.X), waits=[("s", s_sq)])
                    vop(v, lambda: v.reduce_sum(nrm[:, 1:2], sqv[:, R:R + 16],
                                                axis=AX.X))
                    vop(v, lambda: v.reduce_sum(nrm[:, 2:3], sqv[:, R + 16:288],
                                                axis=AX.X))
                v_rs = C["v"]

                @block.scalar
                def _(s, v_rs=v_rs):
                    for j, w_ in [(0, R), (1, 16), (2, 16)]:
                        sop(s, lambda j=j, w_=w_: s.activation(
                            nrm[:, 3 + j:4 + j], nrm[:, j:j + 1], AF.Ln,
                            scale=1.0 / w_, bias=EPS_RMS),
                            waits=([("v", v_rs)] if j == 0 else ()))
                        sop(s, lambda j=j: s.activation(
                            nrm[:, 3 + j:4 + j], nrm[:, 3 + j:4 + j], AF.Exp,
                            scale=-0.5))
                s_r = C["s"]

                @block.vector
                def _(v, s_r=s_r):
                    vop(v, lambda: v.tensor_scalar(
                        tok[:, 0:R], tok[:, 0:R], nrm[:, 3:4], None,
                        op0=ALU.mult), waits=[("s", s_r)])
                    vop(v, lambda: v.tensor_scalar(
                        tok[:, R:R + 16], tok[:, R:R + 16], nrm[:, 4:5], None,
                        op0=ALU.mult))
                    vop(v, lambda: v.tensor_scalar(
                        tok[:, R + 16:288], tok[:, R + 16:288], nrm[:, 5:6],
                        None, op0=ALU.mult))
                    vop(v, lambda: v.tensor_mul(dtn[:], tok[:], lnw_bc[:]),
                        waits=[("ld", ld_consts)])
                v_dtn = C["v"]

                par = tt % 2
                @block.tensor
                def _(t, tt=tt, par=par, v_dtn=v_dtn):
                    t.wait_ge(SEM["v"], v_dtn)
                    last = t.transpose(ps_b[:, par, 0:128], dtn[:, 0:128],
                                       identb[:])
                    last = t.transpose(ps_b[:, par, 128:256], dtn[:, 128:256],
                                       identb[:])
                    last = t.transpose(ps_b[0:16, par, 256:384],
                                       dtn[:, 256:272], identb[:])
                    last = t.transpose(ps_b[0:16, par, 384:512],
                                       dtn[:, 272:288], identb[:])
                    tick(last, "t")
                t_tr2 = C["t"]

                @block.vector
                def _(v, tt=tt, par=par, t_tr2=t_tr2):
                    w = [("t", t_tr2), ("sd", C["sd"])]
                    vop(v, lambda: v.tensor_copy(stage5[:, par, :],
                                                 ps_b[:, par, :]), waits=w)
                v_stage = C["v"]

                @block.gpsimd
                def _(g, tt=tt, par=par, v_stage=v_stage):
                    dma(g, "sd", dtnT_sp[:, :, tt * 128:(tt + 1) * 128],
                        stage5.ap()[:, par, 0:256]
                        .rearrange("p (a b) -> p a b", a=2),
                        waits=[("v", v_stage)])
                    dma(g, "sd", BT_sp[:, tt * 128:(tt + 1) * 128],
                        stage5[0:16, par, 256:384])
                    dma(g, "sd", CT_sp[:, tt * 128:(tt + 1) * 128],
                        stage5[0:16, par, 384:512])
                sd_hist5[tt] = C["sd"]
        v_p5 = C["v"]
        sd_p5 = C["sd"]

        if upto <= 5:
            fence_all()
            return nc

        with ExitStack() as p6:
            e6 = p6.enter_context
            QW = 1024
            wdts = e6(nc.sbuf_tensor("p6_wdt", [128, 8, 2, 128], BF16))
            dtn_q = e6(nc.sbuf_tensor("p6_dtnq", [128, 2, 2, QW], BF16))
            btq = e6(nc.sbuf_tensor("p6_btq", [16, 2, QW], BF16))
            ctq = e6(nc.sbuf_tensor("p6_ctq", [16, 2, QW], BF16))
            Bbc = e6(nc.sbuf_tensor("p6_B", [128, 8, QW], BF16))
            Cbc = e6(nc.sbuf_tensor("p6_C", [128, 8, QW], BF16))
            dl_all = e6(nc.sbuf_tensor("p6_dla", [128, 8, QW], BF16))
            du_all = e6(nc.sbuf_tensor("p6_dua", [128, 8, QW], BF16))
            a_buf = e6(nc.sbuf_tensor("p6_ab", [128, 8, QW], BF16))
            h4 = e6(nc.sbuf_tensor("p6_h", [128, 4, QW], BF16))
            bsc2 = e6(nc.sbuf_tensor("p6_bs", [128, 2, QW], BF16))
            ptmp = e6(nc.sbuf_tensor("p6_pt", [128, 2, QW], BF16))
            yac = e6(nc.sbuf_tensor("p6_yac", [128, 8, QW], BF16))
            u_b = e6(nc.sbuf_tensor("p6_u", [128, QW], BF16))
            gs_b = e6(nc.sbuf_tensor("p6_gs", [128, QW], BF16))
            yf = e6(nc.sbuf_tensor("p6_yf", [128, QW], F32))
            yo = e6(nc.sbuf_tensor("p6_yo", [128, QW], BF16))
            az = e6(nc.sbuf_tensor("p6_az", [128, QW], F32))
            zp2 = e6(nc.sbuf_tensor("p6_zp", [128, 2, QW], F32))
            el2 = e6(nc.sbuf_tensor("p6_el", [128, 2, QW], F32))
            ps_d = e6(nc.psum_tensor("p6_psd", [128, 2, 512], F32))
            ps_bb = e6(nc.psum_tensor("p6_psbb", [128, 2, 2, 512], F32))

            @block.sync
            def _(sy):
                dma(sy, "qq",
                    wdts.ap().rearrange("p a b c -> p (a b c)"), wdt_ld[:],
                    waits=[("t", C["t"]), ("v", C["v"]), ("sd", C["sd"])])
            ld_wdt = C["qq"]

            psd_flat = ps_d.ap().rearrange("p a b -> p (a b)")
            v_scan_hist = {j: 0 for j in range(8)}
            v_bc_war = {0: 0, 1: 0}
            v_az_last = 0

            for b in range(2):
                for qt in range(4):
                    qi = b * 4 + qt
                    t0 = b * LB + qt * QW
                    qpar = qi % 2
                    @block.sync
                    def _(sy, t0=t0, qpar=qpar):
                        w = [("sd", sd_p5), ("t", C["t"])]
                        dma(sy, "qq", dtn_q.ap()[:, qpar, :, :],
                            dtnT_sp[:, :, t0:t0 + QW], waits=w)
                        dma(sy, "qq", btq[:, qpar, :], BT_sp[:, t0:t0 + QW])
                        dma(sy, "qq", ctq[:, qpar, :], CT_sp[:, t0:t0 + QW])
                    ld_q = C["qq"]

                    v_dl = {}
                    for sl in range(8):
                        spar = sl % 2
                        @block.tensor
                        def _(t, sl=sl, qpar=qpar, ld_q=ld_q):
                            t.wait_ge(SEM["qq"], ld_q)
                            t.wait_ge(SEM["v"], v_az_last)
                            for k in range(2):
                                for bk in range(2):
                                    last = t.matmul(ps_d[:, bk, :],
                                             wdts[:, sl, k, :],
                                             dtn_q[:, qpar, k,
                                                   bk * 512:(bk + 1) * 512],
                                             start=(k == 0), stop=(k == 1))
                            tick(last, "t")
                        t_d = C["t"]

                        @block.sync
                        def _(sy, sl=sl, t0=t0):
                            dma(sy, "ld", u_b[:],
                                uT_sp[sl * 128:(sl + 1) * 128, t0:t0 + QW],
                                waits=[("v", C["v"]), ("sd", C["sd"])])
                        ld_u = C["ld"]

                        @block.vector
                        def _(v, sl=sl, spar=spar, t_d=t_d):
                            vop(v, lambda: v.tensor_scalar(
                                zp2[:, spar, :], psd_flat,
                                dtb_sb[:, sl:sl + 1], None,
                                op0=ALU.add),
                                waits=[("t", t_d)])
                        v_az_last = C["v"]

                        @block.scalar
                        def _(s, spar=spar, v_az=v_az_last):
                            sop(s, lambda: s.activation(az[:], zp2[:, spar, :],
                                                        AF.Abs),
                                waits=[("v", v_az)])
                            sop(s, lambda: s.activation(el2[:, spar, :], az[:],
                                                        AF.Exp, scale=-1.0))
                            sop(s, lambda: s.activation(el2[:, spar, :],
                                                        el2[:, spar, :],
                                                        AF.Ln, bias=1.0))
                            sop(s, lambda: s.activation(zp2[:, spar, :],
                                                        zp2[:, spar, :],
                                                        AF.Relu))
                        s_sp = C["s"]

                        @block.vector
                        def _(v, sl=sl, spar=spar, s_sp=s_sp, ld_u=ld_u):
                            vop(v, lambda: v.tensor_add(dl_all[:, sl, :],
                                                        zp2[:, spar, :],
                                                        el2[:, spar, :]),
                                waits=[("s", s_sp)])
                            vop(v, lambda: v.tensor_mul(du_all[:, sl, :],
                                                        dl_all[:, sl, :],
                                                        u_b[:]),
                                waits=[("ld", ld_u)])
                        v_dl[sl] = C["v"]

                    for ng in range(2):
                        s_bc = C["s"]
                        for j in range(8):
                            n = ng * 8 + j
                            @block.tensor
                            def _(t, n=n, qpar=qpar, ld_q=ld_q, s_bc=s_bc):
                                t.wait_ge(SEM["s"], s_bc)
                                t.wait_ge(SEM["qq"], ld_q)
                                for hk in range(2):
                                    last = t.matmul(ps_bb[:, 0, hk, :],
                                             sel_sb[:, n * 128:(n + 1) * 128],
                                             btq[0:16, qpar,
                                                 hk * 512:(hk + 1) * 512],
                                             start=True, stop=True)
                                for hk in range(2):
                                    last = t.matmul(ps_bb[:, 1, hk, :],
                                             sel_sb[:, n * 128:(n + 1) * 128],
                                             ctq[0:16, qpar,
                                                 hk * 512:(hk + 1) * 512],
                                             start=True, stop=True)
                                tick(last, "t")
                            t_bc = C["t"]

                            @block.scalar
                            def _(s, j=j, t_bc=t_bc, ng=ng):
                                sop(s, lambda: s.copy(
                                    Bbc[:, j, :],
                                    ps_bb.ap()[:, 0]
                                    .rearrange("p a b -> p (a b)")),
                                    waits=[("t", t_bc),
                                           ("v", v_bc_war[ng])])
                                sop(s, lambda: s.copy(
                                    Cbc[:, j, :],
                                    ps_bb.ap()[:, 1]
                                    .rearrange("p a b -> p (a b)")))
                            s_bc = C["s"]
                        s_bc_done = C["s"]

                        for sl in range(8):
                            s_base = C["s"]
                            @block.scalar
                            def _(s, sl=sl, ng=ng):
                                for j in range(8):
                                    n = ng * 8 + j
                                    w = [("v", max(v_dl[sl],
                                                   v_scan_hist[j]))]
                                    sop(s, lambda j=j, n=n: s.activation(
                                        a_buf[:, j, :], dl_all[:, sl, :],
                                        AF.Exp,
                                        scale=A_sb[:, sl * 16 + n:
                                                   sl * 16 + n + 1]),
                                        waits=w)

                            for j in range(8):
                                n = ng * 8 + j
                                hslot = j % 4
                                s_exp = s_base + j + 1
                                @block.vector
                                def _(v, sl=sl, j=j, n=n, hslot=hslot,
                                      s_exp=s_exp, qt=qt, ng=ng):
                                    vop(v, lambda: v.tensor_mul(
                                        bsc2[:, j % 2, :], du_all[:, sl, :],
                                        Bbc[:, j, :]),
                                        waits=([("s", s_bc_done)]
                                               if j == 0 else ()))
                                    if qt == 0:
                                        init = 0.0
                                    else:
                                        init = hst_sb[:, sl, n:n + 1]
                                    vop(v, lambda: v.tensor_tensor_scan(
                                        h4[:, hslot, :], a_buf[:, j, :],
                                        bsc2[:, j % 2, :], init,
                                        op0=ALU.mult, op1=ALU.add),
                                        waits=[("s", s_exp)])
                                    vop(v, lambda: v.tensor_copy(
                                        hst_sb[:, sl, n:n + 1],
                                        h4[:, hslot, QW - 1:QW]))
                                    acc = yac[:, sl, :] if ng == 0 \
                                        else ptmp[:, 0, :]
                                    if j == 0:
                                        vop(v, lambda: v.tensor_mul(
                                            acc, h4[:, hslot, :],
                                            Cbc[:, j, :]))
                                    else:
                                        vop(v, lambda: v.tensor_mul(
                                            ptmp[:, 1, :], h4[:, hslot, :],
                                            Cbc[:, j, :]))
                                        vop(v, lambda: v.tensor_add(
                                            acc, acc, ptmp[:, 1, :]))
                                v_scan_hist[j] = C["v"]

                            if ng == 1:
                                @block.sync
                                def _(sy, sl=sl, t0=t0):
                                    dma(sy, "ld", u_b[:],
                                        uT_sp[sl * 128:(sl + 1) * 128,
                                              t0:t0 + QW],
                                        waits=[("v", C["v"])])
                                    dma(sy, "ld", gs_b[:],
                                        gsT_sp[sl * 128:(sl + 1) * 128,
                                               t0:t0 + QW])
                                ld_ug = C["ld"]

                                @block.vector
                                def _(v, sl=sl, ld_ug=ld_ug):
                                    vop(v, lambda: v.tensor_add(
                                        yf[:], ptmp[:, 0, :], yac[:, sl, :]))
                                    vop(v, lambda: v.scalar_tensor_tensor(
                                        yf[:], u_b[:], dp_sb[:, sl:sl + 1],
                                        yf[:], op0=ALU.mult, op1=ALU.add),
                                        waits=[("ld", ld_ug)])
                                    vop(v, lambda: v.tensor_mul(
                                        yo[:], yf[:], gs_b[:]),
                                        waits=[("sd", C["sd"])])
                                v_y = C["v"]

                                @block.gpsimd
                                def _(g, sl=sl, b=b, qt=qt, v_y=v_y):
                                    dma(g, "sd",
                                        yT_mine[b][sl * 128:(sl + 1) * 128,
                                                   qt * QW:(qt + 1) * QW],
                                        yo[:], waits=[("v", v_y)])
                        v_bc_war[ng] = C["v"]
                    if qt == 3:
                        sd_yh = C["sd"]
                        @block.gpsimd
                        def _(g, b=b, sd_yh=sd_yh):
                            g.wait_ge(SEM["sd"], sd_yh)
                            g.collective_compute(
                                "AllGather", ALU.bypass,
                                replica_groups=[list(range(NCORE))],
                                ins=[yT_mine[b].opt()],
                                outs=[yT_parts[b]
                                      .rearrange("r p t -> (r p) t").opt()],
                            ).then_inc(cc_sems[2 + b])

        if upto <= 6:
            @block.gpsimd
            def _(g):
                g.wait_ge(cc_sems[2], 1)
                g.wait_ge(cc_sems[3], 1)
            fence_all()
            return nc

        with ExitStack() as p8:
            e8 = p8.enter_context
            wos = e8(nc.sbuf_tensor("p8_wo", [128, 64, COL], BF16))
            ywin = e8(nc.sbuf_tensor("p8_yw", [128, 8, 1024], BF16))
            gq = e8(nc.sbuf_tensor("p8_g", [128, 8, COL], BF16))
            xres = e8(nc.sbuf_tensor("p8_x", [128, 8, COL], F32))
            sg = e8(nc.sbuf_tensor("p8_sg", [128, 2, COL], F32))
            ps8 = e8(nc.psum_tensor("p8_ps", [128, 8, 512], F32))

            @block.gpsimd
            def _(g):
                g.wait_ge(cc_sems[2], 1)
                tick(g.engine_nop(), "p")
            p_agy = [C["p"], None]

            @block.scalar
            def _(sy):
                for g4 in range(4):
                    dma(sy, "l2",
                        wos.ap()[:, g4 * 16:(g4 + 1) * 16, :]
                        .rearrange("p a b -> p (a b)"),
                        wout_ld[:, g4 * 16:(g4 + 1) * 16, :]
                        .rearrange("p a b -> p (a b)"),
                        waits=([("t", C["t"]), ("v", C["v"])]
                               if g4 == 0 else ()))
            ld_wo = C["l2"]

            x_hist = {i: 0 for i in range(4)}
            y_hist = {i: 0 for i in range(8)}
            YSEMS = ["y0", "y1", "y2", "y3", "w0", "w1", "w2", "w3"]
            sg_hist = {0: 0, 1: 0}
            v_comb_last = 0
            first_q = {(h, q): True for h in range(2) for q in range(2)}

            for tg in range(8):
                if tg == 4:
                    @block.gpsimd
                    def _(g):
                        g.wait_ge(cc_sems[3], 1)
                        tick(g.engine_nop(), "p")
                    p_agy[1] = C["p"]
                @block.gpsimd
                def _(g, tg=tg):
                    dma(g, "sd", gq.ap(),
                        gq_sp[tg * 8:tg * 8 + 8].rearrange("a p b -> p a b"),
                        waits=[("v", C["v"])])
                    for tt in range(8):
                        dma(g, "sd", xres[:, tt, :],
                            x_cols[tg * 1024 + tt * 128:
                                   tg * 1024 + (tt + 1) * 128, :])
                ld_xr = C["sd"]

                for kb in range(64):
                    slot = kb % 8
                    qsel = kb % 2
                    eng_blk = block.sync if qsel == 0 else block.scalar
                    ysem = YSEMS[slot]
                    @eng_blk
                    def _(sy, tg=tg, kb=kb, slot=slot, qsel=qsel, ysem=ysem):
                        w = [("t", y_hist[slot])]
                        if first_q[(tg // 4, qsel)]:
                            w.append(("p", p_agy[tg // 4]))
                            first_q[(tg // 4, qsel)] = False
                        dma(sy, ysem, ywin[:, slot, :],
                            yT_parts[tg // 4][kb // 8, (kb % 8) * 128:
                                              (kb % 8) * 128 + 128,
                                              (tg % 4) * 1024:
                                              (tg % 4 + 1) * 1024],
                            waits=w)
                    ld_y = C[ysem]

                    @block.tensor
                    def _(t, kb=kb, slot=slot, ysem=ysem, ld_y=ld_y,
                          v_comb_last=v_comb_last):
                        t.wait_ge(SEM[ysem], ld_y)
                        t.wait_ge(SEM["l2"], ld_wo)
                        if kb == 0:
                            t.wait_ge(SEM["v"], v_comb_last)
                        for tt in range(8):
                            last = t.matmul(ps8[:, tt, :],
                                     ywin[:, slot, tt * 128:(tt + 1) * 128],
                                     wos[:, kb, :],
                                     start=(kb == 0), stop=(kb == 63))
                        tick(last, "t")
                    y_hist[slot] = C["t"]
                t_s = C["t"]

                for tt in range(8):
                    sp = tt % 2
                    @block.vector
                    def _(v, tg=tg, tt=tt, sp=sp, t_s=t_s, ld_xr=ld_xr):
                        vop(v, lambda: v.tensor_mul(sg[:, sp, :],
                                                    ps8[:, tt, :],
                                                    gq[:, tt, :]),
                            waits=[("t", t_s), ("sd", C["sd"])])
                        vop(v, lambda: v.tensor_add(sg[:, sp, :],
                                                    sg[:, sp, :],
                                                    xres[:, tt, :]),
                            waits=[("sd", ld_xr)])
                    v_o = C["v"]

                    @block.gpsimd
                    def _(g, tg=tg, tt=tt, sp=sp, v_o=v_o):
                        dma(g, "sd",
                            out_cols[tg * 1024 + tt * 128:
                                     tg * 1024 + (tt + 1) * 128, :],
                            sg[:, sp, :], waits=[("v", v_o)])
                    sg_hist[sp] = C["sd"]
                v_comb_last = C["v"]

            sd_total = C["sd"]

            @block.gpsimd
            def _(g):
                g.wait_ge(SEM["sd"], sd_total)

            @block.sync
            def _(sy):
                sy.wait_ge(SEM["sd"], sd_total)

    return nc


_NC_CACHE = None


def kernel(L0_attn_hidden_states, L0_attn_W_in, L0_attn_conv_w, L0_attn_conv_b,
           L0_attn_Wx, L0_attn_dt_ln_w, L0_attn_b_ln_w, L0_attn_c_ln_w,
           L0_attn_Wdt, L0_attn_dt_bias, L0_attn_A, L0_attn_D_param,
           L0_attn_W_out, L0_ffn_weight_q, L0_ffn_scales_and_zeros):
    global _NC_CACHE
    x = np.asarray(L0_attn_hidden_states, dtype=np.float32)
    w_in = np.asarray(L0_attn_W_in, dtype=np.float32)
    conv_w = np.asarray(L0_attn_conv_w, dtype=np.float32)
    conv_b = np.asarray(L0_attn_conv_b, dtype=np.float32)
    wx = np.asarray(L0_attn_Wx, dtype=np.float32)
    dt_ln = np.asarray(L0_attn_dt_ln_w, dtype=np.float32)
    b_ln = np.asarray(L0_attn_b_ln_w, dtype=np.float32)
    c_ln = np.asarray(L0_attn_c_ln_w, dtype=np.float32)
    wdt = np.asarray(L0_attn_Wdt, dtype=np.float32)
    dt_bias = np.asarray(L0_attn_dt_bias, dtype=np.float32)
    a_mat = np.asarray(L0_attn_A, dtype=np.float32)
    d_param = np.asarray(L0_attn_D_param, dtype=np.float32)
    w_out = np.asarray(L0_attn_W_out, dtype=np.float32)
    wq = np.asarray(L0_ffn_weight_q, dtype=np.int32)
    snz = np.asarray(L0_ffn_scales_and_zeros, dtype=np.float32)

    return _run(x, w_in, conv_w, conv_b, wx, dt_ln, b_ln, c_ln, wdt, dt_bias,
                a_mat, d_param, w_out, wq, snz)


def make_in_maps(x, w_in, conv_w, conv_b, wx, dt_ln, b_ln, c_ln, wdt, dt_bias,
                 a_mat, d_param, w_out, wq, snz):
    xf = x.reshape(TOK, H)
    ident_b = np.eye(128, dtype=ml_dtypes.bfloat16)
    ident_f = np.eye(128, dtype=np.float32)
    sel16 = np.zeros((16, 2048), dtype=ml_dtypes.bfloat16)
    for n in range(16):
        sel16[n, n * 128:(n + 1) * 128] = 1.0

    in_maps = []
    for c in range(NCORE):
        dlo = c * DL
        clo = c * COL
        w_slab = np.concatenate([w_in[dlo:dlo + DL, :],
                                 w_in[D + dlo:D + dlo + DL, :]], axis=0)
        wdt_local = wdt[dlo:dlo + DL, :]
        wx_local = wx[:, dlo:dlo + DL]
        wout_local = w_out[clo:clo + COL, :]
        wq_local = wq[clo:clo + COL, :]
        in_maps.append(dict(
            x_rows=np.ascontiguousarray(xf[c * 1024:(c + 1) * 1024, :]),
            x_cols=np.ascontiguousarray(xf[:, clo:clo + COL]),
            win_f=np.ascontiguousarray(
                w_slab.reshape(16, 128, 32, 128)
                .transpose(0, 3, 2, 1).reshape(16, 128, H)),
            conv_w=np.ascontiguousarray(
                conv_w[dlo:dlo + DL, :].reshape(8, 128, 4)
                .transpose(1, 0, 2).reshape(128, 32)),
            conv_b=np.ascontiguousarray(
                conv_b[dlo:dlo + DL].reshape(8, 128).T),
            wx_f=np.ascontiguousarray(
                wx_local.T.reshape(8, 128, 288)
                .transpose(1, 0, 2).reshape(128, 2304)),
            wdt_f=np.ascontiguousarray(
                wdt_local.reshape(8, 128, 2, 128)
                .transpose(3, 0, 2, 1).reshape(128, 2048)),
            dt_bias=np.ascontiguousarray(
                dt_bias[dlo:dlo + DL].reshape(8, 128).T),
            a_mat=np.ascontiguousarray(
                a_mat[dlo:dlo + DL, :].reshape(8, 128, NN)
                .transpose(1, 0, 2).reshape(128, 128)),
            d_param=np.ascontiguousarray(
                d_param[dlo:dlo + DL].reshape(8, 128).T),
            wout_f=np.ascontiguousarray(
                wout_local.T.reshape(64, 128, COL)
                .transpose(1, 0, 2).reshape(128, 64 * COL)),
            wqT=np.ascontiguousarray(
                wq_local.T.reshape(32, 128, COL)),
            sc_bc=np.ascontiguousarray(np.broadcast_to(
                snz[:, clo:clo + COL, 0][:, None, :], (32, 128, COL))),
            zr_bc=np.ascontiguousarray(np.broadcast_to(
                snz[:, clo:clo + COL, 1][:, None, :], (32, 128, COL))),
            lnw_in=np.ascontiguousarray(np.broadcast_to(
                np.concatenate([dt_ln, b_ln, c_ln])[None, :], (128, 288))),
            ident_b=ident_b, ident_f=ident_f, sel16=sel16,
        ))

    return in_maps


def _run(x, w_in, conv_w, conv_b, wx, dt_ln, b_ln, c_ln, wdt, dt_bias,
         a_mat, d_param, w_out, wq, snz):
    global _NC_CACHE
    in_maps = make_in_maps(x, w_in, conv_w, conv_b, wx, dt_ln, b_ln, c_ln,
                           wdt, dt_bias, a_mat, d_param, w_out, wq, snz)
    if _NC_CACHE is None:
        _NC_CACHE = build_program()
    res = run_bass_kernel_spmd(_NC_CACHE, in_maps, core_ids=list(range(NCORE)))
    parts = [res.results[c]["out_cols"] for c in range(NCORE)]
    out = np.concatenate(parts, axis=1)
    return out.reshape(2, LB, H)


# version 22 (final)
# speedup vs baseline: 1.0383x; 1.0025x over previous
import numpy as np
import ml_dtypes
import concourse.bass as bass
import concourse.mybir as mybir
from concourse.bass_utils import run_bass_kernel_spmd
from contextlib import ExitStack

F32 = mybir.dt.float32
BF16 = mybir.dt.bfloat16
I32 = mybir.dt.int32
AF = mybir.ActivationFunctionType
ALU = mybir.AluOpType
AX = mybir.AxisListType

H = 4096
D = 8192
DL = 1024
NN = 16
R = 256
TOK = 8192
LB = 4096
COL = 512
NCORE = 8
EPS_LN = 1e-5
EPS_RMS = 1e-6


def build_program(upto=99):
    nc = bass.Bass("TRN2", target_bir_lowering=False, debug=False,
                   detect_race_conditions=False)

    def ein(name, shape, dt=F32):
        return nc.dram_tensor(name, shape, dt, kind="ExternalInput").ap()

    x_rows = ein("x_rows", [1024, H])
    x_cols = ein("x_cols", [TOK, COL])
    win_f = ein("win_f", [16, 128, H])
    conv_w = ein("conv_w", [128, 32])
    conv_b = ein("conv_b", [128, 8])
    wx_f = ein("wx_f", [128, 2304])
    wdt_f = ein("wdt_f", [128, 2048])
    dt_bias = ein("dt_bias", [128, 8])
    a_mat = ein("a_mat", [128, 128])
    d_param = ein("d_param", [128, 8])
    wout_f = ein("wout_f", [128, 64 * COL])
    wqT = ein("wqT", [32, 128, COL], I32)
    sc_bc = ein("sc_bc", [32, 128, COL])
    zr_bc = ein("zr_bc", [32, 128, COL])
    lnw_in = ein("lnw_in", [128, 288])
    ident_b = ein("ident_b", [128, 128], BF16)
    ident_f = ein("ident_f", [128, 128], F32)
    sel16 = ein("sel16", [16, 2048], BF16)

    out_cols = nc.dram_tensor("out_cols", [TOK, COL], F32,
                              kind="ExternalOutput").ap()

    def scr(name, shape, dt=BF16):
        return nc.dram_tensor(name, shape, dt).ap()

    xnT_mine = scr("xnT_mine", [H, 1024])
    xnT_parts = scr("xnT_parts", [NCORE, H, 1024])
    win_ld = scr("win_ld", [16, 128, H])
    wdt_ld = scr("wdt_ld", [128, 2048])
    wx_ld = scr("wx_ld", [128, 2304])
    wout_ld = scr("wout_ld", [128, 64, COL])
    wdeq_ld = scr("wdeq_ld", [32, 128, COL])
    uT_sp = scr("uT_sp", [DL, TOK])
    gsT_sp = scr("gsT_sp", [DL, TOK])
    sspT_mine = scr("sspT_mine", [288, TOK], F32)
    sspT_sum = scr("sspT_sum", [288, TOK], F32)
    yT_mine = [scr(f"yT_mine{h}", [DL, LB]) for h in range(2)]
    yT_parts = [scr(f"yT_parts{h}", [NCORE, DL, LB]) for h in range(2)]
    dtnT_sp = scr("dtnT_sp", [128, 2, TOK])
    gq_sp = scr("gq_sp", [64, 128, COL])
    BT_sp = scr("BT_sp", [16, TOK])
    CT_sp = scr("CT_sp", [16, TOK])

    st = ExitStack()
    with st:
        e = st.enter_context

        A_sb = e(nc.sbuf_tensor("A_sb", [128, 128], F32))
        cw_sb = e(nc.sbuf_tensor("cw_sb", [128, 32], F32))
        cb_sb = e(nc.sbuf_tensor("cb_sb", [128, 8], F32))
        dp_sb = e(nc.sbuf_tensor("dp_sb", [128, 8], F32))
        dtb_sb = e(nc.sbuf_tensor("dtb_sb", [128, 8], F32))
        identb = e(nc.sbuf_tensor("identb", [128, 128], BF16))
        identf = e(nc.sbuf_tensor("identf", [128, 128], F32))
        sel_sb = e(nc.sbuf_tensor("sel_sb", [16, 2048], BF16))
        lnw_bc = e(nc.sbuf_tensor("lnw_bc", [128, 288], F32))
        hst_sb = e(nc.sbuf_tensor("hst_sb", [128, 8, 16], BF16))
        halo_sb = e(nc.sbuf_tensor("halo_sb", [128, 8, 4], F32))
        kconst = e(nc.sbuf_tensor("kconst", [128, 8], F32))

        SEM = {}
        for k in ["t", "v", "s", "p", "ld", "l2", "sd",
                  "w0", "w1", "w2", "w3", "s0", "s1", "u0", "u1", "qq",
                  "y0", "y1", "y2", "y3"]:
            SEM[k] = e(nc.semaphore("sem_" + k))
        cc_sems = [e(nc.semaphore(f"sem_cc{i}")) for i in range(4)]
        C = {k: 0 for k in SEM}
        block = e(nc.Block())

        def tick(instr, k, by=1):
            C[k] += by
            instr.then_inc(SEM[k], by)
            return C[k]

        def mkop(key):
            def op(eng, emit, waits=()):
                for wk, wv in waits:
                    eng.wait_ge(SEM[wk], wv)
                eng.wait_ge(SEM[key], C[key])
                return tick(emit(), key)
            return op
        vop = mkop("v")
        sop = mkop("s")
        pop = mkop("p")

        def dma(eng, key, out, in_, waits=()):
            for wk, wv in waits:
                eng.wait_ge(SEM[wk], wv)
            return tick(eng.dma_start(out=out, in_=in_), key, 16)

        def fence_all():
            @block.sync
            def _(sy):
                for k in C:
                    sy.wait_ge(SEM[k], C[k])

        _CONSTS = [EPS_LN, EPS_RMS, 1.0 / H, 1.0 / R, 1.0 / 16.0]

        @block.vector
        def _(v):
            for i, val in enumerate(_CONSTS):
                vop(v, lambda i=i, val=val: v.memset(kconst[:, i:i + 1], val))
        for i, val in enumerate(_CONSTS):
            nc.const_aps.register(F32, val, kconst[:, i:i + 1])

        @block.sync
        def _(sy):
            dma(sy, "ld", A_sb[:], a_mat[:])
            dma(sy, "ld", cw_sb[:], conv_w[:])
            dma(sy, "ld", cb_sb[:], conv_b[:])
            dma(sy, "ld", dp_sb[:], d_param[:])
            dma(sy, "ld", dtb_sb[:], dt_bias[:])
            dma(sy, "ld", identb[:], ident_b[:])
            dma(sy, "ld", identf[:], ident_f[:])
            dma(sy, "ld", sel_sb[:], sel16[:])
            dma(sy, "ld", lnw_bc[:], lnw_in[:])
        ld_consts = C["ld"]
        v_lnbc = C["v"]

        if upto <= 0:
            fence_all()
            return nc


        if upto <= 1:
            fence_all()
            return nc

        with ExitStack() as p2:
            e2 = p2.enter_context
            xtile = [e2(nc.sbuf_tensor(f"p2_xt{i}", [128, H], F32)) for i in range(2)]
            xn_bf = [e2(nc.sbuf_tensor(f"p2_xn{i}", [128, H], BF16)) for i in range(8)]
            stat = e2(nc.sbuf_tensor("p2_stat", [128, 8], F32))
            sq = e2(nc.sbuf_tensor("p2_sq", [128, H], F32))
            band2 = e2(nc.sbuf_tensor("p2_band", [128, 1024], BF16))
            ps_tr = e2(nc.psum_tensor("p2_pstr", [128, 1024], BF16))
            for tt in range(8):
                @block.sync
                def _(sy, tt=tt):
                    dma(sy, "ld", xtile[tt % 2][:],
                        x_rows[tt * 128:(tt + 1) * 128, :],
                        waits=[("v", C["v"]), ("s", C["s"])])
                ld_here = C["ld"]
                xt = xtile[tt % 2]

                @block.vector
                def _(v, xt=xt, ld_here=ld_here):
                    vop(v, lambda: v.reduce_sum(stat[:, 0:1], xt[:], axis=AX.X),
                        waits=[("ld", ld_here)])
                v_m1 = C["v"]

                @block.scalar
                def _(s, xt=xt, v_m1=v_m1, ld_here=ld_here):
                    sop(s, lambda: s.mul(stat[:, 1:2], stat[:, 0:1], 1.0 / H),
                        waits=[("v", v_m1)])
                    sop(s, lambda: s.activation(sq[:], xt[:], AF.Square),
                        waits=[("ld", ld_here)])
                s_sq = C["s"]

                @block.vector
                def _(v, s_sq=s_sq):
                    vop(v, lambda: v.reduce_sum(stat[:, 2:3], sq[:], axis=AX.X),
                        waits=[("s", s_sq)])
                    vop(v, lambda: v.tensor_mul(stat[:, 3:4], stat[:, 1:2],
                                                stat[:, 1:2]))
                v_m2 = C["v"]

                @block.scalar
                def _(s, v_m2=v_m2):
                    sop(s, lambda: s.mul(stat[:, 4:5], stat[:, 2:3], 1.0 / H),
                        waits=[("v", v_m2)])
                s_ms = C["s"]

                @block.vector
                def _(v, s_ms=s_ms):
                    vop(v, lambda: v.tensor_sub(stat[:, 5:6], stat[:, 4:5],
                                                stat[:, 3:4]),
                        waits=[("s", s_ms)])
                v_var = C["v"]

                @block.scalar
                def _(s, v_var=v_var):
                    sop(s, lambda: s.activation(stat[:, 6:7], stat[:, 5:6],
                                                AF.Ln, bias=EPS_LN),
                        waits=[("v", v_var)])
                    sop(s, lambda: s.activation(stat[:, 7:8], stat[:, 6:7],
                                                AF.Exp, scale=-0.5))
                s_rstd = C["s"]

                @block.vector
                def _(v, tt=tt, xt=xt, s_rstd=s_rstd):
                    vop(v, lambda: v.tensor_scalar(
                        xn_bf[tt][:], xt[:], stat[:, 1:2], stat[:, 7:8],
                        op0=ALU.subtract, op1=ALU.mult),
                        waits=[("s", s_rstd)])
            v_xn = C["v"]

            for hb in range(32):
                @block.tensor
                def _(t, hb=hb, v_xn=v_xn):
                    t.wait_ge(SEM["v"], v_xn if hb == 0 else C["v"])
                    for tt in range(8):
                        last = t.transpose(ps_tr[:, tt * 128:(tt + 1) * 128],
                                    xn_bf[tt][:, hb * 128:(hb + 1) * 128],
                                    identb[:])
                    tick(last, "t")
                t_tr = C["t"]

                @block.vector
                def _(v, t_tr=t_tr):
                    vop(v, lambda: v.tensor_copy(band2[:], ps_tr[:]),
                        waits=[("t", t_tr), ("sd", C["sd"])])
                v_b = C["v"]

                @block.gpsimd
                def _(g, hb=hb, v_b=v_b):
                    dma(g, "sd", xnT_mine[hb * 128:(hb + 1) * 128, :],
                        band2[:], waits=[("v", v_b)])
            sd_xnT = C["sd"]

            @block.gpsimd
            def _(g):
                g.wait_ge(SEM["sd"], sd_xnT)
                g.collective_compute(
                    "AllGather", ALU.bypass,
                    replica_groups=[list(range(NCORE))],
                    ins=[xnT_mine.opt()],
                    outs=[xnT_parts.rearrange("r p t -> (r p) t").opt()],
                ).then_inc(cc_sems[0])

        with ExitStack() as p1:
            e1 = p1.enter_context
            wtile = [e1(nc.sbuf_tensor(f"p1_wt{i}", [128, 4096], F32))
                     for i in range(2)]
            wcast = [e1(nc.sbuf_tensor(f"p1_wc{i}", [128, 4096], BF16))
                     for i in range(2)]
            qtl = e1(nc.sbuf_tensor("p1_qt", [128, 2, COL], I32))
            sctl = e1(nc.sbuf_tensor("p1_sc", [128, 2, COL], F32))
            zrtl = e1(nc.sbuf_tensor("p1_zr", [128, 2, COL], F32))
            qf = e1(nc.sbuf_tensor("p1_qf", [128, 2, COL], F32))
            wqc = e1(nc.sbuf_tensor("p1_wqc", [128, 2, COL], BF16))
            h_cast = {0: 0, 1: 0}
            h_cst = {0: 0, 1: 0}

            cast_jobs = []
            for rb in range(16):
                cast_jobs.append((4096, win_f[rb],
                                  win_ld[rb]))
            cast_jobs.append((2048, wdt_f[:], wdt_ld[:]))
            cast_jobs.append((2304, wx_f[:], wx_ld[:]))
            for g8 in range(8):
                cast_jobs.append((4096, wout_f[:, g8 * 4096:(g8 + 1) * 4096],
                                  wout_ld[:, g8 * 8:(g8 + 1) * 8, :]
                                  .rearrange("p a b -> p (a b)")))

            for i, (wdth, srcap, dstap) in enumerate(cast_jobs):
                par = i % 2
                psem = "y%d" % par
                @block.sync
                def _(sy, i=i, par=par, psem=psem, srcap=srcap, wdth=wdth):
                    w = [("v", h_cast[par])]
                    if i < 2:
                        w += [("t", C["t"]), ("v", C["v"]), ("s", C["s"]),
                              ("sd", C["sd"])]
                    dma(sy, psem, wtile[par][:, 0:wdth], srcap, waits=w)
                ld_h = C[psem]

                @block.vector
                def _(v, par=par, psem=psem, wdth=wdth, ld_h=ld_h):
                    vop(v, lambda: v.tensor_copy(wcast[par][:, 0:wdth],
                                                 wtile[par][:, 0:wdth]),
                        waits=[(psem, ld_h), ("sd", C["sd"])])
                h_cast[par] = C["v"]
                v_c = C["v"]

                @block.gpsimd
                def _(g, par=par, wdth=wdth, dstap=dstap, v_c=v_c):
                    dma(g, "sd", dstap, wcast[par][:, 0:wdth],
                        waits=[("v", v_c)])
                h_cst[par] = C["sd"]
                if i == 15:
                    sd_win = C["sd"]

            for i in range(16):
                @block.sync
                def _(sy, i=i):
                    w = ([("v", C["v"])] if i >= 1 else
                         [("t", C["t"]), ("v", C["v"]), ("s", C["s"]),
                          ("sd", C["sd"])])
                    dma(sy, "ld", qtl.ap(),
                        wqT[2 * i:2 * i + 2].rearrange("k p c -> p k c"),
                        waits=w)
                    dma(sy, "ld", sctl.ap(),
                        sc_bc[2 * i:2 * i + 2].rearrange("k p c -> p k c"))
                    dma(sy, "ld", zrtl.ap(),
                        zr_bc[2 * i:2 * i + 2].rearrange("k p c -> p k c"))
                ld_h = C["ld"]

                @block.vector
                def _(v, i=i, ld_h=ld_h):
                    vop(v, lambda: v.tensor_copy(
                        qf.ap().rearrange("p a b -> p (a b)"),
                        qtl.ap().rearrange("p a b -> p (a b)")),
                        waits=[("ld", ld_h), ("sd", C["sd"])])
                    vop(v, lambda: v.scalar_tensor_tensor(
                        qf.ap().rearrange("p a b -> p (a b)"),
                        qf.ap().rearrange("p a b -> p (a b)"),
                        8.0,
                        sctl.ap().rearrange("p a b -> p (a b)"),
                        op0=ALU.subtract, op1=ALU.mult))
                    vop(v, lambda: v.tensor_add(
                        wqc.ap().rearrange("p a b -> p (a b)"),
                        qf.ap().rearrange("p a b -> p (a b)"),
                        zrtl.ap().rearrange("p a b -> p (a b)")))
                v_c = C["v"]

                @block.gpsimd
                def _(g, i=i, v_c=v_c):
                    dma(g, "sd",
                        wdeq_ld[2 * i:2 * i + 2].rearrange("k p c -> p k c"),
                        wqc.ap(), waits=[("v", v_c)])
        sd_prep = C["sd"]

        @block.gpsimd
        def _(g):
            g.wait_ge(cc_sems[0], 1)
            tick(g.engine_nop(), "p")
        p_ag0 = C["p"]

        with ExitStack() as pf:
            ef = pf.enter_context
            wqs = ef(nc.sbuf_tensor("pf_wq", [128, 32, COL], BF16))
            xwf = ef(nc.sbuf_tensor("pf_xw", [128, 4, 1024], BF16))
            gq_st = ef(nc.sbuf_tensor("pf_gst", [128, 8, COL], BF16))
            ps8f = ef(nc.psum_tensor("pf_ps", [128, 8, 512], F32))

            @block.sync
            def _(sy):
                for kb in range(32):
                    dma(sy, "ld", wqs[:, kb, :], wdeq_ld[kb],
                        waits=([("sd", sd_prep)] if kb == 0 else ()))
            ld_wq = C["ld"]

            x_hist = {i: 0 for i in range(4)}
            s_gq_last = 0
            for tg in range(8):
                for kb in range(32):
                    slot = kb % 4
                    qsel = kb % 2
                    eng_blk = block.sync if qsel == 0 else block.scalar
                    xsem = "w%d" % slot
                    @eng_blk
                    def _(sy, tg=tg, kb=kb, slot=slot, xsem=xsem):
                        w = [("t", x_hist[slot])]
                        if tg == 0 and kb < 2:
                            w.append(("p", p_ag0))
                            w.append(("sd", sd_prep))
                        dma(sy, xsem, xwf[:, slot, :],
                            xnT_parts[tg, kb * 128:(kb + 1) * 128, :],
                            waits=w)
                    ld_x = C[xsem]

                    @block.tensor
                    def _(t, kb=kb, slot=slot, xsem=xsem, ld_x=ld_x,
                          s_gq_last=s_gq_last):
                        t.wait_ge(SEM[xsem], ld_x)
                        if kb == 0:
                            t.wait_ge(SEM["ld"], ld_wq)
                            t.wait_ge(SEM["s"], s_gq_last)
                        for tt in range(8):
                            last = t.matmul(ps8f[:, tt, :],
                                     xwf[:, slot, tt * 128:(tt + 1) * 128],
                                     wqs[:, kb, :],
                                     start=(kb == 0), stop=(kb == 31))
                        tick(last, "t")
                    x_hist[slot] = C["t"]
                t_ffn = C["t"]

                @block.scalar
                def _(sc, tg=tg, t_ffn=t_ffn):
                    for tt in range(8):
                        sop(sc, lambda tt=tt: sc.activation(
                            gq_st[:, tt, :], ps8f[:, tt, :], AF.Silu),
                            waits=([("t", t_ffn), ("sd", C["sd"])]
                                   if tt == 0 else ()))
                s_gq_last = C["s"]

                @block.gpsimd
                def _(g, tg=tg, s_gq_last=s_gq_last):
                    dma(g, "sd", gq_sp[tg * 8:tg * 8 + 8]
                        .rearrange("a p b -> p a b"),
                        gq_st.ap(), waits=[("s", s_gq_last)])
        sd_ffn = C["sd"]
        s_ffn = C["s"]

        if upto <= 2:
            @block.gpsimd
            def _(g):
                g.wait_ge(cc_sems[0], 1)
            fence_all()
            return nc

        with ExitStack() as p3:
            e3 = p3.enter_context
            xng = e3(nc.sbuf_tensor("p3_xng", [128, 32, 1024], BF16))
            wslab = [e3(nc.sbuf_tensor(f"p3_ws{i}", [128, 32, 128], BF16))
                     for i in range(2)]
            upre = e3(nc.sbuf_tensor("p3_upre", [128, 2, 1028], F32))
            uacc = e3(nc.sbuf_tensor("p3_uacc", [128, 2, 1024], F32))
            ub = e3(nc.sbuf_tensor("p3_ub", [128, 2, 1024], BF16))
            ps_a = e3(nc.psum_tensor("p3_psa", [128, 2, 2, 512], F32))

            @block.vector
            def _(v):
                vop(v, lambda: v.memset(halo_sb.ap().rearrange("p a b -> p (a b)"),
                                        0.0))

            h_evac = {0: 0, 1: 0}
            h_vsave = {0: 0, 1: 0}
            h_uread = {0: 0, 1: 0}
            h_store = {0: 0, 1: 0}
            h_wld = {0: 0, 1: 0}

            for grp in range(8):
                xbuf = xng
                gate = [("t", C["t"]), ("p", p_ag0)]
                if grp == 0:
                    gate += [("sd", C["sd"]), ("s", C["s"]), ("v", C["v"])]
                for qsel, blk in ((0, block.sync), (1, block.scalar)):
                    @blk
                    def _(sy, grp=grp, xbuf=xbuf, qsel=qsel, gate=gate):
                        first = True
                        for hb in range(32):
                            if hb % 2 != qsel:
                                continue
                            dma(sy, "w%d" % (hb // 8), xbuf[:, hb, :],
                                xnT_parts[grp, hb * 128:(hb + 1) * 128, :],
                                waits=(gate if first else ()))
                            first = False
                wv_cnt = [C["w%d" % i] for i in range(4)]

                for dt in range(16):
                    par = dt % 2
                    wbuf = wslab[par]
                    @block.sync
                    def _(sy, dt=dt, par=par, wbuf=wbuf, grp=grp):
                        w = [("t", h_wld[par])]
                        if grp == 0 and dt < 2:
                            w.append(("sd", sd_ffn))
                            w.append(("s", s_ffn))
                        dma(sy, "s%d" % par,
                            wbuf.ap().rearrange("p a b -> p (a b)"),
                            win_ld[dt], waits=w)
                    ld_w = C["s%d" % par]

                    @block.tensor
                    def _(t, dt=dt, par=par, wbuf=wbuf, xbuf=xbuf, grp=grp,
                          wv_cnt=wv_cnt, ld_w=ld_w):
                        t.wait_ge(SEM["s%d" % par], ld_w)
                        t.wait_ge(SEM["s"], h_evac[par])
                        for k in range(32):
                            if dt == 0 and k % 8 == 0:
                                t.wait_ge(SEM["w%d" % (k // 8)],
                                          wv_cnt[k // 8])
                            for bk in range(2):
                                last = t.matmul(ps_a[:, par, bk, :],
                                         wbuf[:, k, :],
                                         xbuf[:, k, bk * 512:(bk + 1) * 512],
                                         start=(k == 0), stop=(k == 31))
                        tick(last, "t")
                    t_mm = C["t"]
                    h_wld[par] = t_mm

                    ps_flat = ps_a.ap()[:, par].rearrange("p a b -> p (a b)")
                    if dt < 8:
                        @block.scalar
                        def _(s, par=par, t_mm=t_mm, ps_flat=ps_flat):
                            sop(s, lambda: s.copy(upre[:, par, 4:1028],
                                                  ps_flat),
                                waits=[("t", t_mm), ("v", h_vsave[par])])
                        s_evac = C["s"]
                        h_evac[par] = s_evac

                        @block.vector
                        def _(v, dt=dt, par=par, s_evac=s_evac):
                            vop(v, lambda: v.tensor_copy(upre[:, par, 1:4],
                                                         halo_sb[:, dt, 1:4]),
                                waits=[("s", s_evac)])
                            vop(v, lambda: v.tensor_scalar(
                                uacc[:, par, :], upre[:, par, 4:1028],
                                cw_sb[:, dt * 4 + 3:dt * 4 + 4], None,
                                op0=ALU.mult),
                                waits=[("s", h_uread[par])])
                            for kk in range(3):
                                vop(v, lambda kk=kk: v.scalar_tensor_tensor(
                                    uacc[:, par, :],
                                    upre[:, par, 3 - kk:1027 - kk],
                                    cw_sb[:, dt * 4 + 2 - kk:dt * 4 + 3 - kk],
                                    uacc[:, par, :],
                                    op0=ALU.mult, op1=ALU.add))
                            vop(v, lambda: v.tensor_copy(
                                halo_sb[:, dt, 1:4],
                                upre[:, par, 1025:1028]))
                        v_savehalo = C["v"]
                        h_vsave[par] = v_savehalo

                        @block.scalar
                        def _(s, dt=dt, par=par, v_savehalo=v_savehalo):
                            sop(s, lambda: s.activation(
                                ub[:, par, :], uacc[:, par, :], AF.Silu,
                                bias=cb_sb[:, dt:dt + 1]),
                                waits=[("v", v_savehalo),
                                       ("sd", C["sd"])])
                        s_u = C["s"]
                        h_uread[par] = s_u

                        @block.gpsimd
                        def _(g, grp=grp, dt=dt, par=par, s_u=s_u):
                            dma(g, "sd",
                                uT_sp[dt * 128:(dt + 1) * 128,
                                      grp * 1024:(grp + 1) * 1024],
                                ub[:, par, :], waits=[("s", s_u)])
                        h_store[par] = C["sd"]
                    else:
                        @block.scalar
                        def _(s, par=par, t_mm=t_mm, ps_flat=ps_flat):
                            sop(s, lambda: s.activation(
                                ub[:, par, :], ps_flat, AF.Silu),
                                waits=[("t", t_mm), ("sd", C["sd"])])
                        s_g = C["s"]
                        h_evac[par] = s_g

                        @block.gpsimd
                        def _(g, grp=grp, dt=dt, par=par, s_g=s_g):
                            dma(g, "sd",
                                gsT_sp[(dt - 8) * 128:(dt - 7) * 128,
                                       grp * 1024:(grp + 1) * 1024],
                                ub[:, par, :], waits=[("s", s_g)])
                        h_store[par] = C["sd"]

                if grp == 3:
                    @block.vector
                    def _(v):
                        vop(v, lambda: v.memset(
                            halo_sb.ap().rearrange("p a b -> p (a b)"), 0.0))

        if upto <= 3:
            fence_all()
            return nc

        with ExitStack() as p4:
            e4 = p4.enter_context
            ubig = [e4(nc.sbuf_tensor(f"p4_u{i}", [128, 8, 2048], BF16))
                    for i in range(2)]
            wxs = e4(nc.sbuf_tensor("p4_wx", [128, 8, 288], BF16))
            ev = e4(nc.sbuf_tensor("p4_ev", [128, 2, 2048], F32))
            ps_m = e4(nc.psum_tensor("p4_psm", [128, 2, 4, 512], F32))

            @block.sync
            def _(sy):
                dma(sy, "ld", wxs.ap().rearrange("p a b -> p (a b)"),
                    wx_ld[:],
                    waits=[("t", C["t"]), ("s", C["s"]), ("v", C["v"]),
                           ("sd", C["sd"])])
            ld_wx = C["ld"]

            h_u4 = {0: 0, 1: 0}
            h_pm = {0: 0, 1: 0}
            h_ev = {0: 0, 1: 0}
            mi = 0
            for qt in range(4):
                ubuf = ubig[qt % 2]
                usem = "u%d" % (qt % 2)
                @block.sync
                def _(sy, qt=qt, ubuf=ubuf, usem=usem):
                    w = [("t", h_u4[qt % 2]), ("sd", C["sd"])]
                    for kb in range(8):
                        dma(sy, usem, ubuf[:, kb, :],
                            uT_sp[kb * 128:(kb + 1) * 128,
                                  qt * 2048:(qt + 1) * 2048],
                            waits=(w if kb == 0 else ()))
                ld_u = C[usem]

                for mt in range(3):
                    mrows = 128 if mt < 2 else 32
                    mp = mi % 2
                    @block.tensor
                    def _(t, qt=qt, mt=mt, mp=mp, mrows=mrows, ubuf=ubuf,
                          usem=usem, ld_u=ld_u):
                        t.wait_ge(SEM[usem], ld_u)
                        t.wait_ge(SEM["ld"], ld_wx)
                        t.wait_ge(SEM["v"], h_pm[mp])
                        for k in range(8):
                            for bk in range(4):
                                last = t.matmul(ps_m[0:mrows, mp, bk, :],
                                         wxs[:, k, mt * 128:mt * 128 + mrows],
                                         ubuf[:, k, bk * 512:(bk + 1) * 512],
                                         start=(k == 0), stop=(k == 7))
                        tick(last, "t")
                    t_mm = C["t"]
                    h_u4[qt % 2] = t_mm

                    @block.vector
                    def _(v, mp=mp, mrows=mrows, t_mm=t_mm):
                        vop(v, lambda: v.tensor_copy(
                            ev[0:mrows, mp, :],
                            ps_m.ap()[0:mrows, mp]
                            .rearrange("p a b -> p (a b)")),
                            waits=[("t", t_mm), ("sd", C["sd"])])
                    v_ev = C["v"]
                    h_pm[mp] = v_ev

                    @block.gpsimd
                    def _(g, qt=qt, mt=mt, mp=mp, mrows=mrows, v_ev=v_ev):
                        dma(g, "sd",
                            sspT_mine[mt * 128:mt * 128 + mrows,
                                      qt * 2048:(qt + 1) * 2048],
                            ev[0:mrows, mp, :], waits=[("v", v_ev)])
                    h_ev[mp] = C["sd"]
                    mi += 1
            sd_ssp = C["sd"]

            @block.gpsimd
            def _(g):
                g.wait_ge(SEM["sd"], sd_ssp)
                g.collective_compute(
                    "AllReduce", ALU.add,
                    replica_groups=[list(range(NCORE))],
                    ins=[sspT_mine.opt()], outs=[sspT_sum.opt()],
                ).then_inc(cc_sems[1])

        if upto <= 4:
            @block.gpsimd
            def _(g):
                g.wait_ge(cc_sems[1], 1)
            fence_all()
            return nc

        with ExitStack() as p5:
            e5 = p5.enter_context
            ssp = [e5(nc.sbuf_tensor(f"p5_ssp{i}", [128, TOK // 2], F32))
                   for i in range(4)]
            ssp3 = e5(nc.sbuf_tensor("p5_sspC", [32, TOK], F32))
            tok = e5(nc.sbuf_tensor("p5_tok", [128, 288], F32))
            dtn = e5(nc.sbuf_tensor("p5_dtn", [128, 288], BF16))
            nrm = e5(nc.sbuf_tensor("p5_nrm", [128, 8], F32))
            sqv = e5(nc.sbuf_tensor("p5_sq", [128, 288], F32))
            ps_f = e5(nc.psum_tensor("p5_psf", [128, 512], F32))
            ps_b = e5(nc.psum_tensor("p5_psb", [128, 2, 512], BF16))
            stage5 = e5(nc.sbuf_tensor("p5_stage", [128, 2, 512], BF16))
            sd_hist5 = {}

            @block.gpsimd
            def _(g):
                g.wait_ge(cc_sems[1], 1)
                tick(g.engine_nop(), "p")
            p_ar = C["p"]

            @block.sync
            def _(sy):
                w = [("p", p_ar), ("v", C["v"]), ("t", C["t"])]
                for i in range(4):
                    dma(sy, "ld", ssp[i][:],
                        sspT_sum[(i % 2) * 128:(i % 2) * 128 + 128,
                                 (i // 2) * 4096:(i // 2) * 4096 + 4096],
                        waits=(w if i == 0 else ()))
                dma(sy, "ld", ssp3[:], sspT_sum[256:288, :])
            ld_ssp = C["ld"]

            for tt in range(64):
                half = (tt * 128) // 4096
                off = (tt * 128) % 4096
                @block.tensor
                def _(t, tt=tt, half=half, off=off, ld_ssp=ld_ssp):
                    t.wait_ge(SEM["ld"], ld_ssp)
                    t.wait_ge(SEM["v"], C["v"])
                    last = t.transpose(ps_f[:, 0:128],
                                ssp[2 * half][:, off:off + 128], identf[:])
                    last = t.transpose(ps_f[:, 128:256],
                                ssp[2 * half + 1][:, off:off + 128], identf[:])
                    last = t.transpose(ps_f[:, 256:288],
                                ssp3[:, tt * 128:tt * 128 + 128],
                                identf[0:32, 0:32])
                    tick(last, "t")
                t_tr = C["t"]

                @block.vector
                def _(v, t_tr=t_tr):
                    vop(v, lambda: v.tensor_copy(tok[:], ps_f[:, 0:288]),
                        waits=[("t", t_tr)])
                v_tok = C["v"]

                @block.scalar
                def _(s, v_tok=v_tok):
                    sop(s, lambda: s.activation(sqv[:], tok[:], AF.Square),
                        waits=[("v", v_tok)])
                s_sq = C["s"]

                @block.vector
                def _(v, s_sq=s_sq):
                    vop(v, lambda: v.reduce_sum(nrm[:, 0:1], sqv[:, 0:R],
                                                axis=AX.X), waits=[("s", s_sq)])
                    vop(v, lambda: v.reduce_sum(nrm[:, 1:2], sqv[:, R:R + 16],
                                                axis=AX.X))
                    vop(v, lambda: v.reduce_sum(nrm[:, 2:3], sqv[:, R + 16:288],
                                                axis=AX.X))
                v_rs = C["v"]

                @block.scalar
                def _(s, v_rs=v_rs):
                    for j, w_ in [(0, R), (1, 16), (2, 16)]:
                        sop(s, lambda j=j, w_=w_: s.activation(
                            nrm[:, 3 + j:4 + j], nrm[:, j:j + 1], AF.Ln,
                            scale=1.0 / w_, bias=EPS_RMS),
                            waits=([("v", v_rs)] if j == 0 else ()))
                        sop(s, lambda j=j: s.activation(
                            nrm[:, 3 + j:4 + j], nrm[:, 3 + j:4 + j], AF.Exp,
                            scale=-0.5))
                s_r = C["s"]

                @block.vector
                def _(v, s_r=s_r):
                    vop(v, lambda: v.tensor_scalar(
                        tok[:, 0:R], tok[:, 0:R], nrm[:, 3:4], None,
                        op0=ALU.mult), waits=[("s", s_r)])
                    vop(v, lambda: v.tensor_scalar(
                        tok[:, R:R + 16], tok[:, R:R + 16], nrm[:, 4:5], None,
                        op0=ALU.mult))
                    vop(v, lambda: v.tensor_scalar(
                        tok[:, R + 16:288], tok[:, R + 16:288], nrm[:, 5:6],
                        None, op0=ALU.mult))
                    vop(v, lambda: v.tensor_mul(dtn[:], tok[:], lnw_bc[:]),
                        waits=[("ld", ld_consts)])
                v_dtn = C["v"]

                par = tt % 2
                @block.tensor
                def _(t, tt=tt, par=par, v_dtn=v_dtn):
                    t.wait_ge(SEM["v"], v_dtn)
                    last = t.transpose(ps_b[:, par, 0:128], dtn[:, 0:128],
                                       identb[:])
                    last = t.transpose(ps_b[:, par, 128:256], dtn[:, 128:256],
                                       identb[:])
                    last = t.transpose(ps_b[0:16, par, 256:384],
                                       dtn[:, 256:272], identb[:])
                    last = t.transpose(ps_b[0:16, par, 384:512],
                                       dtn[:, 272:288], identb[:])
                    tick(last, "t")
                t_tr2 = C["t"]

                @block.vector
                def _(v, tt=tt, par=par, t_tr2=t_tr2):
                    w = [("t", t_tr2), ("sd", C["sd"])]
                    vop(v, lambda: v.tensor_copy(stage5[:, par, :],
                                                 ps_b[:, par, :]), waits=w)
                v_stage = C["v"]

                @block.gpsimd
                def _(g, tt=tt, par=par, v_stage=v_stage):
                    dma(g, "sd", dtnT_sp[:, :, tt * 128:(tt + 1) * 128],
                        stage5.ap()[:, par, 0:256]
                        .rearrange("p (a b) -> p a b", a=2),
                        waits=[("v", v_stage)])
                    dma(g, "sd", BT_sp[:, tt * 128:(tt + 1) * 128],
                        stage5[0:16, par, 256:384])
                    dma(g, "sd", CT_sp[:, tt * 128:(tt + 1) * 128],
                        stage5[0:16, par, 384:512])
                sd_hist5[tt] = C["sd"]
        v_p5 = C["v"]
        sd_p5 = C["sd"]

        if upto <= 5:
            fence_all()
            return nc

        with ExitStack() as p6:
            e6 = p6.enter_context
            QW = 1024
            wdts = e6(nc.sbuf_tensor("p6_wdt", [128, 8, 2, 128], BF16))
            dtn_q = e6(nc.sbuf_tensor("p6_dtnq", [128, 2, 2, QW], BF16))
            btq = e6(nc.sbuf_tensor("p6_btq", [16, 2, QW], BF16))
            ctq = e6(nc.sbuf_tensor("p6_ctq", [16, 2, QW], BF16))
            Bbc = e6(nc.sbuf_tensor("p6_B", [128, 8, QW], BF16))
            Cbc = e6(nc.sbuf_tensor("p6_C", [128, 8, QW], BF16))
            dl_all = e6(nc.sbuf_tensor("p6_dla", [128, 8, QW], BF16))
            du_all = e6(nc.sbuf_tensor("p6_dua", [128, 8, QW], BF16))
            a_buf = e6(nc.sbuf_tensor("p6_ab", [128, 8, QW], BF16))
            h4 = e6(nc.sbuf_tensor("p6_h", [128, 4, QW], BF16))
            bsc2 = e6(nc.sbuf_tensor("p6_bs", [128, 2, QW], BF16))
            ptmp = e6(nc.sbuf_tensor("p6_pt", [128, 2, QW], BF16))
            yac = e6(nc.sbuf_tensor("p6_yac", [128, 8, QW], BF16))
            u_b = e6(nc.sbuf_tensor("p6_u", [128, QW], BF16))
            gs_b = e6(nc.sbuf_tensor("p6_gs", [128, QW], BF16))
            yf = e6(nc.sbuf_tensor("p6_yf", [128, QW], F32))
            yo = e6(nc.sbuf_tensor("p6_yo", [128, QW], BF16))
            az = e6(nc.sbuf_tensor("p6_az", [128, QW], F32))
            zp2 = e6(nc.sbuf_tensor("p6_zp", [128, 2, QW], F32))
            el2 = e6(nc.sbuf_tensor("p6_el", [128, 2, QW], F32))
            ps_d = e6(nc.psum_tensor("p6_psd", [128, 2, 512], F32))
            ps_bb = e6(nc.psum_tensor("p6_psbb", [128, 2, 2, 512], F32))

            @block.sync
            def _(sy):
                dma(sy, "qq",
                    wdts.ap().rearrange("p a b c -> p (a b c)"), wdt_ld[:],
                    waits=[("t", C["t"]), ("v", C["v"]), ("sd", C["sd"])])
            ld_wdt = C["qq"]

            psd_flat = ps_d.ap().rearrange("p a b -> p (a b)")
            v_scan_hist = {j: 0 for j in range(8)}
            v_bc_war = {0: 0, 1: 0}
            v_az_last = 0

            for b in range(2):
                for qt in range(4):
                    qi = b * 4 + qt
                    t0 = b * LB + qt * QW
                    qpar = qi % 2
                    @block.sync
                    def _(sy, t0=t0, qpar=qpar):
                        w = [("sd", sd_p5), ("t", C["t"])]
                        dma(sy, "qq", dtn_q.ap()[:, qpar, :, :],
                            dtnT_sp[:, :, t0:t0 + QW], waits=w)
                        dma(sy, "qq", btq[:, qpar, :], BT_sp[:, t0:t0 + QW])
                        dma(sy, "qq", ctq[:, qpar, :], CT_sp[:, t0:t0 + QW])
                    ld_q = C["qq"]

                    v_dl = {}
                    for sl in range(8):
                        spar = sl % 2
                        @block.tensor
                        def _(t, sl=sl, qpar=qpar, ld_q=ld_q):
                            t.wait_ge(SEM["qq"], ld_q)
                            t.wait_ge(SEM["v"], v_az_last)
                            for k in range(2):
                                for bk in range(2):
                                    last = t.matmul(ps_d[:, bk, :],
                                             wdts[:, sl, k, :],
                                             dtn_q[:, qpar, k,
                                                   bk * 512:(bk + 1) * 512],
                                             start=(k == 0), stop=(k == 1))
                            tick(last, "t")
                        t_d = C["t"]

                        @block.sync
                        def _(sy, sl=sl, t0=t0):
                            dma(sy, "ld", u_b[:],
                                uT_sp[sl * 128:(sl + 1) * 128, t0:t0 + QW],
                                waits=[("v", C["v"]), ("sd", C["sd"])])
                        ld_u = C["ld"]

                        @block.vector
                        def _(v, sl=sl, spar=spar, t_d=t_d):
                            vop(v, lambda: v.tensor_scalar(
                                zp2[:, spar, :], psd_flat,
                                dtb_sb[:, sl:sl + 1], None,
                                op0=ALU.add),
                                waits=[("t", t_d)])
                        v_az_last = C["v"]

                        @block.scalar
                        def _(s, spar=spar, v_az=v_az_last):
                            sop(s, lambda: s.activation(az[:], zp2[:, spar, :],
                                                        AF.Abs),
                                waits=[("v", v_az)])
                            sop(s, lambda: s.activation(el2[:, spar, :], az[:],
                                                        AF.Exp, scale=-1.0))
                            sop(s, lambda: s.activation(el2[:, spar, :],
                                                        el2[:, spar, :],
                                                        AF.Ln, bias=1.0))
                            sop(s, lambda: s.activation(zp2[:, spar, :],
                                                        zp2[:, spar, :],
                                                        AF.Relu))
                        s_sp = C["s"]

                        @block.vector
                        def _(v, sl=sl, spar=spar, s_sp=s_sp, ld_u=ld_u):
                            vop(v, lambda: v.tensor_add(dl_all[:, sl, :],
                                                        zp2[:, spar, :],
                                                        el2[:, spar, :]),
                                waits=[("s", s_sp)])
                            vop(v, lambda: v.tensor_mul(du_all[:, sl, :],
                                                        dl_all[:, sl, :],
                                                        u_b[:]),
                                waits=[("ld", ld_u)])
                        v_dl[sl] = C["v"]

                    for ng in range(2):
                        s_bc = C["s"]
                        for j in range(8):
                            n = ng * 8 + j
                            @block.tensor
                            def _(t, n=n, qpar=qpar, ld_q=ld_q, s_bc=s_bc):
                                t.wait_ge(SEM["s"], s_bc)
                                t.wait_ge(SEM["qq"], ld_q)
                                for hk in range(2):
                                    last = t.matmul(ps_bb[:, 0, hk, :],
                                             sel_sb[:, n * 128:(n + 1) * 128],
                                             btq[0:16, qpar,
                                                 hk * 512:(hk + 1) * 512],
                                             start=True, stop=True)
                                for hk in range(2):
                                    last = t.matmul(ps_bb[:, 1, hk, :],
                                             sel_sb[:, n * 128:(n + 1) * 128],
                                             ctq[0:16, qpar,
                                                 hk * 512:(hk + 1) * 512],
                                             start=True, stop=True)
                                tick(last, "t")
                            t_bc = C["t"]

                            @block.scalar
                            def _(s, j=j, t_bc=t_bc, ng=ng):
                                sop(s, lambda: s.copy(
                                    Bbc[:, j, :],
                                    ps_bb.ap()[:, 0]
                                    .rearrange("p a b -> p (a b)")),
                                    waits=[("t", t_bc),
                                           ("v", v_bc_war[ng])])
                                sop(s, lambda: s.copy(
                                    Cbc[:, j, :],
                                    ps_bb.ap()[:, 1]
                                    .rearrange("p a b -> p (a b)")))
                            s_bc = C["s"]
                        s_bc_done = C["s"]

                        for sl in range(8):
                            s_base = C["s"]
                            @block.scalar
                            def _(s, sl=sl, ng=ng):
                                for j in range(8):
                                    n = ng * 8 + j
                                    w = [("v", max(v_dl[sl],
                                                   v_scan_hist[j]))]
                                    sop(s, lambda j=j, n=n: s.activation(
                                        a_buf[:, j, :], dl_all[:, sl, :],
                                        AF.Exp,
                                        scale=A_sb[:, sl * 16 + n:
                                                   sl * 16 + n + 1]),
                                        waits=w)

                            for j in range(8):
                                n = ng * 8 + j
                                hslot = j % 4
                                s_exp = s_base + j + 1
                                @block.vector
                                def _(v, sl=sl, j=j, n=n, hslot=hslot,
                                      s_exp=s_exp, qt=qt, ng=ng):
                                    vop(v, lambda: v.tensor_mul(
                                        bsc2[:, j % 2, :], du_all[:, sl, :],
                                        Bbc[:, j, :]),
                                        waits=([("s", s_bc_done)]
                                               if j == 0 else ()))
                                    if qt == 0:
                                        init = 0.0
                                    else:
                                        init = hst_sb[:, sl, n:n + 1]
                                    vop(v, lambda: v.tensor_tensor_scan(
                                        h4[:, hslot, :], a_buf[:, j, :],
                                        bsc2[:, j % 2, :], init,
                                        op0=ALU.mult, op1=ALU.add),
                                        waits=[("s", s_exp)])
                                    if j % 4 == 3:
                                        lo = ng * 8 + j - 3
                                        vop(v, lambda lo=lo: v.tensor_copy(
                                            hst_sb[:, sl, lo:lo + 4],
                                            h4.ap()[:, 0:4, QW - 1:QW]
                                            .rearrange("p a b -> p (a b)")))
                                    acc = yac[:, sl, :] if ng == 0 \
                                        else ptmp[:, 0, :]
                                    if j == 0:
                                        vop(v, lambda: v.tensor_mul(
                                            acc, h4[:, hslot, :],
                                            Cbc[:, j, :]))
                                    else:
                                        vop(v, lambda: v.tensor_mul(
                                            ptmp[:, 1, :], h4[:, hslot, :],
                                            Cbc[:, j, :]))
                                        vop(v, lambda: v.tensor_add(
                                            acc, acc, ptmp[:, 1, :]))
                                v_scan_hist[j] = C["v"]

                            if ng == 1:
                                @block.sync
                                def _(sy, sl=sl, t0=t0):
                                    dma(sy, "ld", u_b[:],
                                        uT_sp[sl * 128:(sl + 1) * 128,
                                              t0:t0 + QW],
                                        waits=[("v", C["v"])])
                                    dma(sy, "ld", gs_b[:],
                                        gsT_sp[sl * 128:(sl + 1) * 128,
                                               t0:t0 + QW])
                                ld_ug = C["ld"]

                                @block.vector
                                def _(v, sl=sl, ld_ug=ld_ug):
                                    vop(v, lambda: v.tensor_add(
                                        yf[:], ptmp[:, 0, :], yac[:, sl, :]))
                                    vop(v, lambda: v.scalar_tensor_tensor(
                                        yf[:], u_b[:], dp_sb[:, sl:sl + 1],
                                        yf[:], op0=ALU.mult, op1=ALU.add),
                                        waits=[("ld", ld_ug)])
                                    vop(v, lambda: v.tensor_mul(
                                        yo[:], yf[:], gs_b[:]),
                                        waits=[("sd", C["sd"])])
                                v_y = C["v"]

                                @block.gpsimd
                                def _(g, sl=sl, b=b, qt=qt, v_y=v_y):
                                    dma(g, "sd",
                                        yT_mine[b][sl * 128:(sl + 1) * 128,
                                                   qt * QW:(qt + 1) * QW],
                                        yo[:], waits=[("v", v_y)])
                        v_bc_war[ng] = C["v"]
                    if qt == 3:
                        sd_yh = C["sd"]
                        @block.gpsimd
                        def _(g, b=b, sd_yh=sd_yh):
                            g.wait_ge(SEM["sd"], sd_yh)
                            g.collective_compute(
                                "AllGather", ALU.bypass,
                                replica_groups=[list(range(NCORE))],
                                ins=[yT_mine[b].opt()],
                                outs=[yT_parts[b]
                                      .rearrange("r p t -> (r p) t").opt()],
                            ).then_inc(cc_sems[2 + b])

        if upto <= 6:
            @block.gpsimd
            def _(g):
                g.wait_ge(cc_sems[2], 1)
                g.wait_ge(cc_sems[3], 1)
            fence_all()
            return nc

        with ExitStack() as p8:
            e8 = p8.enter_context
            wos = e8(nc.sbuf_tensor("p8_wo", [128, 64, COL], BF16))
            ywin = e8(nc.sbuf_tensor("p8_yw", [128, 8, 1024], BF16))
            gq = e8(nc.sbuf_tensor("p8_g", [128, 8, COL], BF16))
            xres = e8(nc.sbuf_tensor("p8_x", [128, 8, COL], F32))
            sg = e8(nc.sbuf_tensor("p8_sg", [128, 2, COL], F32))
            ps8 = e8(nc.psum_tensor("p8_ps", [128, 8, 512], F32))

            @block.gpsimd
            def _(g):
                g.wait_ge(cc_sems[2], 1)
                tick(g.engine_nop(), "p")
            p_agy = [C["p"], None]

            @block.scalar
            def _(sy):
                for g4 in range(4):
                    dma(sy, "l2",
                        wos.ap()[:, g4 * 16:(g4 + 1) * 16, :]
                        .rearrange("p a b -> p (a b)"),
                        wout_ld[:, g4 * 16:(g4 + 1) * 16, :]
                        .rearrange("p a b -> p (a b)"),
                        waits=([("t", C["t"]), ("v", C["v"])]
                               if g4 == 0 else ()))
            ld_wo = C["l2"]

            x_hist = {i: 0 for i in range(4)}
            y_hist = {i: 0 for i in range(8)}
            YSEMS = ["y0", "y1", "y2", "y3", "w0", "w1", "w2", "w3"]
            sg_hist = {0: 0, 1: 0}
            v_comb_last = 0
            first_q = {(h, q): True for h in range(2) for q in range(2)}

            for tg in range(8):
                if tg == 4:
                    @block.gpsimd
                    def _(g):
                        g.wait_ge(cc_sems[3], 1)
                        tick(g.engine_nop(), "p")
                    p_agy[1] = C["p"]
                @block.gpsimd
                def _(g, tg=tg):
                    dma(g, "sd", gq.ap(),
                        gq_sp[tg * 8:tg * 8 + 8].rearrange("a p b -> p a b"),
                        waits=[("v", C["v"])])
                    for tt in range(8):
                        dma(g, "sd", xres[:, tt, :],
                            x_cols[tg * 1024 + tt * 128:
                                   tg * 1024 + (tt + 1) * 128, :])
                ld_xr = C["sd"]

                for kb in range(64):
                    slot = kb % 8
                    qsel = kb % 2
                    eng_blk = block.sync if qsel == 0 else block.scalar
                    ysem = YSEMS[slot]
                    @eng_blk
                    def _(sy, tg=tg, kb=kb, slot=slot, qsel=qsel, ysem=ysem):
                        w = [("t", y_hist[slot])]
                        if first_q[(tg // 4, qsel)]:
                            w.append(("p", p_agy[tg // 4]))
                            first_q[(tg // 4, qsel)] = False
                        dma(sy, ysem, ywin[:, slot, :],
                            yT_parts[tg // 4][kb // 8, (kb % 8) * 128:
                                              (kb % 8) * 128 + 128,
                                              (tg % 4) * 1024:
                                              (tg % 4 + 1) * 1024],
                            waits=w)
                    ld_y = C[ysem]

                    @block.tensor
                    def _(t, kb=kb, slot=slot, ysem=ysem, ld_y=ld_y,
                          v_comb_last=v_comb_last):
                        t.wait_ge(SEM[ysem], ld_y)
                        t.wait_ge(SEM["l2"], ld_wo)
                        if kb == 0:
                            t.wait_ge(SEM["v"], v_comb_last)
                        for tt in range(8):
                            last = t.matmul(ps8[:, tt, :],
                                     ywin[:, slot, tt * 128:(tt + 1) * 128],
                                     wos[:, kb, :],
                                     start=(kb == 0), stop=(kb == 63))
                        tick(last, "t")
                    y_hist[slot] = C["t"]
                t_s = C["t"]

                for tt in range(8):
                    sp = tt % 2
                    @block.vector
                    def _(v, tg=tg, tt=tt, sp=sp, t_s=t_s, ld_xr=ld_xr):
                        vop(v, lambda: v.tensor_mul(sg[:, sp, :],
                                                    ps8[:, tt, :],
                                                    gq[:, tt, :]),
                            waits=[("t", t_s), ("sd", C["sd"])])
                        vop(v, lambda: v.tensor_add(sg[:, sp, :],
                                                    sg[:, sp, :],
                                                    xres[:, tt, :]),
                            waits=[("sd", ld_xr)])
                    v_o = C["v"]

                    @block.gpsimd
                    def _(g, tg=tg, tt=tt, sp=sp, v_o=v_o):
                        dma(g, "sd",
                            out_cols[tg * 1024 + tt * 128:
                                     tg * 1024 + (tt + 1) * 128, :],
                            sg[:, sp, :], waits=[("v", v_o)])
                    sg_hist[sp] = C["sd"]
                v_comb_last = C["v"]

            sd_total = C["sd"]

            @block.gpsimd
            def _(g):
                g.wait_ge(SEM["sd"], sd_total)

            @block.sync
            def _(sy):
                sy.wait_ge(SEM["sd"], sd_total)

    return nc


_NC_CACHE = None


def kernel(L0_attn_hidden_states, L0_attn_W_in, L0_attn_conv_w, L0_attn_conv_b,
           L0_attn_Wx, L0_attn_dt_ln_w, L0_attn_b_ln_w, L0_attn_c_ln_w,
           L0_attn_Wdt, L0_attn_dt_bias, L0_attn_A, L0_attn_D_param,
           L0_attn_W_out, L0_ffn_weight_q, L0_ffn_scales_and_zeros):
    global _NC_CACHE
    x = np.asarray(L0_attn_hidden_states, dtype=np.float32)
    w_in = np.asarray(L0_attn_W_in, dtype=np.float32)
    conv_w = np.asarray(L0_attn_conv_w, dtype=np.float32)
    conv_b = np.asarray(L0_attn_conv_b, dtype=np.float32)
    wx = np.asarray(L0_attn_Wx, dtype=np.float32)
    dt_ln = np.asarray(L0_attn_dt_ln_w, dtype=np.float32)
    b_ln = np.asarray(L0_attn_b_ln_w, dtype=np.float32)
    c_ln = np.asarray(L0_attn_c_ln_w, dtype=np.float32)
    wdt = np.asarray(L0_attn_Wdt, dtype=np.float32)
    dt_bias = np.asarray(L0_attn_dt_bias, dtype=np.float32)
    a_mat = np.asarray(L0_attn_A, dtype=np.float32)
    d_param = np.asarray(L0_attn_D_param, dtype=np.float32)
    w_out = np.asarray(L0_attn_W_out, dtype=np.float32)
    wq = np.asarray(L0_ffn_weight_q, dtype=np.int32)
    snz = np.asarray(L0_ffn_scales_and_zeros, dtype=np.float32)

    return _run(x, w_in, conv_w, conv_b, wx, dt_ln, b_ln, c_ln, wdt, dt_bias,
                a_mat, d_param, w_out, wq, snz)


def make_in_maps(x, w_in, conv_w, conv_b, wx, dt_ln, b_ln, c_ln, wdt, dt_bias,
                 a_mat, d_param, w_out, wq, snz):
    xf = x.reshape(TOK, H)
    ident_b = np.eye(128, dtype=ml_dtypes.bfloat16)
    ident_f = np.eye(128, dtype=np.float32)
    sel16 = np.zeros((16, 2048), dtype=ml_dtypes.bfloat16)
    for n in range(16):
        sel16[n, n * 128:(n + 1) * 128] = 1.0

    in_maps = []
    for c in range(NCORE):
        dlo = c * DL
        clo = c * COL
        w_slab = np.concatenate([w_in[dlo:dlo + DL, :],
                                 w_in[D + dlo:D + dlo + DL, :]], axis=0)
        wdt_local = wdt[dlo:dlo + DL, :]
        wx_local = wx[:, dlo:dlo + DL]
        wout_local = w_out[clo:clo + COL, :]
        wq_local = wq[clo:clo + COL, :]
        in_maps.append(dict(
            x_rows=np.ascontiguousarray(xf[c * 1024:(c + 1) * 1024, :]),
            x_cols=np.ascontiguousarray(xf[:, clo:clo + COL]),
            win_f=np.ascontiguousarray(
                w_slab.reshape(16, 128, 32, 128)
                .transpose(0, 3, 2, 1).reshape(16, 128, H)),
            conv_w=np.ascontiguousarray(
                conv_w[dlo:dlo + DL, :].reshape(8, 128, 4)
                .transpose(1, 0, 2).reshape(128, 32)),
            conv_b=np.ascontiguousarray(
                conv_b[dlo:dlo + DL].reshape(8, 128).T),
            wx_f=np.ascontiguousarray(
                wx_local.T.reshape(8, 128, 288)
                .transpose(1, 0, 2).reshape(128, 2304)),
            wdt_f=np.ascontiguousarray(
                wdt_local.reshape(8, 128, 2, 128)
                .transpose(3, 0, 2, 1).reshape(128, 2048)),
            dt_bias=np.ascontiguousarray(
                dt_bias[dlo:dlo + DL].reshape(8, 128).T),
            a_mat=np.ascontiguousarray(
                a_mat[dlo:dlo + DL, :].reshape(8, 128, NN)
                .transpose(1, 0, 2).reshape(128, 128)),
            d_param=np.ascontiguousarray(
                d_param[dlo:dlo + DL].reshape(8, 128).T),
            wout_f=np.ascontiguousarray(
                wout_local.T.reshape(64, 128, COL)
                .transpose(1, 0, 2).reshape(128, 64 * COL)),
            wqT=np.ascontiguousarray(
                wq_local.T.reshape(32, 128, COL)),
            sc_bc=np.ascontiguousarray(np.broadcast_to(
                snz[:, clo:clo + COL, 0][:, None, :], (32, 128, COL))),
            zr_bc=np.ascontiguousarray(np.broadcast_to(
                snz[:, clo:clo + COL, 1][:, None, :], (32, 128, COL))),
            lnw_in=np.ascontiguousarray(np.broadcast_to(
                np.concatenate([dt_ln, b_ln, c_ln])[None, :], (128, 288))),
            ident_b=ident_b, ident_f=ident_f, sel16=sel16,
        ))

    return in_maps


def _run(x, w_in, conv_w, conv_b, wx, dt_ln, b_ln, c_ln, wdt, dt_bias,
         a_mat, d_param, w_out, wq, snz):
    global _NC_CACHE
    in_maps = make_in_maps(x, w_in, conv_w, conv_b, wx, dt_ln, b_ln, c_ln,
                           wdt, dt_bias, a_mat, d_param, w_out, wq, snz)
    if _NC_CACHE is None:
        _NC_CACHE = build_program()
    res = run_bass_kernel_spmd(_NC_CACHE, in_maps, core_ids=list(range(NCORE)))
    parts = [res.results[c]["out_cols"] for c in range(NCORE)]
    out = np.concatenate(parts, axis=1)
    return out.reshape(2, LB, H)


# version 23
# speedup vs baseline: 1.0675x; 1.0282x over previous
import numpy as np
import ml_dtypes
import concourse.bass as bass
import concourse.mybir as mybir
from concourse.bass_utils import run_bass_kernel_spmd
from contextlib import ExitStack

F32 = mybir.dt.float32
BF16 = mybir.dt.bfloat16
I32 = mybir.dt.int32
AF = mybir.ActivationFunctionType
ALU = mybir.AluOpType
AX = mybir.AxisListType

H = 4096
D = 8192
DL = 1024
NN = 16
R = 256
TOK = 8192
LB = 4096
COL = 512
NCORE = 8
EPS_LN = 1e-5
EPS_RMS = 1e-6


def build_program(upto=99):
    nc = bass.Bass("TRN2", target_bir_lowering=False, debug=False,
                   detect_race_conditions=False)

    def ein(name, shape, dt=F32):
        return nc.dram_tensor(name, shape, dt, kind="ExternalInput").ap()

    x_rows = ein("x_rows", [1024, H])
    x_cols = ein("x_cols", [TOK, COL])
    win_f = ein("win_f", [16, 128, H])
    conv_w = ein("conv_w", [128, 32])
    conv_b = ein("conv_b", [128, 8])
    wx_f = ein("wx_f", [128, 2304])
    wdt_f = ein("wdt_f", [128, 2048])
    dt_bias = ein("dt_bias", [128, 8])
    a_mat = ein("a_mat", [128, 128])
    d_param = ein("d_param", [128, 8])
    wout_f = ein("wout_f", [128, 64 * COL])
    wqT = ein("wqT", [32, 128, COL], I32)
    sc_bc = ein("sc_bc", [32, 128, COL])
    zr_bc = ein("zr_bc", [32, 128, COL])
    lnw_in = ein("lnw_in", [128, 288])
    ident_b = ein("ident_b", [128, 128], BF16)
    ident_f = ein("ident_f", [128, 128], F32)
    sel16 = ein("sel16", [16, 2048], BF16)

    out_cols = nc.dram_tensor("out_cols", [TOK, COL], F32,
                              kind="ExternalOutput").ap()

    def scr(name, shape, dt=BF16):
        return nc.dram_tensor(name, shape, dt).ap()

    xnT_mine = scr("xnT_mine", [H, 1024])
    xnT_parts = scr("xnT_parts", [NCORE, H, 1024])
    win_ld = scr("win_ld", [16, 128, H])
    wdt_ld = scr("wdt_ld", [128, 2048])
    wx_ld = scr("wx_ld", [128, 2304])
    wout_ld = scr("wout_ld", [128, 64, COL])
    wdeq_ld = scr("wdeq_ld", [32, 128, COL])
    uT_sp = scr("uT_sp", [DL, TOK])
    gsT_sp = scr("gsT_sp", [DL, TOK])
    sspT_mine = scr("sspT_mine", [288, TOK], F32)
    sspT_sum = scr("sspT_sum", [288, TOK], F32)
    yT_mine = [scr(f"yT_mine{h}", [DL, LB]) for h in range(2)]
    yT_parts = [scr(f"yT_parts{h}", [NCORE, DL, LB]) for h in range(2)]
    dtnT_sp = scr("dtnT_sp", [128, 2, TOK])
    gq_sp = scr("gq_sp", [64, 128, COL])
    BT_sp = scr("BT_sp", [16, TOK])
    CT_sp = scr("CT_sp", [16, TOK])

    st = ExitStack()
    with st:
        e = st.enter_context

        A_sb = e(nc.sbuf_tensor("A_sb", [128, 128], F32))
        cw_sb = e(nc.sbuf_tensor("cw_sb", [128, 32], F32))
        cb_sb = e(nc.sbuf_tensor("cb_sb", [128, 8], F32))
        dp_sb = e(nc.sbuf_tensor("dp_sb", [128, 8], F32))
        dtb_sb = e(nc.sbuf_tensor("dtb_sb", [128, 8], F32))
        identb = e(nc.sbuf_tensor("identb", [128, 128], BF16))
        identf = e(nc.sbuf_tensor("identf", [128, 128], F32))
        sel_sb = e(nc.sbuf_tensor("sel_sb", [16, 2048], BF16))
        lnw_bc = e(nc.sbuf_tensor("lnw_bc", [128, 288], F32))
        hst_sb = e(nc.sbuf_tensor("hst_sb", [128, 8, 16], BF16))
        halo_sb = e(nc.sbuf_tensor("halo_sb", [128, 8, 4], F32))
        kconst = e(nc.sbuf_tensor("kconst", [128, 8], F32))

        SEM = {}
        for k in ["t", "v", "s", "p", "ld", "l2", "sd",
                  "w0", "w1", "w2", "w3", "s0", "s1", "u0", "u1", "qq",
                  "y0", "y1", "y2", "y3"]:
            SEM[k] = e(nc.semaphore("sem_" + k))
        cc_sems = [e(nc.semaphore(f"sem_cc{i}")) for i in range(4)]
        C = {k: 0 for k in SEM}
        block = e(nc.Block())

        def tick(instr, k, by=1):
            C[k] += by
            instr.then_inc(SEM[k], by)
            return C[k]

        def mkop(key):
            def op(eng, emit, waits=()):
                for wk, wv in waits:
                    eng.wait_ge(SEM[wk], wv)
                eng.wait_ge(SEM[key], C[key])
                return tick(emit(), key)
            return op
        vop = mkop("v")
        sop = mkop("s")
        pop = mkop("p")

        def dma(eng, key, out, in_, waits=()):
            for wk, wv in waits:
                eng.wait_ge(SEM[wk], wv)
            return tick(eng.dma_start(out=out, in_=in_), key, 16)

        def fence_all():
            @block.sync
            def _(sy):
                for k in C:
                    sy.wait_ge(SEM[k], C[k])

        _CONSTS = [EPS_LN, EPS_RMS, 1.0 / H, 1.0 / R, 1.0 / 16.0]

        @block.vector
        def _(v):
            for i, val in enumerate(_CONSTS):
                vop(v, lambda i=i, val=val: v.memset(kconst[:, i:i + 1], val))
        for i, val in enumerate(_CONSTS):
            nc.const_aps.register(F32, val, kconst[:, i:i + 1])

        @block.sync
        def _(sy):
            dma(sy, "ld", A_sb[:], a_mat[:])
            dma(sy, "ld", cw_sb[:], conv_w[:])
            dma(sy, "ld", cb_sb[:], conv_b[:])
            dma(sy, "ld", dp_sb[:], d_param[:])
            dma(sy, "ld", dtb_sb[:], dt_bias[:])
            dma(sy, "ld", identb[:], ident_b[:])
            dma(sy, "ld", identf[:], ident_f[:])
            dma(sy, "ld", sel_sb[:], sel16[:])
            dma(sy, "ld", lnw_bc[:], lnw_in[:])
        ld_consts = C["ld"]
        v_lnbc = C["v"]

        if upto <= 0:
            fence_all()
            return nc


        if upto <= 1:
            fence_all()
            return nc

        with ExitStack() as p2:
            e2 = p2.enter_context
            xtile = [e2(nc.sbuf_tensor(f"p2_xt{i}", [128, H], F32)) for i in range(2)]
            xn_bf = [e2(nc.sbuf_tensor(f"p2_xn{i}", [128, H], BF16)) for i in range(8)]
            stat = e2(nc.sbuf_tensor("p2_stat", [128, 8], F32))
            sq = e2(nc.sbuf_tensor("p2_sq", [128, H], F32))
            band2 = e2(nc.sbuf_tensor("p2_band", [128, 1024], BF16))
            ps_tr = e2(nc.psum_tensor("p2_pstr", [128, 1024], BF16))
            for tt in range(8):
                @block.sync
                def _(sy, tt=tt):
                    dma(sy, "ld", xtile[tt % 2][:],
                        x_rows[tt * 128:(tt + 1) * 128, :],
                        waits=[("v", C["v"]), ("s", C["s"])])
                ld_here = C["ld"]
                xt = xtile[tt % 2]

                @block.vector
                def _(v, xt=xt, ld_here=ld_here):
                    vop(v, lambda: v.reduce_sum(stat[:, 0:1], xt[:], axis=AX.X),
                        waits=[("ld", ld_here)])
                v_m1 = C["v"]

                @block.scalar
                def _(s, xt=xt, v_m1=v_m1, ld_here=ld_here):
                    sop(s, lambda: s.mul(stat[:, 1:2], stat[:, 0:1], 1.0 / H),
                        waits=[("v", v_m1)])
                    sop(s, lambda: s.activation(sq[:], xt[:], AF.Square),
                        waits=[("ld", ld_here)])
                s_sq = C["s"]

                @block.vector
                def _(v, s_sq=s_sq):
                    vop(v, lambda: v.reduce_sum(stat[:, 2:3], sq[:], axis=AX.X),
                        waits=[("s", s_sq)])
                    vop(v, lambda: v.tensor_mul(stat[:, 3:4], stat[:, 1:2],
                                                stat[:, 1:2]))
                v_m2 = C["v"]

                @block.scalar
                def _(s, v_m2=v_m2):
                    sop(s, lambda: s.mul(stat[:, 4:5], stat[:, 2:3], 1.0 / H),
                        waits=[("v", v_m2)])
                s_ms = C["s"]

                @block.vector
                def _(v, s_ms=s_ms):
                    vop(v, lambda: v.tensor_sub(stat[:, 5:6], stat[:, 4:5],
                                                stat[:, 3:4]),
                        waits=[("s", s_ms)])
                v_var = C["v"]

                @block.scalar
                def _(s, v_var=v_var):
                    sop(s, lambda: s.activation(stat[:, 6:7], stat[:, 5:6],
                                                AF.Ln, bias=EPS_LN),
                        waits=[("v", v_var)])
                    sop(s, lambda: s.activation(stat[:, 7:8], stat[:, 6:7],
                                                AF.Exp, scale=-0.5))
                s_rstd = C["s"]

                @block.vector
                def _(v, tt=tt, xt=xt, s_rstd=s_rstd):
                    vop(v, lambda: v.tensor_scalar(
                        xn_bf[tt][:], xt[:], stat[:, 1:2], stat[:, 7:8],
                        op0=ALU.subtract, op1=ALU.mult),
                        waits=[("s", s_rstd)])
            v_xn = C["v"]

            for hb in range(32):
                @block.tensor
                def _(t, hb=hb, v_xn=v_xn):
                    t.wait_ge(SEM["v"], v_xn if hb == 0 else C["v"])
                    for tt in range(8):
                        last = t.transpose(ps_tr[:, tt * 128:(tt + 1) * 128],
                                    xn_bf[tt][:, hb * 128:(hb + 1) * 128],
                                    identb[:])
                    tick(last, "t")
                t_tr = C["t"]

                @block.vector
                def _(v, t_tr=t_tr):
                    vop(v, lambda: v.tensor_copy(band2[:], ps_tr[:]),
                        waits=[("t", t_tr), ("sd", C["sd"])])
                v_b = C["v"]

                @block.gpsimd
                def _(g, hb=hb, v_b=v_b):
                    dma(g, "sd", xnT_mine[hb * 128:(hb + 1) * 128, :],
                        band2[:], waits=[("v", v_b)])
            sd_xnT = C["sd"]

            @block.gpsimd
            def _(g):
                g.wait_ge(SEM["sd"], sd_xnT)
                g.collective_compute(
                    "AllGather", ALU.bypass,
                    replica_groups=[list(range(NCORE))],
                    ins=[xnT_mine.opt()],
                    outs=[xnT_parts.rearrange("r p t -> (r p) t").opt()],
                ).then_inc(cc_sems[0])

        with ExitStack() as p1:
            e1 = p1.enter_context
            wtile = [e1(nc.sbuf_tensor(f"p1_wt{i}", [128, 4096], F32))
                     for i in range(2)]
            wcast = [e1(nc.sbuf_tensor(f"p1_wc{i}", [128, 4096], BF16))
                     for i in range(2)]
            qtl = e1(nc.sbuf_tensor("p1_qt", [128, 2, COL], I32))
            sctl = e1(nc.sbuf_tensor("p1_sc", [128, 2, COL], F32))
            zrtl = e1(nc.sbuf_tensor("p1_zr", [128, 2, COL], F32))
            qf = e1(nc.sbuf_tensor("p1_qf", [128, 2, COL], F32))
            wqc = e1(nc.sbuf_tensor("p1_wqc", [128, 2, COL], BF16))
            h_cast = {0: 0, 1: 0}
            h_cst = {0: 0, 1: 0}

            cast_jobs = []
            for rb in range(16):
                cast_jobs.append((4096, win_f[rb],
                                  win_ld[rb]))
            cast_jobs.append((2048, wdt_f[:], wdt_ld[:]))
            cast_jobs.append((2304, wx_f[:], wx_ld[:]))
            for g8 in range(8):
                cast_jobs.append((4096, wout_f[:, g8 * 4096:(g8 + 1) * 4096],
                                  wout_ld[:, g8 * 8:(g8 + 1) * 8, :]
                                  .rearrange("p a b -> p (a b)")))

            for i, (wdth, srcap, dstap) in enumerate(cast_jobs):
                par = i % 2
                psem = "y%d" % par
                @block.sync
                def _(sy, i=i, par=par, psem=psem, srcap=srcap, wdth=wdth):
                    w = [("v", h_cast[par])]
                    if i < 2:
                        w += [("t", C["t"]), ("v", C["v"]), ("s", C["s"]),
                              ("sd", C["sd"])]
                    dma(sy, psem, wtile[par][:, 0:wdth], srcap, waits=w)
                ld_h = C[psem]

                @block.vector
                def _(v, par=par, psem=psem, wdth=wdth, ld_h=ld_h):
                    vop(v, lambda: v.tensor_copy(wcast[par][:, 0:wdth],
                                                 wtile[par][:, 0:wdth]),
                        waits=[(psem, ld_h), ("sd", C["sd"])])
                h_cast[par] = C["v"]
                v_c = C["v"]

                @block.gpsimd
                def _(g, par=par, wdth=wdth, dstap=dstap, v_c=v_c):
                    dma(g, "sd", dstap, wcast[par][:, 0:wdth],
                        waits=[("v", v_c)])
                h_cst[par] = C["sd"]
                if i == 15:
                    sd_win = C["sd"]

            for i in range(16):
                @block.sync
                def _(sy, i=i):
                    w = ([("v", C["v"])] if i >= 1 else
                         [("t", C["t"]), ("v", C["v"]), ("s", C["s"]),
                          ("sd", C["sd"])])
                    dma(sy, "ld", qtl.ap(),
                        wqT[2 * i:2 * i + 2].rearrange("k p c -> p k c"),
                        waits=w)
                    dma(sy, "ld", sctl.ap(),
                        sc_bc[2 * i:2 * i + 2].rearrange("k p c -> p k c"))
                    dma(sy, "ld", zrtl.ap(),
                        zr_bc[2 * i:2 * i + 2].rearrange("k p c -> p k c"))
                ld_h = C["ld"]

                @block.vector
                def _(v, i=i, ld_h=ld_h):
                    vop(v, lambda: v.tensor_copy(
                        qf.ap().rearrange("p a b -> p (a b)"),
                        qtl.ap().rearrange("p a b -> p (a b)")),
                        waits=[("ld", ld_h), ("sd", C["sd"])])
                    vop(v, lambda: v.scalar_tensor_tensor(
                        qf.ap().rearrange("p a b -> p (a b)"),
                        qf.ap().rearrange("p a b -> p (a b)"),
                        8.0,
                        sctl.ap().rearrange("p a b -> p (a b)"),
                        op0=ALU.subtract, op1=ALU.mult))
                    vop(v, lambda: v.tensor_add(
                        wqc.ap().rearrange("p a b -> p (a b)"),
                        qf.ap().rearrange("p a b -> p (a b)"),
                        zrtl.ap().rearrange("p a b -> p (a b)")))
                v_c = C["v"]

                @block.gpsimd
                def _(g, i=i, v_c=v_c):
                    dma(g, "sd",
                        wdeq_ld[2 * i:2 * i + 2].rearrange("k p c -> p k c"),
                        wqc.ap(), waits=[("v", v_c)])
        sd_prep = C["sd"]

        @block.gpsimd
        def _(g):
            g.wait_ge(cc_sems[0], 1)
            tick(g.engine_nop(), "p")
        p_ag0 = C["p"]

        with ExitStack() as pf:
            ef = pf.enter_context
            wqs = ef(nc.sbuf_tensor("pf_wq", [128, 32, COL], BF16))
            xwf = ef(nc.sbuf_tensor("pf_xw", [128, 4, 1024], BF16))
            gq_st = ef(nc.sbuf_tensor("pf_gst", [128, 8, COL], BF16))
            ps8f = ef(nc.psum_tensor("pf_ps", [128, 8, 512], F32))

            @block.sync
            def _(sy):
                for kb in range(32):
                    dma(sy, "ld", wqs[:, kb, :], wdeq_ld[kb],
                        waits=([("sd", sd_prep)] if kb == 0 else ()))
            ld_wq = C["ld"]

            x_hist = {i: 0 for i in range(4)}
            s_gq_last = 0
            for tg in range(8):
                for kb in range(32):
                    slot = kb % 4
                    qsel = kb % 2
                    eng_blk = block.sync if qsel == 0 else block.scalar
                    xsem = "w%d" % slot
                    @eng_blk
                    def _(sy, tg=tg, kb=kb, slot=slot, xsem=xsem):
                        w = [("t", x_hist[slot])]
                        if tg == 0 and kb < 2:
                            w.append(("p", p_ag0))
                            w.append(("sd", sd_prep))
                        dma(sy, xsem, xwf[:, slot, :],
                            xnT_parts[tg, kb * 128:(kb + 1) * 128, :],
                            waits=w)
                    ld_x = C[xsem]

                    @block.tensor
                    def _(t, kb=kb, slot=slot, xsem=xsem, ld_x=ld_x,
                          s_gq_last=s_gq_last):
                        t.wait_ge(SEM[xsem], ld_x)
                        if kb == 0:
                            t.wait_ge(SEM["ld"], ld_wq)
                            t.wait_ge(SEM["s"], s_gq_last)
                        for tt in range(8):
                            last = t.matmul(ps8f[:, tt, :],
                                     xwf[:, slot, tt * 128:(tt + 1) * 128],
                                     wqs[:, kb, :],
                                     start=(kb == 0), stop=(kb == 31))
                        tick(last, "t")
                    x_hist[slot] = C["t"]
                t_ffn = C["t"]

                @block.scalar
                def _(sc, tg=tg, t_ffn=t_ffn):
                    for tt in range(8):
                        sop(sc, lambda tt=tt: sc.activation(
                            gq_st[:, tt, :], ps8f[:, tt, :], AF.Silu),
                            waits=([("t", t_ffn), ("sd", C["sd"])]
                                   if tt == 0 else ()))
                s_gq_last = C["s"]

                @block.gpsimd
                def _(g, tg=tg, s_gq_last=s_gq_last):
                    dma(g, "sd", gq_sp[tg * 8:tg * 8 + 8]
                        .rearrange("a p b -> p a b"),
                        gq_st.ap(), waits=[("s", s_gq_last)])
        sd_ffn = C["sd"]
        s_ffn = C["s"]

        if upto <= 2:
            @block.gpsimd
            def _(g):
                g.wait_ge(cc_sems[0], 1)
            fence_all()
            return nc

        with ExitStack() as p3:
            e3 = p3.enter_context
            xng = e3(nc.sbuf_tensor("p3_xng", [128, 32, 1024], BF16))
            wslab = [e3(nc.sbuf_tensor(f"p3_ws{i}", [128, 32, 128], BF16))
                     for i in range(2)]
            upre = e3(nc.sbuf_tensor("p3_upre", [128, 2, 1028], F32))
            uacc = e3(nc.sbuf_tensor("p3_uacc", [128, 2, 1024], F32))
            ub = e3(nc.sbuf_tensor("p3_ub", [128, 2, 1024], BF16))
            ps_a = e3(nc.psum_tensor("p3_psa", [128, 2, 2, 512], F32))

            @block.vector
            def _(v):
                vop(v, lambda: v.memset(halo_sb.ap().rearrange("p a b -> p (a b)"),
                                        0.0))

            h_evac = {0: 0, 1: 0}
            h_vsave = {0: 0, 1: 0}
            h_uread = {0: 0, 1: 0}
            h_store = {0: 0, 1: 0}
            h_wld = {0: 0, 1: 0}

            for grp in range(8):
                xbuf = xng
                gate = [("t", C["t"]), ("p", p_ag0)]
                if grp == 0:
                    gate += [("sd", C["sd"]), ("s", C["s"]), ("v", C["v"])]
                for qsel, blk in ((0, block.sync), (1, block.scalar)):
                    @blk
                    def _(sy, grp=grp, xbuf=xbuf, qsel=qsel, gate=gate):
                        first = True
                        for hb in range(32):
                            if hb % 2 != qsel:
                                continue
                            dma(sy, "w%d" % (hb // 8), xbuf[:, hb, :],
                                xnT_parts[grp, hb * 128:(hb + 1) * 128, :],
                                waits=(gate if first else ()))
                            first = False
                wv_cnt = [C["w%d" % i] for i in range(4)]

                for dt in range(16):
                    par = dt % 2
                    wbuf = wslab[par]
                    @block.sync
                    def _(sy, dt=dt, par=par, wbuf=wbuf, grp=grp):
                        w = [("t", h_wld[par])]
                        if grp == 0 and dt < 2:
                            w.append(("sd", sd_ffn))
                            w.append(("s", s_ffn))
                        dma(sy, "s%d" % par,
                            wbuf.ap().rearrange("p a b -> p (a b)"),
                            win_ld[dt], waits=w)
                    ld_w = C["s%d" % par]

                    @block.tensor
                    def _(t, dt=dt, par=par, wbuf=wbuf, xbuf=xbuf, grp=grp,
                          wv_cnt=wv_cnt, ld_w=ld_w):
                        t.wait_ge(SEM["s%d" % par], ld_w)
                        t.wait_ge(SEM["s"], h_evac[par])
                        for k in range(32):
                            if dt == 0 and k % 8 == 0:
                                t.wait_ge(SEM["w%d" % (k // 8)],
                                          wv_cnt[k // 8])
                            for bk in range(2):
                                last = t.matmul(ps_a[:, par, bk, :],
                                         wbuf[:, k, :],
                                         xbuf[:, k, bk * 512:(bk + 1) * 512],
                                         start=(k == 0), stop=(k == 31))
                        tick(last, "t")
                    t_mm = C["t"]
                    h_wld[par] = t_mm

                    ps_flat = ps_a.ap()[:, par].rearrange("p a b -> p (a b)")
                    if dt < 8:
                        @block.scalar
                        def _(s, par=par, t_mm=t_mm, ps_flat=ps_flat):
                            sop(s, lambda: s.copy(upre[:, par, 4:1028],
                                                  ps_flat),
                                waits=[("t", t_mm), ("v", h_vsave[par])])
                        s_evac = C["s"]
                        h_evac[par] = s_evac

                        @block.vector
                        def _(v, dt=dt, par=par, s_evac=s_evac):
                            vop(v, lambda: v.tensor_copy(upre[:, par, 1:4],
                                                         halo_sb[:, dt, 1:4]),
                                waits=[("s", s_evac)])
                            vop(v, lambda: v.tensor_scalar(
                                uacc[:, par, :], upre[:, par, 4:1028],
                                cw_sb[:, dt * 4 + 3:dt * 4 + 4], None,
                                op0=ALU.mult),
                                waits=[("s", h_uread[par])])
                            for kk in range(3):
                                vop(v, lambda kk=kk: v.scalar_tensor_tensor(
                                    uacc[:, par, :],
                                    upre[:, par, 3 - kk:1027 - kk],
                                    cw_sb[:, dt * 4 + 2 - kk:dt * 4 + 3 - kk],
                                    uacc[:, par, :],
                                    op0=ALU.mult, op1=ALU.add))
                            vop(v, lambda: v.tensor_copy(
                                halo_sb[:, dt, 1:4],
                                upre[:, par, 1025:1028]))
                        v_savehalo = C["v"]
                        h_vsave[par] = v_savehalo

                        @block.scalar
                        def _(s, dt=dt, par=par, v_savehalo=v_savehalo):
                            sop(s, lambda: s.activation(
                                ub[:, par, :], uacc[:, par, :], AF.Silu,
                                bias=cb_sb[:, dt:dt + 1]),
                                waits=[("v", v_savehalo),
                                       ("sd", C["sd"])])
                        s_u = C["s"]
                        h_uread[par] = s_u

                        @block.gpsimd
                        def _(g, grp=grp, dt=dt, par=par, s_u=s_u):
                            dma(g, "sd",
                                uT_sp[dt * 128:(dt + 1) * 128,
                                      grp * 1024:(grp + 1) * 1024],
                                ub[:, par, :], waits=[("s", s_u)])
                        h_store[par] = C["sd"]
                    else:
                        @block.scalar
                        def _(s, par=par, t_mm=t_mm, ps_flat=ps_flat):
                            sop(s, lambda: s.activation(
                                ub[:, par, :], ps_flat, AF.Silu),
                                waits=[("t", t_mm), ("sd", C["sd"])])
                        s_g = C["s"]
                        h_evac[par] = s_g

                        @block.gpsimd
                        def _(g, grp=grp, dt=dt, par=par, s_g=s_g):
                            dma(g, "sd",
                                gsT_sp[(dt - 8) * 128:(dt - 7) * 128,
                                       grp * 1024:(grp + 1) * 1024],
                                ub[:, par, :], waits=[("s", s_g)])
                        h_store[par] = C["sd"]

                if grp == 3:
                    @block.vector
                    def _(v):
                        vop(v, lambda: v.memset(
                            halo_sb.ap().rearrange("p a b -> p (a b)"), 0.0))

        if upto <= 3:
            fence_all()
            return nc

        with ExitStack() as p4:
            e4 = p4.enter_context
            ubig = [e4(nc.sbuf_tensor(f"p4_u{i}", [128, 8, 2048], BF16))
                    for i in range(2)]
            wxs = e4(nc.sbuf_tensor("p4_wx", [128, 8, 288], BF16))
            ev = e4(nc.sbuf_tensor("p4_ev", [128, 2, 2048], F32))
            ps_m = e4(nc.psum_tensor("p4_psm", [128, 2, 4, 512], F32))

            @block.sync
            def _(sy):
                dma(sy, "ld", wxs.ap().rearrange("p a b -> p (a b)"),
                    wx_ld[:],
                    waits=[("t", C["t"]), ("s", C["s"]), ("v", C["v"]),
                           ("sd", C["sd"])])
            ld_wx = C["ld"]

            h_u4 = {0: 0, 1: 0}
            h_pm = {0: 0, 1: 0}
            h_ev = {0: 0, 1: 0}
            mi = 0
            for qt in range(4):
                ubuf = ubig[qt % 2]
                usem = "u%d" % (qt % 2)
                @block.sync
                def _(sy, qt=qt, ubuf=ubuf, usem=usem):
                    w = [("t", h_u4[qt % 2]), ("sd", C["sd"])]
                    for kb in range(8):
                        dma(sy, usem, ubuf[:, kb, :],
                            uT_sp[kb * 128:(kb + 1) * 128,
                                  qt * 2048:(qt + 1) * 2048],
                            waits=(w if kb == 0 else ()))
                ld_u = C[usem]

                for mt in range(3):
                    mrows = 128 if mt < 2 else 32
                    mp = mi % 2
                    @block.tensor
                    def _(t, qt=qt, mt=mt, mp=mp, mrows=mrows, ubuf=ubuf,
                          usem=usem, ld_u=ld_u):
                        t.wait_ge(SEM[usem], ld_u)
                        t.wait_ge(SEM["ld"], ld_wx)
                        t.wait_ge(SEM["v"], h_pm[mp])
                        for k in range(8):
                            for bk in range(4):
                                last = t.matmul(ps_m[0:mrows, mp, bk, :],
                                         wxs[:, k, mt * 128:mt * 128 + mrows],
                                         ubuf[:, k, bk * 512:(bk + 1) * 512],
                                         start=(k == 0), stop=(k == 7))
                        tick(last, "t")
                    t_mm = C["t"]
                    h_u4[qt % 2] = t_mm

                    @block.vector
                    def _(v, mp=mp, mrows=mrows, t_mm=t_mm):
                        vop(v, lambda: v.tensor_copy(
                            ev[0:mrows, mp, :],
                            ps_m.ap()[0:mrows, mp]
                            .rearrange("p a b -> p (a b)")),
                            waits=[("t", t_mm), ("sd", C["sd"])])
                    v_ev = C["v"]
                    h_pm[mp] = v_ev

                    @block.gpsimd
                    def _(g, qt=qt, mt=mt, mp=mp, mrows=mrows, v_ev=v_ev):
                        dma(g, "sd",
                            sspT_mine[mt * 128:mt * 128 + mrows,
                                      qt * 2048:(qt + 1) * 2048],
                            ev[0:mrows, mp, :], waits=[("v", v_ev)])
                    h_ev[mp] = C["sd"]
                    mi += 1
            sd_ssp = C["sd"]

            @block.gpsimd
            def _(g):
                g.wait_ge(SEM["sd"], sd_ssp)
                g.collective_compute(
                    "AllReduce", ALU.add,
                    replica_groups=[list(range(NCORE))],
                    ins=[sspT_mine.opt()], outs=[sspT_sum.opt()],
                ).then_inc(cc_sems[1])

        if upto <= 4:
            @block.gpsimd
            def _(g):
                g.wait_ge(cc_sems[1], 1)
            fence_all()
            return nc

        with ExitStack() as p5:
            e5 = p5.enter_context
            ssp = [e5(nc.sbuf_tensor(f"p5_ssp{i}", [128, TOK // 2], F32))
                   for i in range(4)]
            ssp3 = e5(nc.sbuf_tensor("p5_sspC", [32, TOK], F32))
            tok = e5(nc.sbuf_tensor("p5_tok", [128, 288], F32))
            dtn = e5(nc.sbuf_tensor("p5_dtn", [128, 288], BF16))
            nrm = e5(nc.sbuf_tensor("p5_nrm", [128, 8], F32))
            sqv = e5(nc.sbuf_tensor("p5_sq", [128, 288], F32))
            ps_f = e5(nc.psum_tensor("p5_psf", [128, 512], F32))
            ps_b = e5(nc.psum_tensor("p5_psb", [128, 2, 512], BF16))
            stage5 = e5(nc.sbuf_tensor("p5_stage", [128, 2, 512], BF16))
            sd_hist5 = {}

            @block.gpsimd
            def _(g):
                g.wait_ge(cc_sems[1], 1)
                tick(g.engine_nop(), "p")
            p_ar = C["p"]

            @block.sync
            def _(sy):
                w = [("p", p_ar), ("v", C["v"]), ("t", C["t"])]
                for i in range(4):
                    dma(sy, "ld", ssp[i][:],
                        sspT_sum[(i % 2) * 128:(i % 2) * 128 + 128,
                                 (i // 2) * 4096:(i // 2) * 4096 + 4096],
                        waits=(w if i == 0 else ()))
                dma(sy, "ld", ssp3[:], sspT_sum[256:288, :])
            ld_ssp = C["ld"]

            for tt in range(64):
                half = (tt * 128) // 4096
                off = (tt * 128) % 4096
                @block.tensor
                def _(t, tt=tt, half=half, off=off, ld_ssp=ld_ssp):
                    t.wait_ge(SEM["ld"], ld_ssp)
                    t.wait_ge(SEM["v"], C["v"])
                    last = t.transpose(ps_f[:, 0:128],
                                ssp[2 * half][:, off:off + 128], identf[:])
                    last = t.transpose(ps_f[:, 128:256],
                                ssp[2 * half + 1][:, off:off + 128], identf[:])
                    last = t.transpose(ps_f[:, 256:288],
                                ssp3[:, tt * 128:tt * 128 + 128],
                                identf[0:32, 0:32])
                    tick(last, "t")
                t_tr = C["t"]

                @block.vector
                def _(v, t_tr=t_tr):
                    vop(v, lambda: v.tensor_copy(tok[:], ps_f[:, 0:288]),
                        waits=[("t", t_tr)])
                v_tok = C["v"]

                @block.scalar
                def _(s, v_tok=v_tok):
                    sop(s, lambda: s.activation(sqv[:], tok[:], AF.Square),
                        waits=[("v", v_tok)])
                s_sq = C["s"]

                @block.vector
                def _(v, s_sq=s_sq):
                    vop(v, lambda: v.reduce_sum(nrm[:, 0:1], sqv[:, 0:R],
                                                axis=AX.X), waits=[("s", s_sq)])
                    vop(v, lambda: v.reduce_sum(nrm[:, 1:2], sqv[:, R:R + 16],
                                                axis=AX.X))
                    vop(v, lambda: v.reduce_sum(nrm[:, 2:3], sqv[:, R + 16:288],
                                                axis=AX.X))
                v_rs = C["v"]

                @block.scalar
                def _(s, v_rs=v_rs):
                    for j, w_ in [(0, R), (1, 16), (2, 16)]:
                        sop(s, lambda j=j, w_=w_: s.activation(
                            nrm[:, 3 + j:4 + j], nrm[:, j:j + 1], AF.Ln,
                            scale=1.0 / w_, bias=EPS_RMS),
                            waits=([("v", v_rs)] if j == 0 else ()))
                        sop(s, lambda j=j: s.activation(
                            nrm[:, 3 + j:4 + j], nrm[:, 3 + j:4 + j], AF.Exp,
                            scale=-0.5))
                s_r = C["s"]

                @block.vector
                def _(v, s_r=s_r):
                    vop(v, lambda: v.tensor_scalar(
                        tok[:, 0:R], tok[:, 0:R], nrm[:, 3:4], None,
                        op0=ALU.mult), waits=[("s", s_r)])
                    vop(v, lambda: v.tensor_scalar(
                        tok[:, R:R + 16], tok[:, R:R + 16], nrm[:, 4:5], None,
                        op0=ALU.mult))
                    vop(v, lambda: v.tensor_scalar(
                        tok[:, R + 16:288], tok[:, R + 16:288], nrm[:, 5:6],
                        None, op0=ALU.mult))
                    vop(v, lambda: v.tensor_mul(dtn[:], tok[:], lnw_bc[:]),
                        waits=[("ld", ld_consts)])
                v_dtn = C["v"]

                par = tt % 2
                @block.tensor
                def _(t, tt=tt, par=par, v_dtn=v_dtn):
                    t.wait_ge(SEM["v"], v_dtn)
                    last = t.transpose(ps_b[:, par, 0:128], dtn[:, 0:128],
                                       identb[:])
                    last = t.transpose(ps_b[:, par, 128:256], dtn[:, 128:256],
                                       identb[:])
                    last = t.transpose(ps_b[0:16, par, 256:384],
                                       dtn[:, 256:272], identb[:])
                    last = t.transpose(ps_b[0:16, par, 384:512],
                                       dtn[:, 272:288], identb[:])
                    tick(last, "t")
                t_tr2 = C["t"]

                @block.vector
                def _(v, tt=tt, par=par, t_tr2=t_tr2):
                    w = [("t", t_tr2), ("sd", C["sd"])]
                    vop(v, lambda: v.tensor_copy(stage5[:, par, :],
                                                 ps_b[:, par, :]), waits=w)
                v_stage = C["v"]

                @block.gpsimd
                def _(g, tt=tt, par=par, v_stage=v_stage):
                    dma(g, "sd", dtnT_sp[:, :, tt * 128:(tt + 1) * 128],
                        stage5.ap()[:, par, 0:256]
                        .rearrange("p (a b) -> p a b", a=2),
                        waits=[("v", v_stage)])
                    dma(g, "sd", BT_sp[:, tt * 128:(tt + 1) * 128],
                        stage5[0:16, par, 256:384])
                    dma(g, "sd", CT_sp[:, tt * 128:(tt + 1) * 128],
                        stage5[0:16, par, 384:512])
                sd_hist5[tt] = C["sd"]
        v_p5 = C["v"]
        sd_p5 = C["sd"]

        if upto <= 5:
            fence_all()
            return nc

        with ExitStack() as p6:
            e6 = p6.enter_context
            QW = 1024
            wdts = e6(nc.sbuf_tensor("p6_wdt", [128, 8, 2, 128], BF16))
            dtn_q = e6(nc.sbuf_tensor("p6_dtnq", [128, 2, 2, QW], BF16))
            btq = e6(nc.sbuf_tensor("p6_btq", [16, 2, QW], BF16))
            ctq = e6(nc.sbuf_tensor("p6_ctq", [16, 2, QW], BF16))
            Bbc = e6(nc.sbuf_tensor("p6_B", [128, 8, QW], BF16))
            Cbc = e6(nc.sbuf_tensor("p6_C", [128, 8, QW], BF16))
            dl_all = e6(nc.sbuf_tensor("p6_dla", [128, 8, QW], BF16))
            du_all = e6(nc.sbuf_tensor("p6_dua", [128, 8, QW], BF16))
            a_buf = e6(nc.sbuf_tensor("p6_ab", [128, 8, QW], BF16))
            h4 = e6(nc.sbuf_tensor("p6_h", [128, 4, QW], BF16))
            bsc2 = e6(nc.sbuf_tensor("p6_bs", [128, 2, QW], BF16))
            ptmp = e6(nc.sbuf_tensor("p6_pt", [128, 2, QW], BF16))
            yac = e6(nc.sbuf_tensor("p6_yac", [128, 8, QW], BF16))
            u_b = e6(nc.sbuf_tensor("p6_u", [128, QW], BF16))
            gs_b = e6(nc.sbuf_tensor("p6_gs", [128, QW], BF16))
            yf = e6(nc.sbuf_tensor("p6_yf", [128, QW], F32))
            yo = e6(nc.sbuf_tensor("p6_yo", [128, QW], BF16))
            az = e6(nc.sbuf_tensor("p6_az", [128, QW], F32))
            zp2 = e6(nc.sbuf_tensor("p6_zp", [128, 2, QW], F32))
            el2 = e6(nc.sbuf_tensor("p6_el", [128, 2, QW], F32))
            ps_d = e6(nc.psum_tensor("p6_psd", [128, 2, 512], F32))
            ps_bb = e6(nc.psum_tensor("p6_psbb", [128, 2, 2, 512], F32))

            @block.sync
            def _(sy):
                dma(sy, "qq",
                    wdts.ap().rearrange("p a b c -> p (a b c)"), wdt_ld[:],
                    waits=[("t", C["t"]), ("v", C["v"]), ("sd", C["sd"])])
            ld_wdt = C["qq"]

            psd_flat = ps_d.ap().rearrange("p a b -> p (a b)")
            v_scan_hist = {j: 0 for j in range(8)}
            v_bc_war = {0: 0, 1: 0}
            v_az_last = 0

            for b in range(2):
                for qt in range(4):
                    qi = b * 4 + qt
                    t0 = b * LB + qt * QW
                    qpar = qi % 2
                    @block.sync
                    def _(sy, t0=t0, qpar=qpar):
                        w = [("sd", sd_p5), ("t", C["t"])]
                        dma(sy, "qq", dtn_q.ap()[:, qpar, :, :],
                            dtnT_sp[:, :, t0:t0 + QW], waits=w)
                        dma(sy, "qq", btq[:, qpar, :], BT_sp[:, t0:t0 + QW])
                        dma(sy, "qq", ctq[:, qpar, :], CT_sp[:, t0:t0 + QW])
                    ld_q = C["qq"]

                    v_dl = {}
                    for sl in range(8):
                        spar = sl % 2
                        @block.tensor
                        def _(t, sl=sl, qpar=qpar, ld_q=ld_q):
                            t.wait_ge(SEM["qq"], ld_q)
                            t.wait_ge(SEM["s"], v_az_last)
                            for k in range(2):
                                for bk in range(2):
                                    last = t.matmul(ps_d[:, bk, :],
                                             wdts[:, sl, k, :],
                                             dtn_q[:, qpar, k,
                                                   bk * 512:(bk + 1) * 512],
                                             start=(k == 0), stop=(k == 1))
                            tick(last, "t")
                        t_d = C["t"]

                        @block.sync
                        def _(sy, sl=sl, t0=t0):
                            dma(sy, "ld", u_b[:],
                                uT_sp[sl * 128:(sl + 1) * 128, t0:t0 + QW],
                                waits=[("v", C["v"]), ("sd", C["sd"])])
                        ld_u = C["ld"]

                        @block.scalar
                        def _(s, sl=sl, spar=spar, t_d=t_d):
                            sop(s, lambda: s.activation(
                                az[:], psd_flat, AF.Abs,
                                bias=dtb_sb[:, sl:sl + 1]),
                                waits=[("t", t_d), ("v", C["v"])])
                            sop(s, lambda: s.activation(el2[:, spar, :], az[:],
                                                        AF.Exp, scale=-1.0))
                            sop(s, lambda: s.activation(el2[:, spar, :],
                                                        el2[:, spar, :],
                                                        AF.Ln, bias=1.0))
                            sop(s, lambda: s.activation(
                                zp2[:, spar, :], psd_flat, AF.Relu,
                                bias=dtb_sb[:, sl:sl + 1]))
                        s_sp = C["s"]
                        v_az_last = s_sp

                        @block.vector
                        def _(v, sl=sl, spar=spar, s_sp=s_sp, ld_u=ld_u):
                            vop(v, lambda: v.tensor_add(dl_all[:, sl, :],
                                                        zp2[:, spar, :],
                                                        el2[:, spar, :]),
                                waits=[("s", s_sp)])
                            vop(v, lambda: v.tensor_mul(du_all[:, sl, :],
                                                        dl_all[:, sl, :],
                                                        u_b[:]),
                                waits=[("ld", ld_u)])
                        v_dl[sl] = C["v"]

                    for ng in range(2):
                        s_bc = C["s"]
                        for j in range(8):
                            n = ng * 8 + j
                            @block.tensor
                            def _(t, n=n, qpar=qpar, ld_q=ld_q, s_bc=s_bc):
                                t.wait_ge(SEM["s"], s_bc)
                                t.wait_ge(SEM["qq"], ld_q)
                                for hk in range(2):
                                    last = t.matmul(ps_bb[:, 0, hk, :],
                                             sel_sb[:, n * 128:(n + 1) * 128],
                                             btq[0:16, qpar,
                                                 hk * 512:(hk + 1) * 512],
                                             start=True, stop=True)
                                for hk in range(2):
                                    last = t.matmul(ps_bb[:, 1, hk, :],
                                             sel_sb[:, n * 128:(n + 1) * 128],
                                             ctq[0:16, qpar,
                                                 hk * 512:(hk + 1) * 512],
                                             start=True, stop=True)
                                tick(last, "t")
                            t_bc = C["t"]

                            @block.scalar
                            def _(s, j=j, t_bc=t_bc, ng=ng):
                                sop(s, lambda: s.copy(
                                    Bbc[:, j, :],
                                    ps_bb.ap()[:, 0]
                                    .rearrange("p a b -> p (a b)")),
                                    waits=[("t", t_bc),
                                           ("v", v_bc_war[ng])])
                                sop(s, lambda: s.copy(
                                    Cbc[:, j, :],
                                    ps_bb.ap()[:, 1]
                                    .rearrange("p a b -> p (a b)")))
                            s_bc = C["s"]
                        s_bc_done = C["s"]

                        for sl in range(8):
                            s_base = C["s"]
                            @block.scalar
                            def _(s, sl=sl, ng=ng):
                                for j in range(8):
                                    n = ng * 8 + j
                                    w = [("v", max(v_dl[sl],
                                                   v_scan_hist[j]))]
                                    sop(s, lambda j=j, n=n: s.activation(
                                        a_buf[:, j, :], dl_all[:, sl, :],
                                        AF.Exp,
                                        scale=A_sb[:, sl * 16 + n:
                                                   sl * 16 + n + 1]),
                                        waits=w)

                            for j in range(8):
                                n = ng * 8 + j
                                hslot = j % 4
                                s_exp = s_base + j + 1
                                @block.vector
                                def _(v, sl=sl, j=j, n=n, hslot=hslot,
                                      s_exp=s_exp, qt=qt, ng=ng):
                                    vop(v, lambda: v.tensor_mul(
                                        bsc2[:, j % 2, :], du_all[:, sl, :],
                                        Bbc[:, j, :]),
                                        waits=([("s", s_bc_done)]
                                               if j == 0 else ()))
                                    if qt == 0:
                                        init = 0.0
                                    else:
                                        init = hst_sb[:, sl, n:n + 1]
                                    vop(v, lambda: v.tensor_tensor_scan(
                                        h4[:, hslot, :], a_buf[:, j, :],
                                        bsc2[:, j % 2, :], init,
                                        op0=ALU.mult, op1=ALU.add),
                                        waits=[("s", s_exp)])
                                    if j % 4 == 3:
                                        lo = ng * 8 + j - 3
                                        vop(v, lambda lo=lo: v.tensor_copy(
                                            hst_sb[:, sl, lo:lo + 4],
                                            h4.ap()[:, 0:4, QW - 1:QW]
                                            .rearrange("p a b -> p (a b)")))
                                    acc = yac[:, sl, :] if ng == 0 \
                                        else ptmp[:, 0, :]
                                    if j == 0:
                                        vop(v, lambda: v.tensor_mul(
                                            acc, h4[:, hslot, :],
                                            Cbc[:, j, :]))
                                    else:
                                        vop(v, lambda: v.tensor_mul(
                                            ptmp[:, 1, :], h4[:, hslot, :],
                                            Cbc[:, j, :]))
                                        vop(v, lambda: v.tensor_add(
                                            acc, acc, ptmp[:, 1, :]))
                                v_scan_hist[j] = C["v"]

                            if ng == 1:
                                @block.sync
                                def _(sy, sl=sl, t0=t0):
                                    dma(sy, "ld", u_b[:],
                                        uT_sp[sl * 128:(sl + 1) * 128,
                                              t0:t0 + QW],
                                        waits=[("v", C["v"])])
                                    dma(sy, "ld", gs_b[:],
                                        gsT_sp[sl * 128:(sl + 1) * 128,
                                               t0:t0 + QW])
                                ld_ug = C["ld"]

                                @block.vector
                                def _(v, sl=sl, ld_ug=ld_ug):
                                    vop(v, lambda: v.tensor_add(
                                        yf[:], ptmp[:, 0, :], yac[:, sl, :]))
                                    vop(v, lambda: v.scalar_tensor_tensor(
                                        yf[:], u_b[:], dp_sb[:, sl:sl + 1],
                                        yf[:], op0=ALU.mult, op1=ALU.add),
                                        waits=[("ld", ld_ug)])
                                    vop(v, lambda: v.tensor_mul(
                                        yo[:], yf[:], gs_b[:]),
                                        waits=[("sd", C["sd"])])
                                v_y = C["v"]

                                @block.gpsimd
                                def _(g, sl=sl, b=b, qt=qt, v_y=v_y):
                                    dma(g, "sd",
                                        yT_mine[b][sl * 128:(sl + 1) * 128,
                                                   qt * QW:(qt + 1) * QW],
                                        yo[:], waits=[("v", v_y)])
                        v_bc_war[ng] = C["v"]
                    if qt == 3:
                        sd_yh = C["sd"]
                        @block.gpsimd
                        def _(g, b=b, sd_yh=sd_yh):
                            g.wait_ge(SEM["sd"], sd_yh)
                            g.collective_compute(
                                "AllGather", ALU.bypass,
                                replica_groups=[list(range(NCORE))],
                                ins=[yT_mine[b].opt()],
                                outs=[yT_parts[b]
                                      .rearrange("r p t -> (r p) t").opt()],
                            ).then_inc(cc_sems[2 + b])

        if upto <= 6:
            @block.gpsimd
            def _(g):
                g.wait_ge(cc_sems[2], 1)
                g.wait_ge(cc_sems[3], 1)
            fence_all()
            return nc

        with ExitStack() as p8:
            e8 = p8.enter_context
            wos = e8(nc.sbuf_tensor("p8_wo", [128, 64, COL], BF16))
            ywin = e8(nc.sbuf_tensor("p8_yw", [128, 8, 1024], BF16))
            gq = e8(nc.sbuf_tensor("p8_g", [128, 8, COL], BF16))
            xres = e8(nc.sbuf_tensor("p8_x", [128, 8, COL], F32))
            sg = e8(nc.sbuf_tensor("p8_sg", [128, 2, COL], F32))
            ps8 = e8(nc.psum_tensor("p8_ps", [128, 8, 512], F32))

            @block.gpsimd
            def _(g):
                g.wait_ge(cc_sems[2], 1)
                tick(g.engine_nop(), "p")
            p_agy = [C["p"], None]

            @block.scalar
            def _(sy):
                for g4 in range(4):
                    dma(sy, "l2",
                        wos.ap()[:, g4 * 16:(g4 + 1) * 16, :]
                        .rearrange("p a b -> p (a b)"),
                        wout_ld[:, g4 * 16:(g4 + 1) * 16, :]
                        .rearrange("p a b -> p (a b)"),
                        waits=([("t", C["t"]), ("v", C["v"])]
                               if g4 == 0 else ()))
            ld_wo = C["l2"]

            x_hist = {i: 0 for i in range(4)}
            y_hist = {i: 0 for i in range(8)}
            YSEMS = ["y0", "y1", "y2", "y3", "w0", "w1", "w2", "w3"]
            sg_hist = {0: 0, 1: 0}
            v_comb_last = 0
            first_q = {(h, q): True for h in range(2) for q in range(2)}

            for tg in range(8):
                if tg == 4:
                    @block.gpsimd
                    def _(g):
                        g.wait_ge(cc_sems[3], 1)
                        tick(g.engine_nop(), "p")
                    p_agy[1] = C["p"]
                @block.gpsimd
                def _(g, tg=tg):
                    dma(g, "sd", gq.ap(),
                        gq_sp[tg * 8:tg * 8 + 8].rearrange("a p b -> p a b"),
                        waits=[("v", C["v"])])
                    for tt in range(8):
                        dma(g, "sd", xres[:, tt, :],
                            x_cols[tg * 1024 + tt * 128:
                                   tg * 1024 + (tt + 1) * 128, :])
                ld_xr = C["sd"]

                for kb in range(64):
                    slot = kb % 8
                    qsel = kb % 2
                    eng_blk = block.sync if qsel == 0 else block.scalar
                    ysem = YSEMS[slot]
                    @eng_blk
                    def _(sy, tg=tg, kb=kb, slot=slot, qsel=qsel, ysem=ysem):
                        w = [("t", y_hist[slot])]
                        if first_q[(tg // 4, qsel)]:
                            w.append(("p", p_agy[tg // 4]))
                            first_q[(tg // 4, qsel)] = False
                        dma(sy, ysem, ywin[:, slot, :],
                            yT_parts[tg // 4][kb // 8, (kb % 8) * 128:
                                              (kb % 8) * 128 + 128,
                                              (tg % 4) * 1024:
                                              (tg % 4 + 1) * 1024],
                            waits=w)
                    ld_y = C[ysem]

                    @block.tensor
                    def _(t, kb=kb, slot=slot, ysem=ysem, ld_y=ld_y,
                          v_comb_last=v_comb_last):
                        t.wait_ge(SEM[ysem], ld_y)
                        t.wait_ge(SEM["l2"], ld_wo)
                        if kb == 0:
                            t.wait_ge(SEM["v"], v_comb_last)
                        for tt in range(8):
                            last = t.matmul(ps8[:, tt, :],
                                     ywin[:, slot, tt * 128:(tt + 1) * 128],
                                     wos[:, kb, :],
                                     start=(kb == 0), stop=(kb == 63))
                        tick(last, "t")
                    y_hist[slot] = C["t"]
                t_s = C["t"]

                for tt in range(8):
                    sp = tt % 2
                    @block.vector
                    def _(v, tg=tg, tt=tt, sp=sp, t_s=t_s, ld_xr=ld_xr):
                        vop(v, lambda: v.tensor_mul(sg[:, sp, :],
                                                    ps8[:, tt, :],
                                                    gq[:, tt, :]),
                            waits=[("t", t_s), ("sd", C["sd"])])
                        vop(v, lambda: v.tensor_add(sg[:, sp, :],
                                                    sg[:, sp, :],
                                                    xres[:, tt, :]),
                            waits=[("sd", ld_xr)])
                    v_o = C["v"]

                    @block.gpsimd
                    def _(g, tg=tg, tt=tt, sp=sp, v_o=v_o):
                        dma(g, "sd",
                            out_cols[tg * 1024 + tt * 128:
                                     tg * 1024 + (tt + 1) * 128, :],
                            sg[:, sp, :], waits=[("v", v_o)])
                    sg_hist[sp] = C["sd"]
                v_comb_last = C["v"]

            sd_total = C["sd"]

            @block.gpsimd
            def _(g):
                g.wait_ge(SEM["sd"], sd_total)

            @block.sync
            def _(sy):
                sy.wait_ge(SEM["sd"], sd_total)

    return nc


_NC_CACHE = None


def kernel(L0_attn_hidden_states, L0_attn_W_in, L0_attn_conv_w, L0_attn_conv_b,
           L0_attn_Wx, L0_attn_dt_ln_w, L0_attn_b_ln_w, L0_attn_c_ln_w,
           L0_attn_Wdt, L0_attn_dt_bias, L0_attn_A, L0_attn_D_param,
           L0_attn_W_out, L0_ffn_weight_q, L0_ffn_scales_and_zeros):
    global _NC_CACHE
    x = np.asarray(L0_attn_hidden_states, dtype=np.float32)
    w_in = np.asarray(L0_attn_W_in, dtype=np.float32)
    conv_w = np.asarray(L0_attn_conv_w, dtype=np.float32)
    conv_b = np.asarray(L0_attn_conv_b, dtype=np.float32)
    wx = np.asarray(L0_attn_Wx, dtype=np.float32)
    dt_ln = np.asarray(L0_attn_dt_ln_w, dtype=np.float32)
    b_ln = np.asarray(L0_attn_b_ln_w, dtype=np.float32)
    c_ln = np.asarray(L0_attn_c_ln_w, dtype=np.float32)
    wdt = np.asarray(L0_attn_Wdt, dtype=np.float32)
    dt_bias = np.asarray(L0_attn_dt_bias, dtype=np.float32)
    a_mat = np.asarray(L0_attn_A, dtype=np.float32)
    d_param = np.asarray(L0_attn_D_param, dtype=np.float32)
    w_out = np.asarray(L0_attn_W_out, dtype=np.float32)
    wq = np.asarray(L0_ffn_weight_q, dtype=np.int32)
    snz = np.asarray(L0_ffn_scales_and_zeros, dtype=np.float32)

    return _run(x, w_in, conv_w, conv_b, wx, dt_ln, b_ln, c_ln, wdt, dt_bias,
                a_mat, d_param, w_out, wq, snz)


def make_in_maps(x, w_in, conv_w, conv_b, wx, dt_ln, b_ln, c_ln, wdt, dt_bias,
                 a_mat, d_param, w_out, wq, snz):
    xf = x.reshape(TOK, H)
    ident_b = np.eye(128, dtype=ml_dtypes.bfloat16)
    ident_f = np.eye(128, dtype=np.float32)
    sel16 = np.zeros((16, 2048), dtype=ml_dtypes.bfloat16)
    for n in range(16):
        sel16[n, n * 128:(n + 1) * 128] = 1.0

    in_maps = []
    for c in range(NCORE):
        dlo = c * DL
        clo = c * COL
        w_slab = np.concatenate([w_in[dlo:dlo + DL, :],
                                 w_in[D + dlo:D + dlo + DL, :]], axis=0)
        wdt_local = wdt[dlo:dlo + DL, :]
        wx_local = wx[:, dlo:dlo + DL]
        wout_local = w_out[clo:clo + COL, :]
        wq_local = wq[clo:clo + COL, :]
        in_maps.append(dict(
            x_rows=np.ascontiguousarray(xf[c * 1024:(c + 1) * 1024, :]),
            x_cols=np.ascontiguousarray(xf[:, clo:clo + COL]),
            win_f=np.ascontiguousarray(
                w_slab.reshape(16, 128, 32, 128)
                .transpose(0, 3, 2, 1).reshape(16, 128, H)),
            conv_w=np.ascontiguousarray(
                conv_w[dlo:dlo + DL, :].reshape(8, 128, 4)
                .transpose(1, 0, 2).reshape(128, 32)),
            conv_b=np.ascontiguousarray(
                conv_b[dlo:dlo + DL].reshape(8, 128).T),
            wx_f=np.ascontiguousarray(
                wx_local.T.reshape(8, 128, 288)
                .transpose(1, 0, 2).reshape(128, 2304)),
            wdt_f=np.ascontiguousarray(
                wdt_local.reshape(8, 128, 2, 128)
                .transpose(3, 0, 2, 1).reshape(128, 2048)),
            dt_bias=np.ascontiguousarray(
                dt_bias[dlo:dlo + DL].reshape(8, 128).T),
            a_mat=np.ascontiguousarray(
                a_mat[dlo:dlo + DL, :].reshape(8, 128, NN)
                .transpose(1, 0, 2).reshape(128, 128)),
            d_param=np.ascontiguousarray(
                d_param[dlo:dlo + DL].reshape(8, 128).T),
            wout_f=np.ascontiguousarray(
                wout_local.T.reshape(64, 128, COL)
                .transpose(1, 0, 2).reshape(128, 64 * COL)),
            wqT=np.ascontiguousarray(
                wq_local.T.reshape(32, 128, COL)),
            sc_bc=np.ascontiguousarray(np.broadcast_to(
                snz[:, clo:clo + COL, 0][:, None, :], (32, 128, COL))),
            zr_bc=np.ascontiguousarray(np.broadcast_to(
                snz[:, clo:clo + COL, 1][:, None, :], (32, 128, COL))),
            lnw_in=np.ascontiguousarray(np.broadcast_to(
                np.concatenate([dt_ln, b_ln, c_ln])[None, :], (128, 288))),
            ident_b=ident_b, ident_f=ident_f, sel16=sel16,
        ))

    return in_maps


def _run(x, w_in, conv_w, conv_b, wx, dt_ln, b_ln, c_ln, wdt, dt_bias,
         a_mat, d_param, w_out, wq, snz):
    global _NC_CACHE
    in_maps = make_in_maps(x, w_in, conv_w, conv_b, wx, dt_ln, b_ln, c_ln,
                           wdt, dt_bias, a_mat, d_param, w_out, wq, snz)
    if _NC_CACHE is None:
        _NC_CACHE = build_program()
    res = run_bass_kernel_spmd(_NC_CACHE, in_maps, core_ids=list(range(NCORE)))
    parts = [res.results[c]["out_cols"] for c in range(NCORE)]
    out = np.concatenate(parts, axis=1)
    return out.reshape(2, LB, H)
